# version 16
# baseline (speedup 1.0000x reference)
import numpy as np
import ml_dtypes
import concourse.bass as bass
import concourse.mybir as mybir
from concourse.bass_utils import run_bass_kernel_spmd

F32 = mybir.dt.float32
BF16 = mybir.dt.bfloat16
I32 = mybir.dt.int32
AF = mybir.ActivationFunctionType
ALU = mybir.AluOpType
AX = mybir.AxisListType

NCORE = 8
S = 8192
TOK = 1024
H = 2048
NH = 16
DH = 128
NIT = 43
LUTN = 16
SCALE = float(1.0 / np.sqrt(128.0))
NEG = -1.0e30
BF = ml_dtypes.bfloat16
ATT_LAG = 1


class Plan:
    def __init__(self):
        self.streams = {"sync": [], "tensor": [], "vector": [], "scalar": [], "gpsimd": []}
        self.cnt = {}

    def emit(self, eng, fn):
        self.streams[eng].append(fn)

    def bump(self, s_name, n):
        self.cnt[s_name] = self.cnt.get(s_name, 0) + n
        return self.cnt[s_name]

    def cur(self, s_name):
        return self.cnt.get(s_name, 0)


def build_nc(taps=False):
    nc = bass.Bass("TRN2", target_bir_lowering=False, debug=False)

    ap = {}
    for name, shape, dt in [
        ("cid", [1, 1], I32),
        ("x_own", [TOK, H], F32),
        ("wqkv", [H, 768], F32),
        ("bq4", [128, 4], F32),
        ("bv2", [1, 256], F32),
        ("wo", [H, H], F32),
        ("bo_t", [128, 16], F32),
        ("lut", [1, LUTN], F32),
        ("gidxT", [H, 768], I32),
        ("uidxT", [H, 768], I32),
        ("didxT", [768, H], I32),
        ("b4j", [128, 2 * S], BF16),
        ("b4i", [128, 2 * S], BF16),
        ("maskneg", [128, 2048], BF16),
        ("identf", [128, 128], F32),
    ]:
        ap[name] = nc.dram_tensor(name, shape, dt, kind="ExternalInput").ap()
    out_ap = nc.dram_tensor("out", [TOK, H], F32, kind="ExternalOutput").ap()
    tap_ap = {}
    if taps:
        for name, shape, dt in [
            ("t_xnT", [128, NH * TOK], BF16),
            ("t_qT", [128, 2 * S], BF16),
            ("t_kT", [128, 2 * S], BF16),
            ("t_vT", [128, 2 * S], BF16),
            ("t_aoT", [128, 2 * S], BF16),
            ("t_hT", [128, NIT * TOK], BF16),
            ("t_ao_own", [128, NH * TOK], BF16),
        ]:
            tap_ap[name] = nc.dram_tensor(name, shape, dt, kind="ExternalOutput").ap()

    ag_xn_in = nc.dram_tensor("ag_xn_in", [H, 512], BF16)
    ag_xn_in2 = nc.dram_tensor("ag_xn_in2", [H, 512], BF16)
    ag_xn_out = nc.dram_tensor("ag_xn_out", [NCORE * H, 512], BF16, addr_space="Shared")
    ag_xn_out2 = nc.dram_tensor("ag_xn_out2", [NCORE * H, 512], BF16, addr_space="Shared")
    ag_gu_in = nc.dram_tensor("ag_gu_in", [H, 2 * 768], BF16)
    ag_gu_out = nc.dram_tensor("ag_gu_out", [NCORE * H, 2 * 768], BF16, addr_space="Shared")
    ag_wd_in = nc.dram_tensor("ag_wd_in", [768, H], BF16)
    ag_wd_out = nc.dram_tensor("ag_wd_out", [NCORE * 768, H], BF16, addr_space="Shared")
    ag_at_in1 = nc.dram_tensor("ag_at_in1", [128, S], BF16)
    ag_at_out1 = nc.dram_tensor("ag_at_out1", [NCORE * 128, S], BF16, addr_space="Shared")
    ag_at_in2 = nc.dram_tensor("ag_at_in2", [128, S], BF16)
    ag_at_out2 = nc.dram_tensor("ag_at_out2", [NCORE * 128, S], BF16, addr_space="Shared")

    P = Plan()
    SEM = {}
    open_cms = []

    def sb(name, shape, dt):
        cm = nc.sbuf_tensor(name, shape, dt)
        t = cm.__enter__()
        open_cms.append(cm)
        return t, cm

    def sb_free(cm):
        cm.__exit__(None, None, None)
        open_cms.remove(cm)

    def dma(eng, s_name, out, in_, **kw):
        v = P.bump(s_name, 16)
        s_h = SEM[s_name]
        P.emit(eng, lambda e, o=out, i=in_, s_h=s_h, kw=kw:
               e.dma_start(out=o, in_=i, **kw).then_inc(s_h, 16))
        return v

    def wait(eng, s_name, val):
        if val <= 0:
            return
        s_h = SEM[s_name]
        P.emit(eng, lambda e, s_h=s_h, v=val: e.wait_ge(s_h, v))

    def mm(out, lhsT, rhs, start, stop, inc=False):
        if inc:
            v = P.bump("sMM", 1)
            s_h = SEM["sMM"]
            P.emit("tensor", lambda e, o=out, l=lhsT, r=rhs, a=start, b=stop, s_h=s_h:
                   e.matmul(o, l, r, start=a, stop=b).then_inc(s_h, 1))
            return v
        P.emit("tensor", lambda e, o=out, l=lhsT, r=rhs, a=start, b=stop:
               e.matmul(o, l, r, start=a, stop=b))

    def trans(out, in_, ident, inc=False):
        if inc:
            v = P.bump("sMM", 1)
            s_h = SEM["sMM"]
            P.emit("tensor", lambda e, o=out, i=in_, d=ident, s_h=s_h:
                   e.transpose(o, i, d).then_inc(s_h, 1))
            return v
        P.emit("tensor", lambda e, o=out, i=in_, d=ident: e.transpose(o, i, d))

    def act(out, in_, func, bias=None, scale=None):
        kw = {}
        if bias is not None:
            kw["bias"] = bias
        if scale is not None:
            kw["scale"] = scale
        prev = P.cur("sAC")
        v = P.bump("sAC", 1)
        s_h = SEM["sAC"]
        P.emit("scalar", lambda e, o=out, i=in_, f=func, kw=kw, s_h=s_h, pv=prev:
               (e.wait_ge(s_h, pv) if pv else None,
                e.activation(o, i, f, **kw).then_inc(s_h, 1)))
        return v

    def vop(name, *args, **kw):
        prev = P.cur("sVE")
        v = P.bump("sVE", 1)
        s_h = SEM["sVE"]
        P.emit("vector", lambda e, n=name, a=args, kw=kw, s_h=s_h, pv=prev:
               (e.wait_ge(s_h, pv) if pv else None,
                getattr(e, n)(*a, **kw).then_inc(s_h, 1)))
        return v

    DMA_SEMS = ("sLtab", "sLln", "sLsel", "sLcon", "sLwo", "sLx0", "sLx1",
                "sLw0", "sLw1", "sLwl", "sLxr", "sLG", "sLS")

    def barrier_waits(eng):
        for s_name in ("sMM", "sAC", "sVE") + DMA_SEMS:
            wait(eng, s_name, P.cur(s_name))

    sem_cms = []
    for s_name in DMA_SEMS + ("sMM", "sAC", "sVE", "sCXN", "sCX2", "sCGU", "sCWD", "sCAT"):
        cm = nc.semaphore(s_name)
        SEM[s_name] = cm.__enter__()
        sem_cms.append(cm)

    ps_cms = []

    def psum(name, shape):
        cm = nc.psum_tensor(name, shape, F32)
        t = cm.__enter__()
        ps_cms.append(cm)
        return t
    psg = [psum("psg0", [128, 1024]), psum("psg1", [128, 1024])]
    pv2 = [psum("pv0", [128, 512]), psum("pv1", [128, 512])]
    db2 = [psum("db0", [128, 512]), psum("db1", [128, 512])]

    tbl = {}
    for name, shape, dt in [
        ("onesj", [128, 128], BF16),
        ("ones1f", [1, 128], F32),
        ("identf_s", [128, 128], F32),
        ("bo_s", [128, 16], F32), ("bq4_s", [128, 4], F32),
        ("bv2f", [1, 256], F32), ("lutrow", [1, LUTN], F32),
        ("lutrep", [128, LUTN], F32), ("eps", [128, 1], F32),
    ]:
        tbl[name], _ = sb(name, shape, dt)

    xnownT, cm_xnownT = sb("xnownT", [128, NH * TOK], BF16)

    ld0 = 0
    for dst, src in [
        (tbl["identf_s"][:], ap["identf"]), (tbl["bo_s"][:], ap["bo_t"]),
        (tbl["bq4_s"][:], ap["bq4"]), (tbl["bv2f"][:], ap["bv2"]),
        (tbl["lutrow"][:], ap["lut"]),
    ]:
        ld0 = dma("sync", "sLtab", dst, src)

    vop("memset", tbl["eps"][:], 1.0e-5)
    vop("memset", tbl["onesj"][:], 1.0)
    v0 = vop("memset", tbl["ones1f"][:], 1.0)
    wait("vector", "sLtab", ld0)
    vop("tensor_scalar", tbl["bq4_s"][:, 0:2], tbl["bq4_s"][:, 0:2],
        SCALE, None, ALU.mult)
    wait("tensor", "sLtab", ld0)
    wait("tensor", "sVE", v0)
    m_lut = mm(db2[0][:, 0:LUTN], tbl["ones1f"][:], tbl["lutrow"][:], True, True, inc=True)
    wait("vector", "sMM", m_lut)
    v_lutrep = vop("tensor_copy", tbl["lutrep"][:], db2[0][:, 0:LUTN])

    ln_x, cm_ln_x = sb("ln_x", [128, 8 * H], F32)
    x2s, cm_x2s = sb("x2s", [128, H], F32)
    ln_st, cm_ln_st = sb("ln_st", [128, 4 * 8], F32)
    xn_t, cm_xn_t = sb("xn_t", [128, 2 * H], F32)

    ld_a = dma("sync", "sLln", ln_x[:, 0:4 * H].rearrange("p (a w) -> p a w", a=4),
               ap["x_own"][0:512, :].rearrange("(a p) w -> p a w", p=128))
    ld_b = dma("sync", "sLx1", ln_x[:, 4 * H:8 * H].rearrange("p (a w) -> p a w", a=4),
               ap["x_own"][512:1024, :].rearrange("(a p) w -> p a w", p=128))
    for tau in range(8):
        if tau == 0:
            wait("vector", "sLln", ld_a)
        elif tau == 4:
            wait("vector", "sLx1", ld_b)
        xt = ln_x[:, H * tau:H * (tau + 1)]
        vop("reduce_sum", ln_st[:, 4 * tau:4 * tau + 1], xt, axis=AX.X)
        vop("tensor_scalar", ln_st[:, 4 * tau:4 * tau + 1],
            ln_st[:, 4 * tau:4 * tau + 1], 1.0 / H, None, ALU.mult)
        vop("tensor_tensor", x2s[:], xt, xt, ALU.mult)
        vop("reduce_sum", ln_st[:, 4 * tau + 1:4 * tau + 2], x2s[:], axis=AX.X)
        vop("tensor_scalar", ln_st[:, 4 * tau + 1:4 * tau + 2],
            ln_st[:, 4 * tau + 1:4 * tau + 2], 1.0 / H, None, ALU.mult)
        vop("tensor_tensor", ln_st[:, 4 * tau + 2:4 * tau + 3],
            ln_st[:, 4 * tau:4 * tau + 1], ln_st[:, 4 * tau:4 * tau + 1], ALU.mult)
        vop("tensor_tensor", ln_st[:, 4 * tau + 2:4 * tau + 3],
            ln_st[:, 4 * tau + 1:4 * tau + 2], ln_st[:, 4 * tau + 2:4 * tau + 3],
            ALU.subtract)
    v_var = P.cur("sVE")
    wait("scalar", "sVE", v_var)
    a_sd = act(ln_st[:].rearrange("p (a w) -> p a w", a=8)[:, :, 3:4],
               ln_st[:].rearrange("p (a w) -> p a w", a=8)[:, :, 2:3],
               AF.Sqrt, bias=tbl["eps"][:], scale=1.0)
    wait("vector", "sAC", a_sd)
    vop("reciprocal", ln_st[:].rearrange("p (a w) -> p a w", a=8)[:, :, 3:4],
        ln_st[:].rearrange("p (a w) -> p a w", a=8)[:, :, 3:4])
    ln_tr_last = {}
    ln_cp = {}
    xnv = xnownT[:].rearrange("p (h t) -> p h t", h=NH)
    for tau in range(8):
        xslot = xn_t[:, H * (tau % 2):H * (tau % 2) + H]
        if tau >= 2:
            wait("vector", "sMM", ln_tr_last[tau - 2])
        v_xn = vop("tensor_scalar", xslot, ln_x[:, H * tau:H * (tau + 1)],
                   ln_st[:, 4 * tau:4 * tau + 1],
                   ln_st[:, 4 * tau + 3:4 * tau + 4],
                   ALU.subtract, op1=ALU.mult)
        wait("tensor", "sVE", v_xn)
        for half in range(2):
            if tau >= 1:
                wait("tensor", "sVE", ln_cp[(tau - 1, half)])
            for h8 in range(8):
                hh = 8 * half + h8
                m_t = trans(psg[half][:, 128 * h8:128 * (h8 + 1)],
                            xslot[:, 128 * hh:128 * (hh + 1)],
                            tbl["identf_s"][:], inc=(h8 == 7))
            ln_tr_last[tau] = m_t
            wait("vector", "sMM", m_t)
            ln_cp[(tau, half)] = vop(
                "tensor_copy",
                xnv[:, 8 * half:8 * (half + 1), 128 * tau:128 * (tau + 1)],
                psg[half][:].rearrange("p (h w) -> p h w", h=8))
        if tau == 3:
            wait("gpsimd", "sVE", P.cur("sVE"))
            g_h1 = dma("gpsimd", "sLG",
                       ag_xn_in.ap().rearrange("(h p) t -> p h t", p=128),
                       xnownT[:].rearrange("p (h t) -> p h t", h=NH)[:, :, 0:512])
            wait("gpsimd", "sLG", g_h1)
            P.emit("gpsimd", lambda e: e.collective_compute(
                "AllGather", ALU.bypass, replica_groups=[list(range(NCORE))],
                ins=[ag_xn_in.ap().opt()],
                outs=[ag_xn_out.ap().opt()]).then_inc(SEM["sCXN"]))
    v_ln_done = P.cur("sVE")

    wait("gpsimd", "sVE", v_ln_done)
    g_xn = dma("gpsimd", "sLG",
               ag_xn_in2.ap().rearrange("(h p) t -> p h t", p=128),
               xnownT[:].rearrange("p (h t) -> p h t", h=NH)[:, :, 512:1024])
    wait("gpsimd", "sLG", g_xn)
    P.emit("gpsimd", lambda e: e.collective_compute(
        "AllGather", ALU.bypass, replica_groups=[list(range(NCORE))],
        ins=[ag_xn_in2.ap().opt()], outs=[ag_xn_out2.ap().opt()]).then_inc(SEM["sCX2"]))
    if taps:
        dma("gpsimd", "sLG", tap_ap["t_xnT"], xnownT[:])
    xn_contrib_mark = P.cur("sLG")

    for cm in (cm_xn_t, cm_ln_st, cm_x2s, cm_ln_x, cm_xnownT):
        sb_free(cm)

    qT, cm_qT = sb("qT", [128, 2 * S], BF16)
    kT, cm_kT = sb("kT", [128, 2 * S], BF16)
    vT, cm_vT = sb("vT", [128, 2 * S], BF16)
    aoT, cm_aoT = sb("aoT", [128, 2 * S], BF16)
    ptile, cm_ptile = sb("ptile", [128, 2 * 1024], BF16)
    idx_i, cm_idx_i = sb("idx_i", [128, 1024], I32)
    idx_i2, cm_idx_i2 = sb("idx_i2", [128, 1024], I32)
    selw, cm_selw = sb("selw", [128, 1024], BF16)
    selt, cm_selt = sb("selt", [128, 1024], BF16)
    lntmp, cm_lntmp = sb("lntmp", [128, 512], F32)
    rcpr, cm_rcpr = sb("rcpr", [128, 512], F32)
    w6, cm_w6 = sb("w6", [128, 4 * H], BF16)
    wv2, cm_wv2 = sb("wv2", [128, 2 * H], BF16)
    wcv0, cm_wcv0 = sb("wcv0", [128, H], F32)
    wcv1, cm_wcv1 = sb("wcv1", [128, H], F32)

    wcv = (wcv0, wcv1)
    cast_marks = {}
    for j in range(6):
        if j >= 2:
            wait("scalar", "sAC", cast_marks[j - 2])
        ldv = dma("scalar", "sLwo",
                  wcv[j % 2][:].rearrange("p (h w) -> p h w", h=NH),
                  ap["wqkv"][:, 128 * j:128 * (j + 1)].rearrange("(h p) w -> p h w", p=128))
        wait("scalar", "sLwo", ldv)
        dst = w6[:, H * j:H * (j + 1)] if j < 4 else wv2[:, H * (j - 4):H * (j - 3)]
        cast_marks[j] = act(dst, wcv[j % 2][:], AF.Identity, scale=1.0)
    a_w_ready = cast_marks[5]
    sb_free(cm_wcv1)
    sb_free(cm_wcv0)

    xnstr, cm_xnstr = sb("xnstr", [128, 2 * 8192], BF16)

    sel_state = {"n": 0, "cp": {}, "last_contrib": 0, "pend": None}

    def flush_contrib():
        if sel_state["pend"] is not None:
            pdst, pcols, pmark = sel_state["pend"]
            wait("gpsimd", "sVE", pmark)
            sel_state["last_contrib"] = dma("gpsimd", "sLcon", pdst,
                                            selw[:, 0:pcols])
            sel_state["pend"] = None

    def emit_select_unit(src_ap, cols, dst_ap):
        ku = sel_state["n"]
        sel_state["n"] = ku + 1
        ib = (idx_i, idx_i2)[ku % 2]
        isem = ("sLsel", "sLwl")[ku % 2]
        gate = sel_state["cp"].get(ku - 2, v_lutrep if ku == 0 else 0)
        if gate:
            wait("gpsimd", "sVE", gate)
        ldv = dma("gpsimd", isem, ib[:, 0:cols], src_ap)
        flush_contrib()
        wait("vector", isem, ldv)
        if sel_state["last_contrib"]:
            wait("vector", "sLcon", sel_state["last_contrib"])
        last = 0
        for k in range(LUTN):
            if k == 0:
                vop("tensor_scalar", selw[:, 0:cols], ib[:, 0:cols],
                    0.0, tbl["lutrep"][:, 0:1], ALU.is_equal, ALU.mult)
            else:
                vop("tensor_scalar", selt[:, 0:cols], ib[:, 0:cols],
                    float(k), tbl["lutrep"][:, k:k + 1], ALU.is_equal, ALU.mult)
                last = vop("tensor_tensor", selw[:, 0:cols], selw[:, 0:cols],
                           selt[:, 0:cols], ALU.add)
        sel_state["cp"][ku] = last
        sel_state["pend"] = (dst_ap, cols, last)

    select_units = []
    for mat_name, col_off in (("gidxT", 0), ("uidxT", 768)):
        for hp in range(16):
            select_units.append((
                ap[mat_name][128 * hp:128 * (hp + 1), :], 768,
                ag_gu_in.ap()[128 * hp:128 * (hp + 1), col_off:col_off + 768]))
    for it6 in range(6):
        for half in range(2):
            select_units.append((
                ap["didxT"][128 * it6:128 * (it6 + 1), 1024 * half:1024 * (half + 1)],
                1024,
                ag_wd_in.ap()[128 * it6:128 * (it6 + 1), 1024 * half:1024 * (half + 1)]))
    sel_idx = [0]
    cc_emitted = {"gu": False, "wd": False}

    def emit_gu_cc():
        wait("gpsimd", "sLcon", P.cur("sLcon"))
        P.emit("gpsimd", lambda e: e.collective_compute(
            "AllGather", ALU.bypass, replica_groups=[list(range(NCORE))],
            ins=[ag_gu_in.ap().opt()], outs=[ag_gu_out.ap().opt()]).then_inc(SEM["sCGU"]))

    def emit_wd_cc():
        wait("gpsimd", "sLcon", P.cur("sLcon"))
        P.emit("gpsimd", lambda e: e.collective_compute(
            "AllGather", ALU.bypass, replica_groups=[list(range(NCORE))],
            ins=[ag_wd_in.ap().opt()], outs=[ag_wd_out.ap().opt()]).then_inc(SEM["sCWD"]))
    def pop_select(n):
        for _ in range(n):
            if sel_idx[0] < len(select_units):
                emit_select_unit(*select_units[sel_idx[0]])
                sel_idx[0] += 1
                if sel_idx[0] == 32 and not cc_emitted["gu"]:
                    cc_emitted["gu"] = True
                    flush_contrib()
                    emit_gu_cc()
                if sel_idx[0] == len(select_units) and not cc_emitted["wd"]:
                    cc_emitted["wd"] = True
                    flush_contrib()
                    emit_wd_cc()

    pop_select(9)

    wait("tensor", "sCXN", 1)
    wait("sync", "sCXN", 1)
    wait("sync", "sLG", xn_contrib_mark)
    wait("sync", "sAC", a_w_ready)
    wait("sync", "sLG", P.cur("sLG"))
    wait("scalar", "sLG", P.cur("sLG"))
    wait("tensor", "sVE", v_ln_done)
    wait("tensor", "sAC", P.cur("sAC"))

    qk_evac = {}
    v_evac = {}
    stream_last_mm = {}
    tt_order = [2 * r for r in range(8)] + [2 * r + 1 for r in range(8)]
    for kk, tt in enumerate(tt_order):
        buf = kk % 2
        xc = buf * 8192
        rank = tt // 2
        src_t = ag_xn_out if tt % 2 == 0 else ag_xn_out2
        if kk == 8:
            wait("sync", "sCX2", 1)
            wait("tensor", "sCX2", 1)
        if kk >= 2:
            wait("sync", "sMM", stream_last_mm[kk - 2])
        ldv = dma("sync", "sLx0" if buf == 0 else "sLx1",
                  xnstr[:, xc:xc + 8192].rearrange("p (h t) -> p h t", h=NH),
                  src_t.ap()[rank * H:(rank + 1) * H, 0:512]
                  .rearrange("(h p) t -> p h t", p=128))
        wait("tensor", "sLx0" if buf == 0 else "sLx1", ldv)
        for j in range(4):
            bank = psg[j // 2][:, 512 * (j % 2):512 * (j % 2) + 512]
            if (j, kk - 1) in qk_evac:
                wait("tensor", "sAC", qk_evac[(j, kk - 1)])
            for hh in range(NH):
                mm(bank, w6[:, H * j + 128 * hh: H * j + 128 * (hh + 1)],
                   xnstr[:, xc + 512 * hh: xc + 512 * (hh + 1)],
                   start=(hh == 0), stop=(hh == NH - 1), inc=(hh == NH - 1))
            m_qk = P.cur("sMM")
            wait("scalar", "sMM", m_qk)
            dst = qT if j < 2 else kT
            slot = j % 2
            scl = SCALE if j < 2 else 1.0
            qk_evac[(j, kk)] = act(
                dst[:, slot * S + 512 * tt: slot * S + 512 * (tt + 1)],
                bank, AF.Identity, bias=tbl["bq4_s"][:, j:j + 1], scale=scl)
        for sub in range(4):
            bankA = pv2[sub % 2]
            bankB = db2[sub % 2]
            key = (sub % 2, 4 * kk + sub)
            prev = (sub % 2, 4 * kk + sub - 2)
            if prev in v_evac:
                wait("tensor", "sAC", v_evac[prev])
            for hh in range(NH):
                for slot, bk in ((0, bankA), (1, bankB)):
                    mm(bk[:, 0:128],
                       xnstr[:, xc + 512 * hh + 128 * sub: xc + 512 * hh + 128 * (sub + 1)],
                       wv2[:, H * slot + 128 * hh: H * slot + 128 * (hh + 1)],
                       start=(hh == 0), stop=False)
            for slot, bk in ((0, bankA), (1, bankB)):
                m_v = mm(bk[:, 0:128], tbl["ones1f"][:],
                         tbl["bv2f"][:, 128 * slot:128 * (slot + 1)],
                         start=False, stop=True, inc=(slot == 1))
            stream_last_mm[kk] = m_v
            wait("scalar", "sMM", m_v)
            tg = 4 * tt + sub
            act(vT[:, 0 * S + 128 * tg: 0 * S + 128 * (tg + 1)],
                bankA[:, 0:128], AF.Identity, scale=1.0)
            v_evac[key] = act(vT[:, S + 128 * tg: S + 128 * (tg + 1)],
                              bankB[:, 0:128], AF.Identity, scale=1.0)
        pop_select(2 if kk < 6 else 1)

    sb_free(cm_xnstr)
    sb_free(cm_wv2)
    sb_free(cm_w6)

    b4j_s, cm_b4j = sb("b4j_s", [128, S], BF16)
    b4i_s, cm_b4i = sb("b4i_s", [128, S], BF16)
    masks, cm_masks = sb("masks", [128, 2048], BF16)
    wait("sync", "sMM", P.cur("sMM"))
    wait("sync", "sAC", P.cur("sAC"))
    dma("sync", "sLtab", b4j_s[:], ap["b4j"][:, 0:S])
    dma("sync", "sLtab", b4i_s[:], ap["b4i"][:, 0:S])
    ld_b4 = dma("sync", "sLtab", masks[:], ap["maskneg"])
    wait("tensor", "sLtab", ld_b4)
    wait("vector", "sLtab", ld_b4)
    wait("tensor", "sAC", P.cur("sAC"))

    exp_in_bank = {0: 0, 1: 0}
    pt_mark = {0: 0, 1: 0}
    pvdb_guard = {}

    for g in range(32):
        slot = g // 16
        s16 = g % 16
        so = slot * S
        i0 = 512 * s16
        nch = 4 * s16 + 4
        jc0 = max(0, (i0 - 2048) // 128) if slot == 0 else 0
        jc0 -= jc0 % 2
        ngrp_all = nch // 2
        pvb = pv2[g % 2]
        db = db2[g % 2]
        if g == 16:
            wait("gpsimd", "sVE", pvdb_guard[15][0])
            g_c1 = dma("gpsimd", "sLG", ag_at_in1.ap(), aoT[:, 0:S])
            wait("gpsimd", "sLG", g_c1)
            P.emit("gpsimd", lambda e: e.collective_compute(
                "AllGather", ALU.bypass, replica_groups=[list(range(NCORE))],
                ins=[ag_at_in1.ap().opt()],
                outs=[ag_at_out1.ap().opt()]).then_inc(SEM["sCAT"]))
            last_mm_slot0 = P.cur("sMM")
            wait("sync", "sMM", last_mm_slot0)
            dma("sync", "sLtab", b4j_s[0:4, :], ap["b4j"][0:4, S:2 * S])
            ld_sl = dma("sync", "sLtab", b4i_s[0:4, :], ap["b4i"][0:4, S:2 * S])
            wait("tensor", "sLtab", ld_sl)
        if (g - 2) in pvdb_guard:
            v_n2, a_ln2 = pvdb_guard[g - 2]
            wait("tensor", "sVE", v_n2)
            wait("tensor", "sAC", a_ln2)
        den_marks = {}
        exp_marks = {}

        def emit_pvden(grp):
            wait("tensor", "sAC", exp_marks[grp])
            for half in range(2):
                jc = 2 * grp + half
                pt = ptile[:, 1024 * (grp % 2) + 512 * half:
                           1024 * (grp % 2) + 512 * (half + 1)]
                mm(pvb[:], vT[:, so + 128 * jc: so + 128 * (jc + 1)], pt,
                   start=(jc == jc0), stop=(jc == nch - 1))
                den_marks[jc] = mm(db[:], tbl["onesj"][:], pt,
                                   start=(jc == jc0), stop=(jc == nch - 1), inc=True)
            pt_mark[grp % 2] = den_marks[2 * grp + 1]

        g0 = jc0 // 2
        for grp in range(g0, ngrp_all):
            sgb = psg[grp % 2]
            if exp_in_bank[grp % 2]:
                wait("tensor", "sAC", exp_in_bank[grp % 2])
            m_b = 0
            for half in range(2):
                jc = 2 * grp + half
                sl = sgb[:, 512 * half:512 * half + 512]
                mm(sl, kT[:, so + 128 * jc: so + 128 * (jc + 1)],
                   qT[:, so + i0: so + i0 + 512], start=True, stop=False)
                m_b = mm(sl, b4j_s[:, 128 * jc: 128 * (jc + 1)],
                         b4i_s[:, i0: i0 + 512],
                         start=False, stop=True, inc=True)
            if grp >= ngrp_all - 2:
                dd = grp - (ngrp_all - 2)
                wait("vector", "sMM", m_b)
                v_m = vop("tensor_tensor", sgb[:], sgb[:],
                          masks[:, 1024 * dd:1024 * (dd + 1)], ALU.add)
                wait("scalar", "sVE", v_m)
            wait("scalar", "sMM", m_b)
            if pt_mark[grp % 2]:
                wait("scalar", "sMM", pt_mark[grp % 2])
            e_m = act(ptile[:, 1024 * (grp % 2):1024 * (grp % 2 + 1)], sgb[:],
                      AF.Exp)
            exp_in_bank[grp % 2] = e_m
            exp_marks[grp] = e_m
            if grp - ATT_LAG >= g0:
                emit_pvden(grp - ATT_LAG)
        for gtail in range(max(g0, ngrp_all - ATT_LAG), ngrp_all):
            emit_pvden(gtail)
        m_last_den = den_marks[nch - 1]
        wait("scalar", "sMM", m_last_den)
        act(lntmp[:], db[:], AF.Ln)
        a_ln = act(rcpr[:], lntmp[:], AF.Exp, scale=-1.0)
        wait("vector", "sAC", a_ln)
        v_n = vop("tensor_tensor", aoT[:, so + i0: so + i0 + 512],
                  pvb[:], rcpr[:], ALU.mult)
        pvdb_guard[g] = (v_n, a_ln)
        if slot == 1 or s16 % 2 == 0:
            pop_select(1)
    pop_select(len(select_units))

    att_done = P.cur("sVE")
    if taps:
        wait("gpsimd", "sVE", att_done)
        dma("gpsimd", "sLG", tap_ap["t_qT"], qT[:])
        dma("gpsimd", "sLG", tap_ap["t_kT"], kT[:])
        dma("gpsimd", "sLG", tap_ap["t_vT"], vT[:])
        dma("gpsimd", "sLG", tap_ap["t_aoT"], aoT[:])
    wait("gpsimd", "sVE", att_done)
    g_ctr2 = dma("gpsimd", "sLG", ag_at_in2.ap(), aoT[:, S:2 * S])

    for cm in (cm_masks, cm_b4i, cm_b4j, cm_rcpr, cm_lntmp, cm_selt, cm_selw,
               cm_idx_i2, cm_idx_i, cm_ptile, cm_aoT, cm_vT, cm_kT,
               cm_qT):
        sb_free(cm)

    h_T, cm_hT = sb("h_T", [128, NIT * TOK], BF16)
    ao_own, cm_ao_own = sb("ao_own", [128, NH * TOK], BF16)
    wgu_cms = []
    wgu = []
    for wn in ("wgu0", "wgu1"):
        t, cmw = sb(wn, [128, 2 * H], BF16)
        wgu.append(t)
        wgu_cms.append(cmw)
    sg, cm_sg = sb("sg", [128, 512], BF16)
    xnown2, cm_xnown2 = sb("xnown2", [128, NH * TOK], BF16)

    for s_name in ("sMM", "sVE", "sAC", "sLcon", "sLsel", "sLwl", "sLtab"):
        wait("sync", s_name, P.cur(s_name))
    wait("sync", "sLG", g_ctr2)
    if taps:
        for e2 in ("sync", "vector", "scalar"):
            wait(e2, "sLG", P.cur("sLG"))
    dma("sync", "sLtab",
        xnown2[:].rearrange("p (h t) -> p h t", h=NH)[:, :, 0:512],
        ag_xn_in.ap().rearrange("(h p) t -> p h t", p=128))
    ld_xn2 = dma("sync", "sLtab",
                 xnown2[:].rearrange("p (h t) -> p h t", h=NH)[:, :, 512:1024],
                 ag_xn_in2.ap().rearrange("(h p) t -> p h t", p=128))
    wait("tensor", "sCGU", 1)
    wait("sync", "sCGU", 1)
    wait("tensor", "sLtab", ld_xn2)
    wait("tensor", "sAC", P.cur("sAC"))
    wait("tensor", "sVE", P.cur("sVE"))

    gu_last_mm = {}
    silu_marks = {}
    htt_marks = {}
    for ti in range(NIT):
        buf = wgu[ti % 2]
        rank = ti % NCORE
        loc = ti // NCORE
        gs = "sLw0" if ti % 2 == 0 else "sLw1"
        if ti >= 2:
            wait("sync", "sMM", gu_last_mm[ti - 2])
        dma("sync", gs, buf[:, 0:H].rearrange("p (h w) -> p h w", h=NH),
            ag_gu_out.ap()[rank * H:(rank + 1) * H, 128 * loc:128 * (loc + 1)]
            .rearrange("(h p) w -> p h w", p=128))
        ldv = dma("sync", gs, buf[:, H:2 * H].rearrange("p (h w) -> p h w", h=NH),
                  ag_gu_out.ap()[rank * H:(rank + 1) * H, 768 + 128 * loc:768 + 128 * (loc + 1)]
                  .rearrange("(h p) w -> p h w", p=128))
        if ti == 4:
            wait("gpsimd", gs, ldv)
            P.emit("gpsimd", lambda e: e.collective_compute(
                "AllGather", ALU.bypass, replica_groups=[list(range(NCORE))],
                ins=[ag_at_in2.ap().opt()],
                outs=[ag_at_out2.ap().opt()]).then_inc(SEM["sCAT"]))
        wait("tensor", gs, ldv)
        for t2 in range(2):
            if (ti - 1, t2) in silu_marks:
                wait("tensor", "sAC", silu_marks[(ti - 1, t2)])
            if (ti - 1, t2) in htt_marks:
                wait("tensor", "sVE", htt_marks[(ti - 1, t2)])
        gm = {}
        for hh in range(NH):
            for t2 in range(2):
                gm[t2] = mm(psg[0][:, 512 * t2:512 * (t2 + 1)],
                            buf[:, 128 * hh:128 * (hh + 1)],
                            xnown2[:, hh * TOK + 512 * t2: hh * TOK + 512 * (t2 + 1)],
                            start=(hh == 0), stop=(hh == NH - 1),
                            inc=(hh == NH - 1))
        for hh in range(NH):
            for t2 in range(2):
                gu_last_mm[ti] = mm(
                    psg[1][:, 512 * t2:512 * (t2 + 1)],
                    buf[:, H + 128 * hh:H + 128 * (hh + 1)],
                    xnown2[:, hh * TOK + 512 * t2: hh * TOK + 512 * (t2 + 1)],
                    start=(hh == 0), stop=(hh == NH - 1), inc=(hh == NH - 1))
        m_u = gu_last_mm[ti]
        for t2 in range(2):
            gbank = psg[0][:, 512 * t2:512 * (t2 + 1)]
            ubank = psg[1][:, 512 * t2:512 * (t2 + 1)]
            wait("scalar", "sMM", gm[t2])
            if (ti, t2 - 1) in htt_marks:
                wait("scalar", "sVE", htt_marks[(ti, t2 - 1)])
            elif t2 == 0 and (ti - 1, 1) in htt_marks:
                wait("scalar", "sVE", htt_marks[(ti - 1, 1)])
            a_s = act(sg[:], gbank, AF.Silu)
            silu_marks[(ti, t2)] = a_s
            wait("vector", "sMM", m_u)
            wait("vector", "sAC", a_s)
            htt_marks[(ti, t2)] = vop(
                "tensor_tensor", h_T[:, ti * TOK + 512 * t2: ti * TOK + 512 * (t2 + 1)],
                sg[:], ubank, ALU.mult)
    if taps:
        wait("gpsimd", "sVE", P.cur("sVE"))
        dma("gpsimd", "sLG", tap_ap["t_hT"], h_T[:])

    wait("gpsimd", "sCAT", 2)

    def _ao_own_load(e):
        with e.register("rc") as rc, e.register("ro") as ro:
            e.reg_load(rc, ap["cid"][0:1, 0:1])
            e.reg_mul(ro, rc, TOK)
            for d in range(NH):
                src = (ag_at_out1 if d % 2 == 0 else ag_at_out2).ap()
                r = d // 2
                e.dma_start(
                    out=ao_own[:, d * TOK:(d + 1) * TOK],
                    in_=src[128 * r:128 * (r + 1),
                            bass.ds(bass.RuntimeValue(ro), TOK)],
                ).then_inc(SEM["sLG"], 16)
    P.bump("sLG", 16 * NH)
    P.emit("gpsimd", _ao_own_load)
    g_ao = P.cur("sLG")
    if taps:
        wait("gpsimd", "sLG", g_ao)
        dma("gpsimd", "sLG", tap_ap["t_ao_own"], ao_own[:])

    sb_free(cm_xnown2)
    sb_free(cm_sg)
    sb_free(wgu_cms[1])
    sb_free(wgu_cms[0])

    wlhs0, _ = sb("wlhs0", [128, (16 + 48) * 128], BF16)
    wlhs1, _ = sb("wlhs1", [128, (16 + 48) * 128], BF16)
    ostage0, _ = sb("ostage0", [128, 512], F32)
    ostage1, _ = sb("ostage1", [128, 512], F32)
    outasm, _ = sb("outasm", [128, 4 * H], F32)
    xres, _ = sb("xres", [128, H], F32)
    wst0, _ = sb("wst0", [128, H], F32)

    wait("sync", "sCWD", 1)
    wait("tensor", "sCWD", 1)
    wait("tensor", "sLG", g_ao)
    barrier_waits("tensor")
    barrier_waits("sync")
    barrier_waits("vector")
    barrier_waits("scalar")

    evac_marks = {}
    tr_marks = {}
    cp_marks = {}
    wo_cast = {}
    wl_last = {0: 0, 1: 0}
    store_marks = []
    hp_seq = [(a2, hp) for a2 in range(2) for hp in range(16)]
    for idx, (a2, hp) in enumerate(hp_seq):
        wl = (wlhs0, wlhs1)[idx % 2]
        osg = (ostage0, ostage1)[idx % 2]
        mainbank = psg[idx % 2][:, 0:512]
        if a2 == 1 and hp == 0 and store_marks:
            wait("vector", "sLS", store_marks[-1])
        wls = "sLw0" if idx % 2 == 0 else "sLw1"
        if wl_last[idx % 2]:
            wait("sync", "sMM", wl_last[idx % 2])
        if idx >= 1:
            wait("sync", "sVE", wo_cast[idx - 1])
        ld_wo = dma("sync", wls, wst0[:].rearrange("p (d w) -> p d w", d=NH),
                    ap["wo"][:, 128 * hp:128 * (hp + 1)].rearrange("(d p) w -> p d w", p=128))
        wd_v = ag_wd_out.ap().rearrange("(e q) w -> q e w", e=NCORE)
        for l6 in range(6):
            ldv = dma("sync", wls,
                      wl[:, H + 1024 * l6:H + 1024 * (l6 + 1)]
                      .rearrange("p (ee w) -> p ee w", ee=8),
                      wd_v[128 * l6:128 * (l6 + 1), :, 128 * hp:128 * (hp + 1)])
        wait("vector", wls, ldv)
        if wl_last[idx % 2]:
            wait("vector", "sMM", wl_last[idx % 2])
        wo_cast[idx] = vop("tensor_copy", wl[:, 0:H], wst0[:])
        wait("tensor", wls, ldv)
        wait("tensor", "sVE", wo_cast[idx])
        if (idx - 2) in evac_marks:
            wait("tensor", "sAC", evac_marks[idx - 2])
        for d in range(NH):
            mm(mainbank, wl[:, 128 * d:128 * (d + 1)],
               ao_own[:, d * TOK + 512 * a2: d * TOK + 512 * (a2 + 1)],
               start=(d == 0), stop=False)
        for ti in range(NIT):
            mm(mainbank, wl[:, H + 128 * ti:H + 128 * (ti + 1)],
               h_T[:, ti * TOK + 512 * a2: ti * TOK + 512 * (a2 + 1)],
               start=False, stop=(ti == NIT - 1), inc=(ti == NIT - 1))
        m_e = P.cur("sMM")
        wl_last[idx % 2] = m_e
        wait("scalar", "sMM", m_e)
        if (idx - 2) in tr_marks:
            wait("scalar", "sMM", tr_marks[idx - 2])
        a_e = act(osg[:], mainbank, AF.Identity,
                  bias=tbl["bo_s"][:, hp:hp + 1], scale=1.0)
        evac_marks[idx] = a_e
        wait("tensor", "sAC", a_e)
        for t4 in range(4):
            tbank = pv2[t4 % 2]
            key = (idx, t4)
            prevkey = (idx, t4 - 2) if t4 >= 2 else (idx - 1, t4 + 2)
            if prevkey in cp_marks:
                wait("tensor", "sVE", cp_marks[prevkey])
            m_t = trans(tbank[:, 0:128], osg[:, 128 * t4:128 * (t4 + 1)],
                        tbl["identf_s"][:], inc=True)
            if t4 == 3:
                tr_marks[idx] = m_t
            wait("vector", "sMM", m_t)
            cp_marks[key] = vop(
                "tensor_copy", outasm[:, t4 * H + 128 * hp: t4 * H + 128 * (hp + 1)],
                tbank[:, 0:128])
        if hp == 15:
            v_add_prev = 0
            for t4 in range(4):
                if v_add_prev:
                    wait("sync", "sVE", v_add_prev)
                ldx = dma("sync", "sLxr", xres[:],
                          ap["x_own"][512 * a2 + 128 * t4: 512 * a2 + 128 * (t4 + 1), :])
                wait("vector", "sLxr", ldx)
                v_add_prev = vop("tensor_tensor", outasm[:, t4 * H:(t4 + 1) * H],
                                 outasm[:, t4 * H:(t4 + 1) * H], xres[:], ALU.add)
                wait("scalar", "sVE", v_add_prev)
                st = dma("scalar", "sLS",
                         out_ap[512 * a2 + 128 * t4: 512 * a2 + 128 * (t4 + 1), :],
                         outasm[:, t4 * H:(t4 + 1) * H])
                store_marks.append(st)

    wait("scalar", "sLS", store_marks[-1])
    wait("gpsimd", "sLS", store_marks[-1])

    with nc.Block() as block:
        @block.sync
        def _(e):
            for f in P.streams["sync"]:
                f(e)

        @block.tensor
        def _(e):
            for f in P.streams["tensor"]:
                f(e)

        @block.vector
        def _(e):
            for f in P.streams["vector"]:
                f(e)

        @block.scalar
        def _(e):
            for f in P.streams["scalar"]:
                f(e)

        @block.gpsimd
        def _(e):
            for f in P.streams["gpsimd"]:
                f(e)

    for cm in reversed(list(open_cms)):
        sb_free(cm)
    for cm in list(reversed(ps_cms)) + list(reversed(sem_cms)):
        cm.__exit__(None, None, None)
    return nc


def _host_inputs(x, Wqkv, bqkv, Wo, bo, lut, gidx, uidx, didx):
    x2d = np.ascontiguousarray(x.reshape(S, H))
    slopes = [2.0 ** (-(k) / 2.0) for k in range(NH)]
    jj = np.arange(128)[:, None]
    icq = np.arange(512)[None, :]
    maskneg = np.zeros((128, 2048), BF)
    for dd in range(2):
        for hl in range(2):
            d = 2 * dd + hl
            maskneg[:, 1024 * dd + 512 * hl:1024 * dd + 512 * (hl + 1)] = \
                np.where(128 * d + jj <= icq, 0.0, NEG).astype(BF)
    identf = np.eye(128, dtype=np.float32)
    pos = np.arange(S, dtype=np.float64)

    perm = np.empty(H, np.int64)
    for r in range(NCORE):
        for s2 in range(2):
            h = r + 8 * s2
            perm[256 * r + 128 * s2:256 * r + 128 * (s2 + 1)] = \
                np.arange(128 * h, 128 * (h + 1))
    wo_perm = np.ascontiguousarray(Wo[perm, :])

    in_maps = []
    for c in range(NCORE):
        heads = (c, 8 + c)
        cols = []
        for base in (0, 2048, 4096):
            for h in heads:
                cols.append((base + 128 * h, base + 128 * (h + 1)))
        wqkv_c = np.ascontiguousarray(
            np.concatenate([Wqkv[:, a:b] for a, b in cols], axis=1))
        bq4 = np.ascontiguousarray(
            np.stack([bqkv[a:b] for a, b in cols[:4]], axis=1))
        bv2 = np.concatenate([bqkv[a:b] for a, b in cols[4:]])[None, :]
        tiles = [c + 8 * k for k in range(6)]

        def _rows(idx_mat, t):
            if t < NIT:
                return idx_mat[128 * t:128 * (t + 1)]
            return np.zeros((128, idx_mat.shape[1]), np.int32)
        gidxT = np.ascontiguousarray(
            np.concatenate([_rows(gidx, t) for t in tiles]).T.astype(np.int32))
        uidxT = np.ascontiguousarray(
            np.concatenate([_rows(uidx, t) for t in tiles]).T.astype(np.int32))

        def _cols(idx_mat, t):
            if t < NIT:
                return idx_mat[:, 128 * t:128 * (t + 1)]
            return np.zeros((idx_mat.shape[0], 128), np.int32)
        didxT = np.ascontiguousarray(
            np.concatenate([_cols(didx, t) for t in tiles], axis=1).T.astype(np.int32))
        b4j = np.zeros((128, 2 * S), BF)
        b4i = np.zeros((128, 2 * S), BF)
        b4j[2, :] = 1.0
        b4j[3, :] = 1.0
        b4i[0, :] = 1.0
        b4i[1, :] = 1.0
        for sl, h in enumerate(heads):
            v = slopes[h] * pos
            hi = v.astype(np.float32).astype(BF)
            lo = (v.astype(np.float32) - hi.astype(np.float32)).astype(BF)
            b4j[0, sl * S:(sl + 1) * S] = hi
            b4j[1, sl * S:(sl + 1) * S] = lo
            b4i[2, sl * S:(sl + 1) * S] = -hi
            b4i[3, sl * S:(sl + 1) * S] = -lo
        in_maps.append({
            "cid": np.array([[c]], np.int32),
            "x_own": np.ascontiguousarray(x2d[TOK * c:TOK * (c + 1)]),
            "wqkv": wqkv_c,
            "bq4": bq4.astype(np.float32),
            "bv2": bv2.astype(np.float32),
            "wo": wo_perm,
            "bo_t": np.ascontiguousarray(bo.reshape(16, 128).T),
            "lut": lut.reshape(1, LUTN).astype(np.float32),
            "gidxT": gidxT, "uidxT": uidxT, "didxT": didxT,
            "b4j": b4j, "b4i": b4i,
            "maskneg": maskneg, "identf": identf,
        })
    return in_maps


def kernel(L0_attn_x, L0_attn_Wqkv, L0_attn_bqkv, L0_attn_Wo, L0_attn_bo,
           L0_ffn_lut, L0_ffn_gate_idx, L0_ffn_up_idx, L0_ffn_down_idx):
    in_maps = _host_inputs(
        np.asarray(L0_attn_x, np.float32), np.asarray(L0_attn_Wqkv, np.float32),
        np.asarray(L0_attn_bqkv, np.float32), np.asarray(L0_attn_Wo, np.float32),
        np.asarray(L0_attn_bo, np.float32), np.asarray(L0_ffn_lut, np.float32),
        np.asarray(L0_ffn_gate_idx, np.int32), np.asarray(L0_ffn_up_idx, np.int32),
        np.asarray(L0_ffn_down_idx, np.int32))
    nc = build_nc(taps=False)
    res = run_bass_kernel_spmd(nc, in_maps, core_ids=list(range(NCORE)))
    out = np.concatenate([np.asarray(res.results[c]["out"]) for c in range(NCORE)],
                         axis=0)
    return out.reshape(1, S, H).astype(np.float32)


# version 17
# speedup vs baseline: 1.0498x; 1.0498x over previous
import numpy as np
import ml_dtypes
import concourse.bass as bass
import concourse.mybir as mybir
from concourse.bass_utils import run_bass_kernel_spmd

F32 = mybir.dt.float32
BF16 = mybir.dt.bfloat16
I32 = mybir.dt.int32
AF = mybir.ActivationFunctionType
ALU = mybir.AluOpType
AX = mybir.AxisListType

NCORE = 8
S = 8192
TOK = 1024
H = 2048
NH = 16
DH = 128
NIT = 43
LUTN = 16
SCALE = float(1.0 / np.sqrt(128.0))
NEG = -1.0e30
BF = ml_dtypes.bfloat16
ATT_LAG = 1


class Plan:
    def __init__(self):
        self.streams = {"sync": [], "tensor": [], "vector": [], "scalar": [], "gpsimd": []}
        self.cnt = {}

    def emit(self, eng, fn):
        self.streams[eng].append(fn)

    def bump(self, s_name, n):
        self.cnt[s_name] = self.cnt.get(s_name, 0) + n
        return self.cnt[s_name]

    def cur(self, s_name):
        return self.cnt.get(s_name, 0)


def build_nc(taps=False):
    nc = bass.Bass("TRN2", target_bir_lowering=False, debug=False)

    ap = {}
    for name, shape, dt in [
        ("cid", [1, 1], I32),
        ("x_own", [TOK, H], F32),
        ("wqkv", [H, 768], F32),
        ("bq4", [128, 4], F32),
        ("bv2", [1, 256], F32),
        ("wo", [H, H], F32),
        ("bo_t", [128, 16], F32),
        ("lut", [1, LUTN], F32),
        ("gidxT", [H, 768], I32),
        ("uidxT", [H, 768], I32),
        ("didxT", [768, H], I32),
        ("b4j", [128, 2 * S], BF16),
        ("b4i", [128, 2 * S], BF16),
        ("maskneg", [128, 2048], BF16),
        ("identf", [128, 128], F32),
    ]:
        ap[name] = nc.dram_tensor(name, shape, dt, kind="ExternalInput").ap()
    out_ap = nc.dram_tensor("out", [TOK, H], F32, kind="ExternalOutput").ap()
    tap_ap = {}
    if taps:
        for name, shape, dt in [
            ("t_xnT", [128, NH * TOK], BF16),
            ("t_qT", [128, 2 * S], BF16),
            ("t_kT", [128, 2 * S], BF16),
            ("t_vT", [128, 2 * S], BF16),
            ("t_aoT", [128, 2 * S], BF16),
            ("t_hT", [128, NIT * TOK], BF16),
            ("t_ao_own", [128, NH * TOK], BF16),
        ]:
            tap_ap[name] = nc.dram_tensor(name, shape, dt, kind="ExternalOutput").ap()

    ag_xn_in = nc.dram_tensor("ag_xn_in", [H, 512], BF16)
    ag_xn_in2 = nc.dram_tensor("ag_xn_in2", [H, 512], BF16)
    ag_xn_out = nc.dram_tensor("ag_xn_out", [NCORE * H, 512], BF16, addr_space="Shared")
    ag_xn_out2 = nc.dram_tensor("ag_xn_out2", [NCORE * H, 512], BF16, addr_space="Shared")
    ag_gu_in = nc.dram_tensor("ag_gu_in", [H, 2 * 768], BF16)
    ag_gu_out = nc.dram_tensor("ag_gu_out", [NCORE * H, 2 * 768], BF16, addr_space="Shared")
    ag_wd_in = nc.dram_tensor("ag_wd_in", [768, H], BF16)
    ag_wd_out = nc.dram_tensor("ag_wd_out", [NCORE * 768, H], BF16, addr_space="Shared")
    ag_at_in1 = nc.dram_tensor("ag_at_in1", [128, S], BF16)
    ag_at_out1 = nc.dram_tensor("ag_at_out1", [NCORE * 128, S], BF16, addr_space="Shared")
    ag_at_in2 = nc.dram_tensor("ag_at_in2", [128, S], BF16)
    ag_at_out2 = nc.dram_tensor("ag_at_out2", [NCORE * 128, S], BF16, addr_space="Shared")

    P = Plan()
    SEM = {}
    open_cms = []

    def sb(name, shape, dt):
        cm = nc.sbuf_tensor(name, shape, dt)
        t = cm.__enter__()
        open_cms.append(cm)
        return t, cm

    def sb_free(cm):
        cm.__exit__(None, None, None)
        open_cms.remove(cm)

    def dma(eng, s_name, out, in_, **kw):
        v = P.bump(s_name, 16)
        s_h = SEM[s_name]
        P.emit(eng, lambda e, o=out, i=in_, s_h=s_h, kw=kw:
               e.dma_start(out=o, in_=i, **kw).then_inc(s_h, 16))
        return v

    def wait(eng, s_name, val):
        if val <= 0:
            return
        s_h = SEM[s_name]
        P.emit(eng, lambda e, s_h=s_h, v=val: e.wait_ge(s_h, v))

    def mm(out, lhsT, rhs, start, stop, inc=False):
        if inc:
            v = P.bump("sMM", 1)
            s_h = SEM["sMM"]
            P.emit("tensor", lambda e, o=out, l=lhsT, r=rhs, a=start, b=stop, s_h=s_h:
                   e.matmul(o, l, r, start=a, stop=b).then_inc(s_h, 1))
            return v
        P.emit("tensor", lambda e, o=out, l=lhsT, r=rhs, a=start, b=stop:
               e.matmul(o, l, r, start=a, stop=b))

    def trans(out, in_, ident, inc=False):
        if inc:
            v = P.bump("sMM", 1)
            s_h = SEM["sMM"]
            P.emit("tensor", lambda e, o=out, i=in_, d=ident, s_h=s_h:
                   e.transpose(o, i, d).then_inc(s_h, 1))
            return v
        P.emit("tensor", lambda e, o=out, i=in_, d=ident: e.transpose(o, i, d))

    def act(out, in_, func, bias=None, scale=None):
        kw = {}
        if bias is not None:
            kw["bias"] = bias
        if scale is not None:
            kw["scale"] = scale
        prev = P.cur("sAC")
        v = P.bump("sAC", 1)
        s_h = SEM["sAC"]
        P.emit("scalar", lambda e, o=out, i=in_, f=func, kw=kw, s_h=s_h, pv=prev:
               (e.wait_ge(s_h, pv) if pv else None,
                e.activation(o, i, f, **kw).then_inc(s_h, 1)))
        return v

    def vop(name, *args, **kw):
        prev = P.cur("sVE")
        v = P.bump("sVE", 1)
        s_h = SEM["sVE"]
        P.emit("vector", lambda e, n=name, a=args, kw=kw, s_h=s_h, pv=prev:
               (e.wait_ge(s_h, pv) if pv else None,
                getattr(e, n)(*a, **kw).then_inc(s_h, 1)))
        return v

    DMA_SEMS = ("sLtab", "sLln", "sLsel", "sLcon", "sLwo", "sLx0", "sLx1",
                "sLw0", "sLw1", "sLwl", "sLxr", "sLG", "sLS")

    def barrier_waits(eng):
        for s_name in ("sMM", "sAC", "sVE") + DMA_SEMS:
            wait(eng, s_name, P.cur(s_name))

    sem_cms = []
    for s_name in DMA_SEMS + ("sMM", "sAC", "sVE", "sCXN", "sCX2", "sCGU", "sCWD", "sCAT"):
        cm = nc.semaphore(s_name)
        SEM[s_name] = cm.__enter__()
        sem_cms.append(cm)

    ps_cms = []

    def psum(name, shape):
        cm = nc.psum_tensor(name, shape, F32)
        t = cm.__enter__()
        ps_cms.append(cm)
        return t
    psg = [psum("psg0", [128, 1024]), psum("psg1", [128, 1024])]
    pv2 = [psum("pv0", [128, 512]), psum("pv1", [128, 512])]
    db2 = [psum("db0", [128, 512]), psum("db1", [128, 512])]

    tbl = {}
    for name, shape, dt in [
        ("onesj", [128, 128], BF16),
        ("ones1f", [1, 128], F32),
        ("identf_s", [128, 128], F32),
        ("bo_s", [128, 16], F32), ("bq4_s", [128, 4], F32),
        ("bv2f", [1, 256], F32), ("lutrow", [1, LUTN], F32),
        ("lutrep", [128, LUTN], F32), ("eps", [128, 1], F32),
    ]:
        tbl[name], _ = sb(name, shape, dt)

    xnownT, cm_xnownT = sb("xnownT", [128, NH * TOK], BF16)

    ld0 = 0
    for dst, src in [
        (tbl["identf_s"][:], ap["identf"]), (tbl["bo_s"][:], ap["bo_t"]),
        (tbl["bq4_s"][:], ap["bq4"]), (tbl["bv2f"][:], ap["bv2"]),
        (tbl["lutrow"][:], ap["lut"]),
    ]:
        ld0 = dma("sync", "sLtab", dst, src)

    vop("memset", tbl["eps"][:], 1.0e-5)
    vop("memset", tbl["onesj"][:], 1.0)
    v0 = vop("memset", tbl["ones1f"][:], 1.0)
    wait("vector", "sLtab", ld0)
    vop("tensor_scalar", tbl["bq4_s"][:, 0:2], tbl["bq4_s"][:, 0:2],
        SCALE, None, ALU.mult)
    wait("tensor", "sLtab", ld0)
    wait("tensor", "sVE", v0)
    m_lut = mm(db2[0][:, 0:LUTN], tbl["ones1f"][:], tbl["lutrow"][:], True, True, inc=True)
    wait("vector", "sMM", m_lut)
    v_lutrep = vop("tensor_copy", tbl["lutrep"][:], db2[0][:, 0:LUTN])

    ln_x, cm_ln_x = sb("ln_x", [128, 8 * H], F32)
    x2s, cm_x2s = sb("x2s", [128, H], F32)
    ln_st, cm_ln_st = sb("ln_st", [128, 4 * 8], F32)
    xn_t, cm_xn_t = sb("xn_t", [128, 2 * H], F32)

    ld_a = dma("sync", "sLln", ln_x[:, 0:4 * H].rearrange("p (a w) -> p a w", a=4),
               ap["x_own"][0:512, :].rearrange("(a p) w -> p a w", p=128))
    ld_b = dma("sync", "sLx1", ln_x[:, 4 * H:8 * H].rearrange("p (a w) -> p a w", a=4),
               ap["x_own"][512:1024, :].rearrange("(a p) w -> p a w", p=128))
    for tau in range(8):
        if tau == 0:
            wait("vector", "sLln", ld_a)
        elif tau == 4:
            wait("vector", "sLx1", ld_b)
        xt = ln_x[:, H * tau:H * (tau + 1)]
        vop("reduce_sum", ln_st[:, 4 * tau:4 * tau + 1], xt, axis=AX.X)
        vop("tensor_scalar", ln_st[:, 4 * tau:4 * tau + 1],
            ln_st[:, 4 * tau:4 * tau + 1], 1.0 / H, None, ALU.mult)
        vop("tensor_tensor", x2s[:], xt, xt, ALU.mult)
        vop("reduce_sum", ln_st[:, 4 * tau + 1:4 * tau + 2], x2s[:], axis=AX.X)
        vop("tensor_scalar", ln_st[:, 4 * tau + 1:4 * tau + 2],
            ln_st[:, 4 * tau + 1:4 * tau + 2], 1.0 / H, None, ALU.mult)
        vop("tensor_tensor", ln_st[:, 4 * tau + 2:4 * tau + 3],
            ln_st[:, 4 * tau:4 * tau + 1], ln_st[:, 4 * tau:4 * tau + 1], ALU.mult)
        vop("tensor_tensor", ln_st[:, 4 * tau + 2:4 * tau + 3],
            ln_st[:, 4 * tau + 1:4 * tau + 2], ln_st[:, 4 * tau + 2:4 * tau + 3],
            ALU.subtract)
    v_var = P.cur("sVE")
    wait("scalar", "sVE", v_var)
    a_sd = act(ln_st[:].rearrange("p (a w) -> p a w", a=8)[:, :, 3:4],
               ln_st[:].rearrange("p (a w) -> p a w", a=8)[:, :, 2:3],
               AF.Sqrt, bias=tbl["eps"][:], scale=1.0)
    wait("vector", "sAC", a_sd)
    vop("reciprocal", ln_st[:].rearrange("p (a w) -> p a w", a=8)[:, :, 3:4],
        ln_st[:].rearrange("p (a w) -> p a w", a=8)[:, :, 3:4])
    ln_tr_last = {}
    ln_cp = {}
    xnv = xnownT[:].rearrange("p (h t) -> p h t", h=NH)
    for tau in range(8):
        xslot = xn_t[:, H * (tau % 2):H * (tau % 2) + H]
        if tau >= 2:
            wait("vector", "sMM", ln_tr_last[tau - 2])
        v_xn = vop("tensor_scalar", xslot, ln_x[:, H * tau:H * (tau + 1)],
                   ln_st[:, 4 * tau:4 * tau + 1],
                   ln_st[:, 4 * tau + 3:4 * tau + 4],
                   ALU.subtract, op1=ALU.mult)
        wait("tensor", "sVE", v_xn)
        for half in range(2):
            if tau >= 1:
                wait("tensor", "sVE", ln_cp[(tau - 1, half)])
            for h8 in range(8):
                hh = 8 * half + h8
                m_t = trans(psg[half][:, 128 * h8:128 * (h8 + 1)],
                            xslot[:, 128 * hh:128 * (hh + 1)],
                            tbl["identf_s"][:], inc=(h8 == 7))
            ln_tr_last[tau] = m_t
            wait("vector", "sMM", m_t)
            ln_cp[(tau, half)] = vop(
                "tensor_copy",
                xnv[:, 8 * half:8 * (half + 1), 128 * tau:128 * (tau + 1)],
                psg[half][:].rearrange("p (h w) -> p h w", h=8))
        if tau == 3:
            wait("gpsimd", "sVE", P.cur("sVE"))
            g_h1 = dma("gpsimd", "sLG",
                       ag_xn_in.ap().rearrange("(h p) t -> p h t", p=128),
                       xnownT[:].rearrange("p (h t) -> p h t", h=NH)[:, :, 0:512])
            wait("gpsimd", "sLG", g_h1)
            P.emit("gpsimd", lambda e: e.collective_compute(
                "AllGather", ALU.bypass, replica_groups=[list(range(NCORE))],
                ins=[ag_xn_in.ap().opt()],
                outs=[ag_xn_out.ap().opt()]).then_inc(SEM["sCXN"]))
    v_ln_done = P.cur("sVE")

    wait("gpsimd", "sVE", v_ln_done)
    g_xn = dma("gpsimd", "sLG",
               ag_xn_in2.ap().rearrange("(h p) t -> p h t", p=128),
               xnownT[:].rearrange("p (h t) -> p h t", h=NH)[:, :, 512:1024])
    wait("gpsimd", "sLG", g_xn)
    P.emit("gpsimd", lambda e: e.collective_compute(
        "AllGather", ALU.bypass, replica_groups=[list(range(NCORE))],
        ins=[ag_xn_in2.ap().opt()], outs=[ag_xn_out2.ap().opt()]).then_inc(SEM["sCX2"]))
    if taps:
        dma("gpsimd", "sLG", tap_ap["t_xnT"], xnownT[:])
    xn_contrib_mark = P.cur("sLG")

    for cm in (cm_xn_t, cm_ln_st, cm_x2s, cm_ln_x, cm_xnownT):
        sb_free(cm)

    qT, cm_qT = sb("qT", [128, 2 * S], BF16)
    kT, cm_kT = sb("kT", [128, 2 * S], BF16)
    vT, cm_vT = sb("vT", [128, 2 * S], BF16)
    aoT, cm_aoT = sb("aoT", [128, 2 * S], BF16)
    ptile, cm_ptile = sb("ptile", [128, 2 * 1024], BF16)
    idx_i, cm_idx_i = sb("idx_i", [128, 1024], I32)
    idx_i2, cm_idx_i2 = sb("idx_i2", [128, 1024], I32)
    selw, cm_selw = sb("selw", [128, 1024], BF16)
    selt, cm_selt = sb("selt", [128, 1024], BF16)
    lntmp, cm_lntmp = sb("lntmp", [128, 512], F32)
    rcpr, cm_rcpr = sb("rcpr", [128, 512], F32)
    w6, cm_w6 = sb("w6", [128, 4 * H], BF16)
    wv2, cm_wv2 = sb("wv2", [128, 2 * H], BF16)
    wcv0, cm_wcv0 = sb("wcv0", [128, H], F32)
    wcv1, cm_wcv1 = sb("wcv1", [128, H], F32)

    wcv = (wcv0, wcv1)
    cast_marks = {}
    for j in range(6):
        if j >= 2:
            wait("scalar", "sAC", cast_marks[j - 2])
        ldv = dma("scalar", "sLwo",
                  wcv[j % 2][:].rearrange("p (h w) -> p h w", h=NH),
                  ap["wqkv"][:, 128 * j:128 * (j + 1)].rearrange("(h p) w -> p h w", p=128))
        wait("scalar", "sLwo", ldv)
        dst = w6[:, H * j:H * (j + 1)] if j < 4 else wv2[:, H * (j - 4):H * (j - 3)]
        cast_marks[j] = act(dst, wcv[j % 2][:], AF.Identity, scale=1.0)
    a_w_ready = cast_marks[5]
    sb_free(cm_wcv1)
    sb_free(cm_wcv0)

    xnstr, cm_xnstr = sb("xnstr", [128, 2 * 8192], BF16)

    sel_state = {"n": 0, "cp": {}, "last_contrib": 0, "pend": None}

    def flush_contrib():
        if sel_state["pend"] is not None:
            pdst, pcols, pmark = sel_state["pend"]
            wait("gpsimd", "sVE", pmark)
            sel_state["last_contrib"] = dma("gpsimd", "sLcon", pdst,
                                            selw[:, 0:pcols])
            sel_state["pend"] = None

    def emit_select_unit(src_ap, cols, dst_ap):
        ku = sel_state["n"]
        sel_state["n"] = ku + 1
        ib = (idx_i, idx_i2)[ku % 2]
        isem = ("sLsel", "sLwl")[ku % 2]
        gate = sel_state["cp"].get(ku - 2, v_lutrep if ku == 0 else 0)
        if gate:
            wait("gpsimd", "sVE", gate)
        ldv = dma("gpsimd", isem, ib[:, 0:cols], src_ap)
        flush_contrib()
        wait("vector", isem, ldv)
        if sel_state["last_contrib"]:
            wait("vector", "sLcon", sel_state["last_contrib"])
        last = 0
        for k in range(LUTN):
            if k == 0:
                vop("tensor_scalar", selw[:, 0:cols], ib[:, 0:cols],
                    0.0, tbl["lutrep"][:, 0:1], ALU.is_equal, ALU.mult)
            else:
                vop("tensor_scalar", selt[:, 0:cols], ib[:, 0:cols],
                    float(k), tbl["lutrep"][:, k:k + 1], ALU.is_equal, ALU.mult)
                last = vop("tensor_tensor", selw[:, 0:cols], selw[:, 0:cols],
                           selt[:, 0:cols], ALU.add)
        sel_state["cp"][ku] = last
        sel_state["pend"] = (dst_ap, cols, last)

    select_units = []
    for mat_name, col_off in (("gidxT", 0), ("uidxT", 768)):
        for hp in range(16):
            select_units.append((
                ap[mat_name][128 * hp:128 * (hp + 1), :], 768,
                ag_gu_in.ap()[128 * hp:128 * (hp + 1), col_off:col_off + 768]))
    for it6 in range(6):
        for half in range(2):
            select_units.append((
                ap["didxT"][128 * it6:128 * (it6 + 1), 1024 * half:1024 * (half + 1)],
                1024,
                ag_wd_in.ap()[128 * it6:128 * (it6 + 1), 1024 * half:1024 * (half + 1)]))
    sel_idx = [0]
    cc_emitted = {"gu": False, "wd": False}

    def emit_gu_cc():
        wait("gpsimd", "sLcon", P.cur("sLcon"))
        P.emit("gpsimd", lambda e: e.collective_compute(
            "AllGather", ALU.bypass, replica_groups=[list(range(NCORE))],
            ins=[ag_gu_in.ap().opt()], outs=[ag_gu_out.ap().opt()]).then_inc(SEM["sCGU"]))

    def emit_wd_cc():
        wait("gpsimd", "sLcon", P.cur("sLcon"))
        P.emit("gpsimd", lambda e: e.collective_compute(
            "AllGather", ALU.bypass, replica_groups=[list(range(NCORE))],
            ins=[ag_wd_in.ap().opt()], outs=[ag_wd_out.ap().opt()]).then_inc(SEM["sCWD"]))
    def pop_select(n):
        for _ in range(n):
            if sel_idx[0] < len(select_units):
                emit_select_unit(*select_units[sel_idx[0]])
                sel_idx[0] += 1
                if sel_idx[0] == 32 and not cc_emitted["gu"]:
                    cc_emitted["gu"] = True
                    flush_contrib()
                    emit_gu_cc()
                if sel_idx[0] == len(select_units):
                    flush_contrib()

    pop_select(9)

    wait("tensor", "sCXN", 1)
    wait("sync", "sCXN", 1)
    wait("sync", "sLG", xn_contrib_mark)
    wait("sync", "sAC", a_w_ready)
    wait("sync", "sLG", P.cur("sLG"))
    wait("scalar", "sLG", P.cur("sLG"))
    wait("tensor", "sVE", v_ln_done)
    wait("tensor", "sAC", P.cur("sAC"))

    qk_evac = {}
    v_evac = {}
    stream_last_mm = {}
    tt_order = [2 * r for r in range(8)] + [2 * r + 1 for r in range(8)]
    for kk, tt in enumerate(tt_order):
        buf = kk % 2
        xc = buf * 8192
        rank = tt // 2
        src_t = ag_xn_out if tt % 2 == 0 else ag_xn_out2
        if kk == 8:
            wait("sync", "sCX2", 1)
            wait("tensor", "sCX2", 1)
        if kk >= 2:
            wait("sync", "sMM", stream_last_mm[kk - 2])
        ldv = dma("sync", "sLx0" if buf == 0 else "sLx1",
                  xnstr[:, xc:xc + 8192].rearrange("p (h t) -> p h t", h=NH),
                  src_t.ap()[rank * H:(rank + 1) * H, 0:512]
                  .rearrange("(h p) t -> p h t", p=128))
        wait("tensor", "sLx0" if buf == 0 else "sLx1", ldv)
        for j in range(4):
            bank = psg[j // 2][:, 512 * (j % 2):512 * (j % 2) + 512]
            if (j, kk - 1) in qk_evac:
                wait("tensor", "sAC", qk_evac[(j, kk - 1)])
            for hh in range(NH):
                mm(bank, w6[:, H * j + 128 * hh: H * j + 128 * (hh + 1)],
                   xnstr[:, xc + 512 * hh: xc + 512 * (hh + 1)],
                   start=(hh == 0), stop=(hh == NH - 1), inc=(hh == NH - 1))
            m_qk = P.cur("sMM")
            wait("scalar", "sMM", m_qk)
            dst = qT if j < 2 else kT
            slot = j % 2
            scl = SCALE if j < 2 else 1.0
            qk_evac[(j, kk)] = act(
                dst[:, slot * S + 512 * tt: slot * S + 512 * (tt + 1)],
                bank, AF.Identity, bias=tbl["bq4_s"][:, j:j + 1], scale=scl)
        for sub in range(4):
            bankA = pv2[sub % 2]
            bankB = db2[sub % 2]
            key = (sub % 2, 4 * kk + sub)
            prev = (sub % 2, 4 * kk + sub - 2)
            if prev in v_evac:
                wait("tensor", "sAC", v_evac[prev])
            for hh in range(NH):
                for slot, bk in ((0, bankA), (1, bankB)):
                    mm(bk[:, 0:128],
                       xnstr[:, xc + 512 * hh + 128 * sub: xc + 512 * hh + 128 * (sub + 1)],
                       wv2[:, H * slot + 128 * hh: H * slot + 128 * (hh + 1)],
                       start=(hh == 0), stop=False)
            for slot, bk in ((0, bankA), (1, bankB)):
                m_v = mm(bk[:, 0:128], tbl["ones1f"][:],
                         tbl["bv2f"][:, 128 * slot:128 * (slot + 1)],
                         start=False, stop=True, inc=(slot == 1))
            stream_last_mm[kk] = m_v
            wait("scalar", "sMM", m_v)
            tg = 4 * tt + sub
            act(vT[:, 0 * S + 128 * tg: 0 * S + 128 * (tg + 1)],
                bankA[:, 0:128], AF.Identity, scale=1.0)
            v_evac[key] = act(vT[:, S + 128 * tg: S + 128 * (tg + 1)],
                              bankB[:, 0:128], AF.Identity, scale=1.0)
        if kk % 2 == 1:
            pop_select(1)

    sb_free(cm_xnstr)
    sb_free(cm_wv2)
    sb_free(cm_w6)

    b4j_s, cm_b4j = sb("b4j_s", [128, S], BF16)
    b4i_s, cm_b4i = sb("b4i_s", [128, S], BF16)
    masks, cm_masks = sb("masks", [128, 2048], BF16)
    wait("sync", "sMM", P.cur("sMM"))
    wait("sync", "sAC", P.cur("sAC"))
    dma("sync", "sLtab", b4j_s[:], ap["b4j"][:, 0:S])
    dma("sync", "sLtab", b4i_s[:], ap["b4i"][:, 0:S])
    ld_b4 = dma("sync", "sLtab", masks[:], ap["maskneg"])
    wait("tensor", "sLtab", ld_b4)
    wait("vector", "sLtab", ld_b4)
    wait("tensor", "sAC", P.cur("sAC"))

    exp_in_bank = {0: 0, 1: 0}
    pt_mark = {0: 0, 1: 0}
    pvdb_guard = {}

    for g in range(32):
        slot = g // 16
        s16 = g % 16
        so = slot * S
        i0 = 512 * s16
        nch = 4 * s16 + 4
        jc0 = max(0, (i0 - 2048) // 128) if slot == 0 else 0
        jc0 -= jc0 % 2
        ngrp_all = nch // 2
        pvb = pv2[g % 2]
        db = db2[g % 2]
        if g == 16:
            wait("gpsimd", "sVE", pvdb_guard[15][0])
            g_c1 = dma("gpsimd", "sLG", ag_at_in1.ap(), aoT[:, 0:S])
            wait("gpsimd", "sLG", g_c1)
            P.emit("gpsimd", lambda e: e.collective_compute(
                "AllGather", ALU.bypass, replica_groups=[list(range(NCORE))],
                ins=[ag_at_in1.ap().opt()],
                outs=[ag_at_out1.ap().opt()]).then_inc(SEM["sCAT"]))
            last_mm_slot0 = P.cur("sMM")
            wait("sync", "sMM", last_mm_slot0)
            dma("sync", "sLtab", b4j_s[0:4, :], ap["b4j"][0:4, S:2 * S])
            ld_sl = dma("sync", "sLtab", b4i_s[0:4, :], ap["b4i"][0:4, S:2 * S])
            wait("tensor", "sLtab", ld_sl)
        if (g - 2) in pvdb_guard:
            v_n2, a_ln2 = pvdb_guard[g - 2]
            wait("tensor", "sVE", v_n2)
            wait("tensor", "sAC", a_ln2)
        den_marks = {}
        exp_marks = {}

        def emit_pvden(grp):
            wait("tensor", "sAC", exp_marks[grp])
            for half in range(2):
                jc = 2 * grp + half
                pt = ptile[:, 1024 * (grp % 2) + 512 * half:
                           1024 * (grp % 2) + 512 * (half + 1)]
                mm(pvb[:], vT[:, so + 128 * jc: so + 128 * (jc + 1)], pt,
                   start=(jc == jc0), stop=(jc == nch - 1))
                den_marks[jc] = mm(db[:], tbl["onesj"][:], pt,
                                   start=(jc == jc0), stop=(jc == nch - 1), inc=True)
            pt_mark[grp % 2] = den_marks[2 * grp + 1]

        g0 = jc0 // 2
        for grp in range(g0, ngrp_all):
            sgb = psg[grp % 2]
            if exp_in_bank[grp % 2]:
                wait("tensor", "sAC", exp_in_bank[grp % 2])
            m_b = 0
            for half in range(2):
                jc = 2 * grp + half
                sl = sgb[:, 512 * half:512 * half + 512]
                mm(sl, kT[:, so + 128 * jc: so + 128 * (jc + 1)],
                   qT[:, so + i0: so + i0 + 512], start=True, stop=False)
                m_b = mm(sl, b4j_s[:, 128 * jc: 128 * (jc + 1)],
                         b4i_s[:, i0: i0 + 512],
                         start=False, stop=True, inc=True)
            if grp >= ngrp_all - 2:
                dd = grp - (ngrp_all - 2)
                wait("vector", "sMM", m_b)
                v_m = vop("tensor_tensor", sgb[:], sgb[:],
                          masks[:, 1024 * dd:1024 * (dd + 1)], ALU.add)
                wait("scalar", "sVE", v_m)
            wait("scalar", "sMM", m_b)
            if pt_mark[grp % 2]:
                wait("scalar", "sMM", pt_mark[grp % 2])
            e_m = act(ptile[:, 1024 * (grp % 2):1024 * (grp % 2 + 1)], sgb[:],
                      AF.Exp)
            exp_in_bank[grp % 2] = e_m
            exp_marks[grp] = e_m
            if grp - ATT_LAG >= g0:
                emit_pvden(grp - ATT_LAG)
        for gtail in range(max(g0, ngrp_all - ATT_LAG), ngrp_all):
            emit_pvden(gtail)
        pop_select(1)
        m_last_den = den_marks[nch - 1]
        wait("scalar", "sMM", m_last_den)
        act(lntmp[:], db[:], AF.Ln)
        a_ln = act(rcpr[:], lntmp[:], AF.Exp, scale=-1.0)
        wait("vector", "sAC", a_ln)
        v_n = vop("tensor_tensor", aoT[:, so + i0: so + i0 + 512],
                  pvb[:], rcpr[:], ALU.mult)
        pvdb_guard[g] = (v_n, a_ln)
    pop_select(len(select_units))

    att_done = P.cur("sVE")
    if taps:
        wait("gpsimd", "sVE", att_done)
        dma("gpsimd", "sLG", tap_ap["t_qT"], qT[:])
        dma("gpsimd", "sLG", tap_ap["t_kT"], kT[:])
        dma("gpsimd", "sLG", tap_ap["t_vT"], vT[:])
        dma("gpsimd", "sLG", tap_ap["t_aoT"], aoT[:])
    wait("gpsimd", "sVE", att_done)
    g_ctr2 = dma("gpsimd", "sLG", ag_at_in2.ap(), aoT[:, S:2 * S])

    for cm in (cm_masks, cm_b4i, cm_b4j, cm_rcpr, cm_lntmp, cm_selt, cm_selw,
               cm_idx_i2, cm_idx_i, cm_ptile, cm_aoT, cm_vT, cm_kT,
               cm_qT):
        sb_free(cm)

    h_T, cm_hT = sb("h_T", [128, NIT * TOK], BF16)
    ao_own, cm_ao_own = sb("ao_own", [128, NH * TOK], BF16)
    wgu_cms = []
    wgu = []
    for wn in ("wgu0", "wgu1"):
        t, cmw = sb(wn, [128, 2 * H], BF16)
        wgu.append(t)
        wgu_cms.append(cmw)
    sg, cm_sg = sb("sg", [128, 512], BF16)
    xnown2, cm_xnown2 = sb("xnown2", [128, NH * TOK], BF16)

    for s_name in ("sMM", "sVE", "sAC", "sLcon", "sLsel", "sLwl", "sLtab"):
        wait("sync", s_name, P.cur(s_name))
    wait("sync", "sLG", g_ctr2)
    if taps:
        for e2 in ("sync", "vector", "scalar"):
            wait(e2, "sLG", P.cur("sLG"))
    dma("sync", "sLtab",
        xnown2[:].rearrange("p (h t) -> p h t", h=NH)[:, :, 0:512],
        ag_xn_in.ap().rearrange("(h p) t -> p h t", p=128))
    ld_xn2 = dma("sync", "sLtab",
                 xnown2[:].rearrange("p (h t) -> p h t", h=NH)[:, :, 512:1024],
                 ag_xn_in2.ap().rearrange("(h p) t -> p h t", p=128))
    wait("tensor", "sCGU", 1)
    wait("sync", "sCGU", 1)
    wait("tensor", "sLtab", ld_xn2)
    wait("tensor", "sAC", P.cur("sAC"))
    wait("tensor", "sVE", P.cur("sVE"))

    gu_last_mm = {}
    silu_marks = {}
    htt_marks = {}
    for ti in range(NIT):
        buf = wgu[ti % 2]
        rank = ti % NCORE
        loc = ti // NCORE
        gs = "sLw0" if ti % 2 == 0 else "sLw1"
        if ti >= 2:
            wait("sync", "sMM", gu_last_mm[ti - 2])
        dma("sync", gs, buf[:, 0:H].rearrange("p (h w) -> p h w", h=NH),
            ag_gu_out.ap()[rank * H:(rank + 1) * H, 128 * loc:128 * (loc + 1)]
            .rearrange("(h p) w -> p h w", p=128))
        ldv = dma("sync", gs, buf[:, H:2 * H].rearrange("p (h w) -> p h w", h=NH),
                  ag_gu_out.ap()[rank * H:(rank + 1) * H, 768 + 128 * loc:768 + 128 * (loc + 1)]
                  .rearrange("(h p) w -> p h w", p=128))
        if ti == 4:
            wait("gpsimd", gs, ldv)
            P.emit("gpsimd", lambda e: e.collective_compute(
                "AllGather", ALU.bypass, replica_groups=[list(range(NCORE))],
                ins=[ag_at_in2.ap().opt()],
                outs=[ag_at_out2.ap().opt()]).then_inc(SEM["sCAT"]))
            emit_wd_cc()
        wait("tensor", gs, ldv)
        for t2 in range(2):
            if (ti - 1, t2) in silu_marks:
                wait("tensor", "sAC", silu_marks[(ti - 1, t2)])
            if (ti - 1, t2) in htt_marks:
                wait("tensor", "sVE", htt_marks[(ti - 1, t2)])
        gm = {}
        for hh in range(NH):
            for t2 in range(2):
                gm[t2] = mm(psg[0][:, 512 * t2:512 * (t2 + 1)],
                            buf[:, 128 * hh:128 * (hh + 1)],
                            xnown2[:, hh * TOK + 512 * t2: hh * TOK + 512 * (t2 + 1)],
                            start=(hh == 0), stop=(hh == NH - 1),
                            inc=(hh == NH - 1))
        for hh in range(NH):
            for t2 in range(2):
                gu_last_mm[ti] = mm(
                    psg[1][:, 512 * t2:512 * (t2 + 1)],
                    buf[:, H + 128 * hh:H + 128 * (hh + 1)],
                    xnown2[:, hh * TOK + 512 * t2: hh * TOK + 512 * (t2 + 1)],
                    start=(hh == 0), stop=(hh == NH - 1), inc=(hh == NH - 1))
        m_u = gu_last_mm[ti]
        for t2 in range(2):
            gbank = psg[0][:, 512 * t2:512 * (t2 + 1)]
            ubank = psg[1][:, 512 * t2:512 * (t2 + 1)]
            wait("scalar", "sMM", gm[t2])
            if (ti, t2 - 1) in htt_marks:
                wait("scalar", "sVE", htt_marks[(ti, t2 - 1)])
            elif t2 == 0 and (ti - 1, 1) in htt_marks:
                wait("scalar", "sVE", htt_marks[(ti - 1, 1)])
            a_s = act(sg[:], gbank, AF.Silu)
            silu_marks[(ti, t2)] = a_s
            wait("vector", "sMM", m_u)
            wait("vector", "sAC", a_s)
            htt_marks[(ti, t2)] = vop(
                "tensor_tensor", h_T[:, ti * TOK + 512 * t2: ti * TOK + 512 * (t2 + 1)],
                sg[:], ubank, ALU.mult)
    if taps:
        wait("gpsimd", "sVE", P.cur("sVE"))
        dma("gpsimd", "sLG", tap_ap["t_hT"], h_T[:])

    wait("gpsimd", "sCAT", 2)

    def _ao_own_load(e):
        with e.register("rc") as rc, e.register("ro") as ro:
            e.reg_load(rc, ap["cid"][0:1, 0:1])
            e.reg_mul(ro, rc, TOK)
            for d in range(NH):
                src = (ag_at_out1 if d % 2 == 0 else ag_at_out2).ap()
                r = d // 2
                e.dma_start(
                    out=ao_own[:, d * TOK:(d + 1) * TOK],
                    in_=src[128 * r:128 * (r + 1),
                            bass.ds(bass.RuntimeValue(ro), TOK)],
                ).then_inc(SEM["sLG"], 16)
    P.bump("sLG", 16 * NH)
    P.emit("gpsimd", _ao_own_load)
    g_ao = P.cur("sLG")
    if taps:
        wait("gpsimd", "sLG", g_ao)
        dma("gpsimd", "sLG", tap_ap["t_ao_own"], ao_own[:])

    sb_free(cm_xnown2)
    sb_free(cm_sg)
    sb_free(wgu_cms[1])
    sb_free(wgu_cms[0])

    wlhs0, _ = sb("wlhs0", [128, (16 + 48) * 128], BF16)
    wlhs1, _ = sb("wlhs1", [128, (16 + 48) * 128], BF16)
    ostage0, _ = sb("ostage0", [128, 512], F32)
    ostage1, _ = sb("ostage1", [128, 512], F32)
    outasm, _ = sb("outasm", [128, 4 * H], F32)
    xres, _ = sb("xres", [128, H], F32)
    wst0, _ = sb("wst0", [128, H], F32)

    wait("sync", "sCWD", 1)
    wait("tensor", "sCWD", 1)
    wait("tensor", "sLG", g_ao)
    barrier_waits("tensor")
    barrier_waits("sync")
    barrier_waits("vector")
    barrier_waits("scalar")

    evac_marks = {}
    tr_marks = {}
    cp_marks = {}
    wo_cast = {}
    wl_last = {0: 0, 1: 0}
    store_marks = []
    hp_seq = [(a2, hp) for a2 in range(2) for hp in range(16)]
    for idx, (a2, hp) in enumerate(hp_seq):
        wl = (wlhs0, wlhs1)[idx % 2]
        osg = (ostage0, ostage1)[idx % 2]
        mainbank = psg[idx % 2][:, 0:512]
        if a2 == 1 and hp == 0 and store_marks:
            wait("vector", "sLS", store_marks[-1])
        wls = "sLw0" if idx % 2 == 0 else "sLw1"
        if wl_last[idx % 2]:
            wait("sync", "sMM", wl_last[idx % 2])
        if idx >= 1:
            wait("sync", "sVE", wo_cast[idx - 1])
        ld_wo = dma("sync", wls, wst0[:].rearrange("p (d w) -> p d w", d=NH),
                    ap["wo"][:, 128 * hp:128 * (hp + 1)].rearrange("(d p) w -> p d w", p=128))
        wd_v = ag_wd_out.ap().rearrange("(e q) w -> q e w", e=NCORE)
        for l6 in range(6):
            ldv = dma("sync", wls,
                      wl[:, H + 1024 * l6:H + 1024 * (l6 + 1)]
                      .rearrange("p (ee w) -> p ee w", ee=8),
                      wd_v[128 * l6:128 * (l6 + 1), :, 128 * hp:128 * (hp + 1)])
        wait("vector", wls, ldv)
        if wl_last[idx % 2]:
            wait("vector", "sMM", wl_last[idx % 2])
        wo_cast[idx] = vop("tensor_copy", wl[:, 0:H], wst0[:])
        wait("tensor", wls, ldv)
        wait("tensor", "sVE", wo_cast[idx])
        if (idx - 2) in evac_marks:
            wait("tensor", "sAC", evac_marks[idx - 2])
        for d in range(NH):
            mm(mainbank, wl[:, 128 * d:128 * (d + 1)],
               ao_own[:, d * TOK + 512 * a2: d * TOK + 512 * (a2 + 1)],
               start=(d == 0), stop=False)
        for ti in range(NIT):
            mm(mainbank, wl[:, H + 128 * ti:H + 128 * (ti + 1)],
               h_T[:, ti * TOK + 512 * a2: ti * TOK + 512 * (a2 + 1)],
               start=False, stop=(ti == NIT - 1), inc=(ti == NIT - 1))
        m_e = P.cur("sMM")
        wl_last[idx % 2] = m_e
        wait("scalar", "sMM", m_e)
        if (idx - 2) in tr_marks:
            wait("scalar", "sMM", tr_marks[idx - 2])
        a_e = act(osg[:], mainbank, AF.Identity,
                  bias=tbl["bo_s"][:, hp:hp + 1], scale=1.0)
        evac_marks[idx] = a_e
        wait("tensor", "sAC", a_e)
        for t4 in range(4):
            tbank = pv2[t4 % 2]
            key = (idx, t4)
            prevkey = (idx, t4 - 2) if t4 >= 2 else (idx - 1, t4 + 2)
            if prevkey in cp_marks:
                wait("tensor", "sVE", cp_marks[prevkey])
            m_t = trans(tbank[:, 0:128], osg[:, 128 * t4:128 * (t4 + 1)],
                        tbl["identf_s"][:], inc=True)
            if t4 == 3:
                tr_marks[idx] = m_t
            wait("vector", "sMM", m_t)
            cp_marks[key] = vop(
                "tensor_copy", outasm[:, t4 * H + 128 * hp: t4 * H + 128 * (hp + 1)],
                tbank[:, 0:128])
        if hp == 15:
            v_add_prev = 0
            for t4 in range(4):
                if v_add_prev:
                    wait("sync", "sVE", v_add_prev)
                ldx = dma("sync", "sLxr", xres[:],
                          ap["x_own"][512 * a2 + 128 * t4: 512 * a2 + 128 * (t4 + 1), :])
                wait("vector", "sLxr", ldx)
                v_add_prev = vop("tensor_tensor", outasm[:, t4 * H:(t4 + 1) * H],
                                 outasm[:, t4 * H:(t4 + 1) * H], xres[:], ALU.add)
                wait("scalar", "sVE", v_add_prev)
                st = dma("scalar", "sLS",
                         out_ap[512 * a2 + 128 * t4: 512 * a2 + 128 * (t4 + 1), :],
                         outasm[:, t4 * H:(t4 + 1) * H])
                store_marks.append(st)

    wait("scalar", "sLS", store_marks[-1])
    wait("gpsimd", "sLS", store_marks[-1])

    with nc.Block() as block:
        @block.sync
        def _(e):
            for f in P.streams["sync"]:
                f(e)

        @block.tensor
        def _(e):
            for f in P.streams["tensor"]:
                f(e)

        @block.vector
        def _(e):
            for f in P.streams["vector"]:
                f(e)

        @block.scalar
        def _(e):
            for f in P.streams["scalar"]:
                f(e)

        @block.gpsimd
        def _(e):
            for f in P.streams["gpsimd"]:
                f(e)

    for cm in reversed(list(open_cms)):
        sb_free(cm)
    for cm in list(reversed(ps_cms)) + list(reversed(sem_cms)):
        cm.__exit__(None, None, None)
    return nc


def _host_inputs(x, Wqkv, bqkv, Wo, bo, lut, gidx, uidx, didx):
    x2d = np.ascontiguousarray(x.reshape(S, H))
    slopes = [2.0 ** (-(k) / 2.0) for k in range(NH)]
    jj = np.arange(128)[:, None]
    icq = np.arange(512)[None, :]
    maskneg = np.zeros((128, 2048), BF)
    for dd in range(2):
        for hl in range(2):
            d = 2 * dd + hl
            maskneg[:, 1024 * dd + 512 * hl:1024 * dd + 512 * (hl + 1)] = \
                np.where(128 * d + jj <= icq, 0.0, NEG).astype(BF)
    identf = np.eye(128, dtype=np.float32)
    pos = np.arange(S, dtype=np.float64)

    perm = np.empty(H, np.int64)
    for r in range(NCORE):
        for s2 in range(2):
            h = r + 8 * s2
            perm[256 * r + 128 * s2:256 * r + 128 * (s2 + 1)] = \
                np.arange(128 * h, 128 * (h + 1))
    wo_perm = np.ascontiguousarray(Wo[perm, :])

    in_maps = []
    for c in range(NCORE):
        heads = (c, 8 + c)
        cols = []
        for base in (0, 2048, 4096):
            for h in heads:
                cols.append((base + 128 * h, base + 128 * (h + 1)))
        wqkv_c = np.ascontiguousarray(
            np.concatenate([Wqkv[:, a:b] for a, b in cols], axis=1))
        bq4 = np.ascontiguousarray(
            np.stack([bqkv[a:b] for a, b in cols[:4]], axis=1))
        bv2 = np.concatenate([bqkv[a:b] for a, b in cols[4:]])[None, :]
        tiles = [c + 8 * k for k in range(6)]

        def _rows(idx_mat, t):
            if t < NIT:
                return idx_mat[128 * t:128 * (t + 1)]
            return np.zeros((128, idx_mat.shape[1]), np.int32)
        gidxT = np.ascontiguousarray(
            np.concatenate([_rows(gidx, t) for t in tiles]).T.astype(np.int32))
        uidxT = np.ascontiguousarray(
            np.concatenate([_rows(uidx, t) for t in tiles]).T.astype(np.int32))

        def _cols(idx_mat, t):
            if t < NIT:
                return idx_mat[:, 128 * t:128 * (t + 1)]
            return np.zeros((idx_mat.shape[0], 128), np.int32)
        didxT = np.ascontiguousarray(
            np.concatenate([_cols(didx, t) for t in tiles], axis=1).T.astype(np.int32))
        b4j = np.zeros((128, 2 * S), BF)
        b4i = np.zeros((128, 2 * S), BF)
        b4j[2, :] = 1.0
        b4j[3, :] = 1.0
        b4i[0, :] = 1.0
        b4i[1, :] = 1.0
        for sl, h in enumerate(heads):
            v = slopes[h] * pos
            hi = v.astype(np.float32).astype(BF)
            lo = (v.astype(np.float32) - hi.astype(np.float32)).astype(BF)
            b4j[0, sl * S:(sl + 1) * S] = hi
            b4j[1, sl * S:(sl + 1) * S] = lo
            b4i[2, sl * S:(sl + 1) * S] = -hi
            b4i[3, sl * S:(sl + 1) * S] = -lo
        in_maps.append({
            "cid": np.array([[c]], np.int32),
            "x_own": np.ascontiguousarray(x2d[TOK * c:TOK * (c + 1)]),
            "wqkv": wqkv_c,
            "bq4": bq4.astype(np.float32),
            "bv2": bv2.astype(np.float32),
            "wo": wo_perm,
            "bo_t": np.ascontiguousarray(bo.reshape(16, 128).T),
            "lut": lut.reshape(1, LUTN).astype(np.float32),
            "gidxT": gidxT, "uidxT": uidxT, "didxT": didxT,
            "b4j": b4j, "b4i": b4i,
            "maskneg": maskneg, "identf": identf,
        })
    return in_maps


def kernel(L0_attn_x, L0_attn_Wqkv, L0_attn_bqkv, L0_attn_Wo, L0_attn_bo,
           L0_ffn_lut, L0_ffn_gate_idx, L0_ffn_up_idx, L0_ffn_down_idx):
    in_maps = _host_inputs(
        np.asarray(L0_attn_x, np.float32), np.asarray(L0_attn_Wqkv, np.float32),
        np.asarray(L0_attn_bqkv, np.float32), np.asarray(L0_attn_Wo, np.float32),
        np.asarray(L0_attn_bo, np.float32), np.asarray(L0_ffn_lut, np.float32),
        np.asarray(L0_ffn_gate_idx, np.int32), np.asarray(L0_ffn_up_idx, np.int32),
        np.asarray(L0_ffn_down_idx, np.int32))
    nc = build_nc(taps=False)
    res = run_bass_kernel_spmd(nc, in_maps, core_ids=list(range(NCORE)))
    out = np.concatenate([np.asarray(res.results[c]["out"]) for c in range(NCORE)],
                         axis=0)
    return out.reshape(1, S, H).astype(np.float32)


# version 18 (final)
# speedup vs baseline: 1.0680x; 1.0173x over previous
import numpy as np
import ml_dtypes
import concourse.bass as bass
import concourse.mybir as mybir
from concourse.bass_utils import run_bass_kernel_spmd

F32 = mybir.dt.float32
BF16 = mybir.dt.bfloat16
I32 = mybir.dt.int32
AF = mybir.ActivationFunctionType
ALU = mybir.AluOpType
AX = mybir.AxisListType

NCORE = 8
S = 8192
TOK = 1024
H = 2048
NH = 16
DH = 128
NIT = 43
LUTN = 16
SCALE = float(1.0 / np.sqrt(128.0))
NEG = -1.0e30
BF = ml_dtypes.bfloat16
ATT_LAG = 1


class Plan:
    def __init__(self):
        self.streams = {"sync": [], "tensor": [], "vector": [], "scalar": [], "gpsimd": []}
        self.cnt = {}

    def emit(self, eng, fn):
        self.streams[eng].append(fn)

    def bump(self, s_name, n):
        self.cnt[s_name] = self.cnt.get(s_name, 0) + n
        return self.cnt[s_name]

    def cur(self, s_name):
        return self.cnt.get(s_name, 0)


def build_nc(taps=False):
    nc = bass.Bass("TRN2", target_bir_lowering=False, debug=False)

    ap = {}
    for name, shape, dt in [
        ("cid", [1, 1], I32),
        ("x_own", [TOK, H], F32),
        ("wqkv", [H, 768], F32),
        ("bq4", [128, 4], F32),
        ("bv2", [1, 256], F32),
        ("wo", [H, H], F32),
        ("bo_t", [128, 16], F32),
        ("lut", [1, LUTN], F32),
        ("gidxT", [H, 768], I32),
        ("uidxT", [H, 768], I32),
        ("didxT", [768, H], I32),
        ("b4j", [128, 2 * S], BF16),
        ("b4i", [128, 2 * S], BF16),
        ("maskneg", [128, 2048], BF16),
        ("identf", [128, 128], F32),
    ]:
        ap[name] = nc.dram_tensor(name, shape, dt, kind="ExternalInput").ap()
    out_ap = nc.dram_tensor("out", [TOK, H], F32, kind="ExternalOutput").ap()
    tap_ap = {}
    if taps:
        for name, shape, dt in [
            ("t_xnT", [128, NH * TOK], BF16),
            ("t_qT", [128, 2 * S], BF16),
            ("t_kT", [128, 2 * S], BF16),
            ("t_vT", [128, 2 * S], BF16),
            ("t_aoT", [128, 2 * S], BF16),
            ("t_hT", [128, NIT * TOK], BF16),
            ("t_ao_own", [128, NH * TOK], BF16),
        ]:
            tap_ap[name] = nc.dram_tensor(name, shape, dt, kind="ExternalOutput").ap()

    ag_xn_in = nc.dram_tensor("ag_xn_in", [H, 512], BF16)
    ag_xn_in2 = nc.dram_tensor("ag_xn_in2", [H, 512], BF16)
    ag_xn_out = nc.dram_tensor("ag_xn_out", [NCORE * H, 512], BF16, addr_space="Shared")
    ag_xn_out2 = nc.dram_tensor("ag_xn_out2", [NCORE * H, 512], BF16, addr_space="Shared")
    ag_gu_in = nc.dram_tensor("ag_gu_in", [H, 2 * 768], BF16)
    ag_gu_out = nc.dram_tensor("ag_gu_out", [NCORE * H, 2 * 768], BF16, addr_space="Shared")
    ag_wd_in = nc.dram_tensor("ag_wd_in", [768, H], BF16)
    ag_wd_out = nc.dram_tensor("ag_wd_out", [NCORE * 768, H], BF16, addr_space="Shared")
    ag_at_in1 = nc.dram_tensor("ag_at_in1", [128, S], BF16)
    ag_at_out1 = nc.dram_tensor("ag_at_out1", [NCORE * 128, S], BF16, addr_space="Shared")
    ag_at_in2 = nc.dram_tensor("ag_at_in2", [128, S], BF16)
    ag_at_out2 = nc.dram_tensor("ag_at_out2", [NCORE * 128, S], BF16, addr_space="Shared")

    P = Plan()
    SEM = {}
    open_cms = []

    def sb(name, shape, dt):
        cm = nc.sbuf_tensor(name, shape, dt)
        t = cm.__enter__()
        open_cms.append(cm)
        return t, cm

    def sb_free(cm):
        cm.__exit__(None, None, None)
        open_cms.remove(cm)

    def dma(eng, s_name, out, in_, **kw):
        v = P.bump(s_name, 16)
        s_h = SEM[s_name]
        P.emit(eng, lambda e, o=out, i=in_, s_h=s_h, kw=kw:
               e.dma_start(out=o, in_=i, **kw).then_inc(s_h, 16))
        return v

    def wait(eng, s_name, val):
        if val <= 0:
            return
        s_h = SEM[s_name]
        P.emit(eng, lambda e, s_h=s_h, v=val: e.wait_ge(s_h, v))

    def mm(out, lhsT, rhs, start, stop, inc=False):
        if inc:
            v = P.bump("sMM", 1)
            s_h = SEM["sMM"]
            P.emit("tensor", lambda e, o=out, l=lhsT, r=rhs, a=start, b=stop, s_h=s_h:
                   e.matmul(o, l, r, start=a, stop=b).then_inc(s_h, 1))
            return v
        P.emit("tensor", lambda e, o=out, l=lhsT, r=rhs, a=start, b=stop:
               e.matmul(o, l, r, start=a, stop=b))

    def trans(out, in_, ident, inc=False):
        if inc:
            v = P.bump("sMM", 1)
            s_h = SEM["sMM"]
            P.emit("tensor", lambda e, o=out, i=in_, d=ident, s_h=s_h:
                   e.transpose(o, i, d).then_inc(s_h, 1))
            return v
        P.emit("tensor", lambda e, o=out, i=in_, d=ident: e.transpose(o, i, d))

    def act(out, in_, func, bias=None, scale=None, accum_out=None):
        kw = {}
        if bias is not None:
            kw["bias"] = bias
        if scale is not None:
            kw["scale"] = scale
        if accum_out is not None:
            kw["accum_out"] = accum_out
        prev = P.cur("sAC")
        v = P.bump("sAC", 1)
        s_h = SEM["sAC"]
        P.emit("scalar", lambda e, o=out, i=in_, f=func, kw=kw, s_h=s_h, pv=prev:
               (e.wait_ge(s_h, pv) if pv else None,
                e.activation(o, i, f, **kw).then_inc(s_h, 1)))
        return v

    def vop(name, *args, **kw):
        prev = P.cur("sVE")
        v = P.bump("sVE", 1)
        s_h = SEM["sVE"]
        P.emit("vector", lambda e, n=name, a=args, kw=kw, s_h=s_h, pv=prev:
               (e.wait_ge(s_h, pv) if pv else None,
                getattr(e, n)(*a, **kw).then_inc(s_h, 1)))
        return v

    DMA_SEMS = ("sLtab", "sLln", "sLsel", "sLcon", "sLwo", "sLx0", "sLx1",
                "sLw0", "sLw1", "sLwl", "sLxr", "sLG", "sLS")

    def barrier_waits(eng):
        for s_name in ("sMM", "sAC", "sVE") + DMA_SEMS:
            wait(eng, s_name, P.cur(s_name))

    sem_cms = []
    for s_name in DMA_SEMS + ("sMM", "sAC", "sVE", "sCXN", "sCX2", "sCGU", "sCWD", "sCAT"):
        cm = nc.semaphore(s_name)
        SEM[s_name] = cm.__enter__()
        sem_cms.append(cm)

    ps_cms = []

    def psum(name, shape):
        cm = nc.psum_tensor(name, shape, F32)
        t = cm.__enter__()
        ps_cms.append(cm)
        return t
    psg = [psum("psg0", [128, 1024]), psum("psg1", [128, 1024])]
    pv2 = [psum("pv0", [128, 512]), psum("pv1", [128, 512])]
    db2 = [psum("db0", [128, 512]), psum("db1", [128, 512])]

    tbl = {}
    for name, shape, dt in [
        ("onesj", [128, 128], BF16),
        ("ones1f", [1, 128], F32),
        ("identf_s", [128, 128], F32),
        ("bo_s", [128, 16], F32), ("bq4_s", [128, 4], F32),
        ("bv2f", [1, 256], F32), ("lutrow", [1, LUTN], F32),
        ("lutrep", [128, LUTN], F32), ("eps", [128, 1], F32),
    ]:
        tbl[name], _ = sb(name, shape, dt)

    xnownT, cm_xnownT = sb("xnownT", [128, NH * TOK], BF16)

    ld0 = 0
    for dst, src in [
        (tbl["identf_s"][:], ap["identf"]), (tbl["bo_s"][:], ap["bo_t"]),
        (tbl["bq4_s"][:], ap["bq4"]), (tbl["bv2f"][:], ap["bv2"]),
        (tbl["lutrow"][:], ap["lut"]),
    ]:
        ld0 = dma("sync", "sLtab", dst, src)

    vop("memset", tbl["eps"][:], 1.0e-5)
    vop("memset", tbl["onesj"][:], 1.0)
    v0 = vop("memset", tbl["ones1f"][:], 1.0)
    wait("vector", "sLtab", ld0)
    vop("tensor_scalar", tbl["bq4_s"][:, 0:2], tbl["bq4_s"][:, 0:2],
        SCALE, None, ALU.mult)
    wait("tensor", "sLtab", ld0)
    wait("tensor", "sVE", v0)
    m_lut = mm(db2[0][:, 0:LUTN], tbl["ones1f"][:], tbl["lutrow"][:], True, True, inc=True)
    wait("vector", "sMM", m_lut)
    v_lutrep = vop("tensor_copy", tbl["lutrep"][:], db2[0][:, 0:LUTN])

    ln_x, cm_ln_x = sb("ln_x", [128, 8 * H], F32)
    x2s, cm_x2s = sb("x2s", [128, H], F32)
    ln_st, cm_ln_st = sb("ln_st", [128, 4 * 8], F32)
    xn_t, cm_xn_t = sb("xn_t", [128, 2 * H], F32)

    ld_a = dma("sync", "sLln", ln_x[:, 0:4 * H].rearrange("p (a w) -> p a w", a=4),
               ap["x_own"][0:512, :].rearrange("(a p) w -> p a w", p=128))
    ld_b = dma("sync", "sLx1", ln_x[:, 4 * H:8 * H].rearrange("p (a w) -> p a w", a=4),
               ap["x_own"][512:1024, :].rearrange("(a p) w -> p a w", p=128))
    ln_tr_last = {}
    ln_cp = {}
    xnv = xnownT[:].rearrange("p (h t) -> p h t", h=NH)
    for tau in range(8):
        if tau == 0:
            wait("scalar", "sLln", ld_a)
        elif tau == 4:
            wait("scalar", "sLx1", ld_b)
        xt = ln_x[:, H * tau:H * (tau + 1)]
        act(x2s[:], xt, AF.Identity, scale=1.0,
            accum_out=ln_st[:, 4 * tau:4 * tau + 1])
        a_st = act(x2s[:], xt, AF.Square,
                   accum_out=ln_st[:, 4 * tau + 1:4 * tau + 2])
        wait("vector", "sAC", a_st)
        vop("tensor_scalar", ln_st[:, 4 * tau:4 * tau + 1],
            ln_st[:, 4 * tau:4 * tau + 1], 1.0 / H, None, ALU.mult)
        vop("tensor_scalar", ln_st[:, 4 * tau + 1:4 * tau + 2],
            ln_st[:, 4 * tau + 1:4 * tau + 2], 1.0 / H, None, ALU.mult)
        vop("tensor_tensor", ln_st[:, 4 * tau + 2:4 * tau + 3],
            ln_st[:, 4 * tau:4 * tau + 1], ln_st[:, 4 * tau:4 * tau + 1], ALU.mult)
        v_var = vop("tensor_tensor", ln_st[:, 4 * tau + 2:4 * tau + 3],
                    ln_st[:, 4 * tau + 1:4 * tau + 2],
                    ln_st[:, 4 * tau + 2:4 * tau + 3], ALU.subtract)
        wait("scalar", "sVE", v_var)
        a_sd = act(ln_st[:, 4 * tau + 3:4 * tau + 4],
                   ln_st[:, 4 * tau + 2:4 * tau + 3],
                   AF.Sqrt, bias=tbl["eps"][:], scale=1.0)
        wait("vector", "sAC", a_sd)
        vop("reciprocal", ln_st[:, 4 * tau + 3:4 * tau + 4],
            ln_st[:, 4 * tau + 3:4 * tau + 4])
        xslot = xn_t[:, H * (tau % 2):H * (tau % 2) + H]
        if tau >= 2:
            wait("vector", "sMM", ln_tr_last[tau - 2])
        v_xn = vop("tensor_scalar", xslot, ln_x[:, H * tau:H * (tau + 1)],
                   ln_st[:, 4 * tau:4 * tau + 1],
                   ln_st[:, 4 * tau + 3:4 * tau + 4],
                   ALU.subtract, op1=ALU.mult)
        wait("tensor", "sVE", v_xn)
        for half in range(2):
            if tau >= 1:
                wait("tensor", "sVE", ln_cp[(tau - 1, half)])
            for h8 in range(8):
                hh = 8 * half + h8
                m_t = trans(psg[half][:, 128 * h8:128 * (h8 + 1)],
                            xslot[:, 128 * hh:128 * (hh + 1)],
                            tbl["identf_s"][:], inc=(h8 == 7))
            ln_tr_last[tau] = m_t
            wait("vector", "sMM", m_t)
            ln_cp[(tau, half)] = vop(
                "tensor_copy",
                xnv[:, 8 * half:8 * (half + 1), 128 * tau:128 * (tau + 1)],
                psg[half][:].rearrange("p (h w) -> p h w", h=8))
        if tau == 3:
            wait("gpsimd", "sVE", P.cur("sVE"))
            g_h1 = dma("gpsimd", "sLG",
                       ag_xn_in.ap().rearrange("(h p) t -> p h t", p=128),
                       xnownT[:].rearrange("p (h t) -> p h t", h=NH)[:, :, 0:512])
            wait("gpsimd", "sLG", g_h1)
            P.emit("gpsimd", lambda e: e.collective_compute(
                "AllGather", ALU.bypass, replica_groups=[list(range(NCORE))],
                ins=[ag_xn_in.ap().opt()],
                outs=[ag_xn_out.ap().opt()]).then_inc(SEM["sCXN"]))
    v_ln_done = P.cur("sVE")

    wait("gpsimd", "sVE", v_ln_done)
    g_xn = dma("gpsimd", "sLG",
               ag_xn_in2.ap().rearrange("(h p) t -> p h t", p=128),
               xnownT[:].rearrange("p (h t) -> p h t", h=NH)[:, :, 512:1024])
    wait("gpsimd", "sLG", g_xn)
    P.emit("gpsimd", lambda e: e.collective_compute(
        "AllGather", ALU.bypass, replica_groups=[list(range(NCORE))],
        ins=[ag_xn_in2.ap().opt()], outs=[ag_xn_out2.ap().opt()]).then_inc(SEM["sCX2"]))
    if taps:
        dma("gpsimd", "sLG", tap_ap["t_xnT"], xnownT[:])
    xn_contrib_mark = P.cur("sLG")

    for cm in (cm_xn_t, cm_ln_st, cm_x2s, cm_ln_x, cm_xnownT):
        sb_free(cm)

    qT, cm_qT = sb("qT", [128, 2 * S], BF16)
    kT, cm_kT = sb("kT", [128, 2 * S], BF16)
    vT, cm_vT = sb("vT", [128, 2 * S], BF16)
    aoT, cm_aoT = sb("aoT", [128, 2 * S], BF16)
    ptile, cm_ptile = sb("ptile", [128, 2 * 1024], BF16)
    idx_i, cm_idx_i = sb("idx_i", [128, 1024], I32)
    idx_i2, cm_idx_i2 = sb("idx_i2", [128, 1024], I32)
    selw, cm_selw = sb("selw", [128, 1024], BF16)
    selt, cm_selt = sb("selt", [128, 1024], BF16)
    lntmp, cm_lntmp = sb("lntmp", [128, 512], F32)
    rcpr, cm_rcpr = sb("rcpr", [128, 512], F32)
    w6, cm_w6 = sb("w6", [128, 4 * H], BF16)
    wv2, cm_wv2 = sb("wv2", [128, 2 * H], BF16)
    wcv0, cm_wcv0 = sb("wcv0", [128, H], F32)
    wcv1, cm_wcv1 = sb("wcv1", [128, H], F32)

    wcv = (wcv0, wcv1)
    cast_marks = {}
    for j in range(6):
        if j >= 2:
            wait("scalar", "sAC", cast_marks[j - 2])
        ldv = dma("scalar", "sLwo",
                  wcv[j % 2][:].rearrange("p (h w) -> p h w", h=NH),
                  ap["wqkv"][:, 128 * j:128 * (j + 1)].rearrange("(h p) w -> p h w", p=128))
        wait("scalar", "sLwo", ldv)
        dst = w6[:, H * j:H * (j + 1)] if j < 4 else wv2[:, H * (j - 4):H * (j - 3)]
        cast_marks[j] = act(dst, wcv[j % 2][:], AF.Identity, scale=1.0)
    a_w_ready = cast_marks[5]
    sb_free(cm_wcv1)
    sb_free(cm_wcv0)

    xnstr, cm_xnstr = sb("xnstr", [128, 2 * 8192], BF16)

    sel_state = {"n": 0, "cp": {}, "last_contrib": 0, "pend": None}

    def flush_contrib():
        if sel_state["pend"] is not None:
            pdst, pcols, pmark = sel_state["pend"]
            wait("gpsimd", "sVE", pmark)
            sel_state["last_contrib"] = dma("gpsimd", "sLcon", pdst,
                                            selw[:, 0:pcols])
            sel_state["pend"] = None

    def emit_select_unit(src_ap, cols, dst_ap):
        ku = sel_state["n"]
        sel_state["n"] = ku + 1
        ib = (idx_i, idx_i2)[ku % 2]
        isem = ("sLsel", "sLwl")[ku % 2]
        gate = sel_state["cp"].get(ku - 2, v_lutrep if ku == 0 else 0)
        if gate:
            wait("gpsimd", "sVE", gate)
        ldv = dma("gpsimd", isem, ib[:, 0:cols], src_ap)
        flush_contrib()
        wait("vector", isem, ldv)
        if sel_state["last_contrib"]:
            wait("vector", "sLcon", sel_state["last_contrib"])
        last = 0
        for k in range(LUTN):
            if k == 0:
                vop("tensor_scalar", selw[:, 0:cols], ib[:, 0:cols],
                    0.0, tbl["lutrep"][:, 0:1], ALU.is_equal, ALU.mult)
            else:
                vop("tensor_scalar", selt[:, 0:cols], ib[:, 0:cols],
                    float(k), tbl["lutrep"][:, k:k + 1], ALU.is_equal, ALU.mult)
                last = vop("tensor_tensor", selw[:, 0:cols], selw[:, 0:cols],
                           selt[:, 0:cols], ALU.add)
        sel_state["cp"][ku] = last
        sel_state["pend"] = (dst_ap, cols, last)

    select_units = []
    for mat_name, col_off in (("gidxT", 0), ("uidxT", 768)):
        for hp in range(16):
            select_units.append((
                ap[mat_name][128 * hp:128 * (hp + 1), :], 768,
                ag_gu_in.ap()[128 * hp:128 * (hp + 1), col_off:col_off + 768]))
    for it6 in range(6):
        for half in range(2):
            select_units.append((
                ap["didxT"][128 * it6:128 * (it6 + 1), 1024 * half:1024 * (half + 1)],
                1024,
                ag_wd_in.ap()[128 * it6:128 * (it6 + 1), 1024 * half:1024 * (half + 1)]))
    sel_idx = [0]
    cc_emitted = {"gu": False, "wd": False}

    def emit_gu_cc():
        wait("gpsimd", "sLcon", P.cur("sLcon"))
        P.emit("gpsimd", lambda e: e.collective_compute(
            "AllGather", ALU.bypass, replica_groups=[list(range(NCORE))],
            ins=[ag_gu_in.ap().opt()], outs=[ag_gu_out.ap().opt()]).then_inc(SEM["sCGU"]))

    def emit_wd_cc():
        wait("gpsimd", "sLcon", P.cur("sLcon"))
        P.emit("gpsimd", lambda e: e.collective_compute(
            "AllGather", ALU.bypass, replica_groups=[list(range(NCORE))],
            ins=[ag_wd_in.ap().opt()], outs=[ag_wd_out.ap().opt()]).then_inc(SEM["sCWD"]))
    def pop_select(n):
        for _ in range(n):
            if sel_idx[0] < len(select_units):
                emit_select_unit(*select_units[sel_idx[0]])
                sel_idx[0] += 1
                if sel_idx[0] == 32 and not cc_emitted["gu"]:
                    cc_emitted["gu"] = True
                    flush_contrib()
                    emit_gu_cc()
                if sel_idx[0] == len(select_units):
                    flush_contrib()

    pop_select(9)

    wait("tensor", "sCXN", 1)
    wait("sync", "sCXN", 1)
    wait("sync", "sLG", xn_contrib_mark)
    wait("sync", "sAC", a_w_ready)
    wait("sync", "sLG", P.cur("sLG"))
    wait("scalar", "sLG", P.cur("sLG"))
    wait("tensor", "sVE", v_ln_done)
    wait("tensor", "sAC", P.cur("sAC"))

    qk_evac = {}
    v_evac = {}
    stream_last_mm = {}
    tt_order = [2 * r for r in range(8)] + [2 * r + 1 for r in range(8)]
    for kk, tt in enumerate(tt_order):
        buf = kk % 2
        xc = buf * 8192
        rank = tt // 2
        src_t = ag_xn_out if tt % 2 == 0 else ag_xn_out2
        if kk == 8:
            wait("sync", "sCX2", 1)
            wait("tensor", "sCX2", 1)
        if kk >= 2:
            wait("sync", "sMM", stream_last_mm[kk - 2])
        ldv = dma("sync", "sLx0" if buf == 0 else "sLx1",
                  xnstr[:, xc:xc + 8192].rearrange("p (h t) -> p h t", h=NH),
                  src_t.ap()[rank * H:(rank + 1) * H, 0:512]
                  .rearrange("(h p) t -> p h t", p=128))
        wait("tensor", "sLx0" if buf == 0 else "sLx1", ldv)
        for j in range(4):
            bank = psg[j // 2][:, 512 * (j % 2):512 * (j % 2) + 512]
            if (j, kk - 1) in qk_evac:
                wait("tensor", "sAC", qk_evac[(j, kk - 1)])
            for hh in range(NH):
                mm(bank, w6[:, H * j + 128 * hh: H * j + 128 * (hh + 1)],
                   xnstr[:, xc + 512 * hh: xc + 512 * (hh + 1)],
                   start=(hh == 0), stop=(hh == NH - 1), inc=(hh == NH - 1))
            m_qk = P.cur("sMM")
            wait("scalar", "sMM", m_qk)
            dst = qT if j < 2 else kT
            slot = j % 2
            scl = SCALE if j < 2 else 1.0
            qk_evac[(j, kk)] = act(
                dst[:, slot * S + 512 * tt: slot * S + 512 * (tt + 1)],
                bank, AF.Identity, bias=tbl["bq4_s"][:, j:j + 1], scale=scl)
        for sub in range(4):
            bankA = pv2[sub % 2]
            bankB = db2[sub % 2]
            key = (sub % 2, 4 * kk + sub)
            prev = (sub % 2, 4 * kk + sub - 2)
            if prev in v_evac:
                wait("tensor", "sAC", v_evac[prev])
            for hh in range(NH):
                for slot, bk in ((0, bankA), (1, bankB)):
                    mm(bk[:, 0:128],
                       xnstr[:, xc + 512 * hh + 128 * sub: xc + 512 * hh + 128 * (sub + 1)],
                       wv2[:, H * slot + 128 * hh: H * slot + 128 * (hh + 1)],
                       start=(hh == 0), stop=False)
            for slot, bk in ((0, bankA), (1, bankB)):
                m_v = mm(bk[:, 0:128], tbl["ones1f"][:],
                         tbl["bv2f"][:, 128 * slot:128 * (slot + 1)],
                         start=False, stop=True, inc=(slot == 1))
            stream_last_mm[kk] = m_v
            wait("scalar", "sMM", m_v)
            tg = 4 * tt + sub
            act(vT[:, 0 * S + 128 * tg: 0 * S + 128 * (tg + 1)],
                bankA[:, 0:128], AF.Identity, scale=1.0)
            v_evac[key] = act(vT[:, S + 128 * tg: S + 128 * (tg + 1)],
                              bankB[:, 0:128], AF.Identity, scale=1.0)
        if kk % 2 == 1:
            pop_select(1)

    sb_free(cm_xnstr)
    sb_free(cm_wv2)
    sb_free(cm_w6)

    b4j_s, cm_b4j = sb("b4j_s", [128, S], BF16)
    b4i_s, cm_b4i = sb("b4i_s", [128, S], BF16)
    masks, cm_masks = sb("masks", [128, 2048], BF16)
    wait("sync", "sMM", P.cur("sMM"))
    wait("sync", "sAC", P.cur("sAC"))
    dma("sync", "sLtab", b4j_s[:], ap["b4j"][:, 0:S])
    dma("sync", "sLtab", b4i_s[:], ap["b4i"][:, 0:S])
    ld_b4 = dma("sync", "sLtab", masks[:], ap["maskneg"])
    wait("tensor", "sLtab", ld_b4)
    wait("vector", "sLtab", ld_b4)
    wait("tensor", "sAC", P.cur("sAC"))

    exp_in_bank = {0: 0, 1: 0}
    pt_mark = {0: 0, 1: 0}
    pvdb_guard = {}

    for g in range(32):
        slot = g // 16
        s16 = g % 16
        so = slot * S
        i0 = 512 * s16
        nch = 4 * s16 + 4
        jc0 = max(0, (i0 - 2048) // 128) if slot == 0 else 0
        jc0 -= jc0 % 2
        ngrp_all = nch // 2
        pvb = pv2[g % 2]
        db = db2[g % 2]
        if g == 16:
            wait("gpsimd", "sVE", pvdb_guard[15][0])
            g_c1 = dma("gpsimd", "sLG", ag_at_in1.ap(), aoT[:, 0:S])
            wait("gpsimd", "sLG", g_c1)
            P.emit("gpsimd", lambda e: e.collective_compute(
                "AllGather", ALU.bypass, replica_groups=[list(range(NCORE))],
                ins=[ag_at_in1.ap().opt()],
                outs=[ag_at_out1.ap().opt()]).then_inc(SEM["sCAT"]))
            last_mm_slot0 = P.cur("sMM")
            wait("sync", "sMM", last_mm_slot0)
            dma("sync", "sLtab", b4j_s[0:4, :], ap["b4j"][0:4, S:2 * S])
            ld_sl = dma("sync", "sLtab", b4i_s[0:4, :], ap["b4i"][0:4, S:2 * S])
            wait("tensor", "sLtab", ld_sl)
        if (g - 2) in pvdb_guard:
            v_n2, a_ln2 = pvdb_guard[g - 2]
            wait("tensor", "sVE", v_n2)
            wait("tensor", "sAC", a_ln2)
        den_marks = {}
        exp_marks = {}

        def emit_pvden(grp):
            wait("tensor", "sAC", exp_marks[grp])
            for half in range(2):
                jc = 2 * grp + half
                pt = ptile[:, 1024 * (grp % 2) + 512 * half:
                           1024 * (grp % 2) + 512 * (half + 1)]
                mm(pvb[:], vT[:, so + 128 * jc: so + 128 * (jc + 1)], pt,
                   start=(jc == jc0), stop=(jc == nch - 1))
                den_marks[jc] = mm(db[:], tbl["onesj"][:], pt,
                                   start=(jc == jc0), stop=(jc == nch - 1), inc=True)
            pt_mark[grp % 2] = den_marks[2 * grp + 1]

        g0 = jc0 // 2
        for grp in range(g0, ngrp_all):
            sgb = psg[grp % 2]
            if exp_in_bank[grp % 2]:
                wait("tensor", "sAC", exp_in_bank[grp % 2])
            m_b = 0
            for half in range(2):
                jc = 2 * grp + half
                sl = sgb[:, 512 * half:512 * half + 512]
                mm(sl, kT[:, so + 128 * jc: so + 128 * (jc + 1)],
                   qT[:, so + i0: so + i0 + 512], start=True, stop=False)
                m_b = mm(sl, b4j_s[:, 128 * jc: 128 * (jc + 1)],
                         b4i_s[:, i0: i0 + 512],
                         start=False, stop=True, inc=True)
            if grp >= ngrp_all - 2:
                dd = grp - (ngrp_all - 2)
                wait("vector", "sMM", m_b)
                v_m = vop("tensor_tensor", sgb[:], sgb[:],
                          masks[:, 1024 * dd:1024 * (dd + 1)], ALU.add)
                wait("scalar", "sVE", v_m)
            wait("scalar", "sMM", m_b)
            if pt_mark[grp % 2]:
                wait("scalar", "sMM", pt_mark[grp % 2])
            e_m = act(ptile[:, 1024 * (grp % 2):1024 * (grp % 2 + 1)], sgb[:],
                      AF.Exp)
            exp_in_bank[grp % 2] = e_m
            exp_marks[grp] = e_m
            if grp - ATT_LAG >= g0:
                emit_pvden(grp - ATT_LAG)
        for gtail in range(max(g0, ngrp_all - ATT_LAG), ngrp_all):
            emit_pvden(gtail)
        if not (slot == 1 and s16 < 3):
            pop_select(1)
        m_last_den = den_marks[nch - 1]
        wait("scalar", "sMM", m_last_den)
        act(lntmp[:], db[:], AF.Ln)
        a_ln = act(rcpr[:], lntmp[:], AF.Exp, scale=-1.0)
        wait("vector", "sAC", a_ln)
        v_n = vop("tensor_tensor", aoT[:, so + i0: so + i0 + 512],
                  pvb[:], rcpr[:], ALU.mult)
        pvdb_guard[g] = (v_n, a_ln)
    pop_select(len(select_units))

    att_done = P.cur("sVE")
    if taps:
        wait("gpsimd", "sVE", att_done)
        dma("gpsimd", "sLG", tap_ap["t_qT"], qT[:])
        dma("gpsimd", "sLG", tap_ap["t_kT"], kT[:])
        dma("gpsimd", "sLG", tap_ap["t_vT"], vT[:])
        dma("gpsimd", "sLG", tap_ap["t_aoT"], aoT[:])
    wait("gpsimd", "sVE", att_done)
    g_ctr2 = dma("gpsimd", "sLG", ag_at_in2.ap(), aoT[:, S:2 * S])

    for cm in (cm_masks, cm_b4i, cm_b4j, cm_rcpr, cm_lntmp, cm_selt, cm_selw,
               cm_idx_i2, cm_idx_i, cm_ptile, cm_aoT, cm_vT, cm_kT,
               cm_qT):
        sb_free(cm)

    h_T, cm_hT = sb("h_T", [128, NIT * TOK], BF16)
    ao_own, cm_ao_own = sb("ao_own", [128, NH * TOK], BF16)
    wgu_cms = []
    wgu = []
    for wn in ("wgu0", "wgu1"):
        t, cmw = sb(wn, [128, 2 * H], BF16)
        wgu.append(t)
        wgu_cms.append(cmw)
    sg, cm_sg = sb("sg", [128, 512], BF16)
    xnown2, cm_xnown2 = sb("xnown2", [128, NH * TOK], BF16)

    for s_name in ("sMM", "sVE", "sAC", "sLcon", "sLsel", "sLwl", "sLtab"):
        wait("sync", s_name, P.cur(s_name))
    wait("sync", "sLG", g_ctr2)
    if taps:
        for e2 in ("sync", "vector", "scalar"):
            wait(e2, "sLG", P.cur("sLG"))
    dma("sync", "sLtab",
        xnown2[:].rearrange("p (h t) -> p h t", h=NH)[:, :, 0:512],
        ag_xn_in.ap().rearrange("(h p) t -> p h t", p=128))
    ld_xn2 = dma("sync", "sLtab",
                 xnown2[:].rearrange("p (h t) -> p h t", h=NH)[:, :, 512:1024],
                 ag_xn_in2.ap().rearrange("(h p) t -> p h t", p=128))
    wait("tensor", "sCGU", 1)
    wait("sync", "sCGU", 1)
    wait("tensor", "sLtab", ld_xn2)
    wait("tensor", "sAC", P.cur("sAC"))
    wait("tensor", "sVE", P.cur("sVE"))

    gu_last_mm = {}
    silu_marks = {}
    htt_marks = {}
    for ti in range(NIT):
        buf = wgu[ti % 2]
        rank = ti % NCORE
        loc = ti // NCORE
        gs = "sLw0" if ti % 2 == 0 else "sLw1"
        if ti >= 2:
            wait("sync", "sMM", gu_last_mm[ti - 2])
        dma("sync", gs, buf[:, 0:H].rearrange("p (h w) -> p h w", h=NH),
            ag_gu_out.ap()[rank * H:(rank + 1) * H, 128 * loc:128 * (loc + 1)]
            .rearrange("(h p) w -> p h w", p=128))
        ldv = dma("sync", gs, buf[:, H:2 * H].rearrange("p (h w) -> p h w", h=NH),
                  ag_gu_out.ap()[rank * H:(rank + 1) * H, 768 + 128 * loc:768 + 128 * (loc + 1)]
                  .rearrange("(h p) w -> p h w", p=128))
        if ti == 4:
            wait("gpsimd", gs, ldv)
            P.emit("gpsimd", lambda e: e.collective_compute(
                "AllGather", ALU.bypass, replica_groups=[list(range(NCORE))],
                ins=[ag_at_in2.ap().opt()],
                outs=[ag_at_out2.ap().opt()]).then_inc(SEM["sCAT"]))
        elif ti == 16:
            wait("gpsimd", gs, ldv)
            emit_wd_cc()
        wait("tensor", gs, ldv)
        for t2 in range(2):
            if (ti - 1, t2) in silu_marks:
                wait("tensor", "sAC", silu_marks[(ti - 1, t2)])
            if (ti - 1, t2) in htt_marks:
                wait("tensor", "sVE", htt_marks[(ti - 1, t2)])
        gm = {}
        for hh in range(NH):
            for t2 in range(2):
                gm[t2] = mm(psg[0][:, 512 * t2:512 * (t2 + 1)],
                            buf[:, 128 * hh:128 * (hh + 1)],
                            xnown2[:, hh * TOK + 512 * t2: hh * TOK + 512 * (t2 + 1)],
                            start=(hh == 0), stop=(hh == NH - 1),
                            inc=(hh == NH - 1))
        for hh in range(NH):
            for t2 in range(2):
                gu_last_mm[ti] = mm(
                    psg[1][:, 512 * t2:512 * (t2 + 1)],
                    buf[:, H + 128 * hh:H + 128 * (hh + 1)],
                    xnown2[:, hh * TOK + 512 * t2: hh * TOK + 512 * (t2 + 1)],
                    start=(hh == 0), stop=(hh == NH - 1), inc=(hh == NH - 1))
        m_u = gu_last_mm[ti]
        for t2 in range(2):
            gbank = psg[0][:, 512 * t2:512 * (t2 + 1)]
            ubank = psg[1][:, 512 * t2:512 * (t2 + 1)]
            wait("scalar", "sMM", gm[t2])
            if (ti, t2 - 1) in htt_marks:
                wait("scalar", "sVE", htt_marks[(ti, t2 - 1)])
            elif t2 == 0 and (ti - 1, 1) in htt_marks:
                wait("scalar", "sVE", htt_marks[(ti - 1, 1)])
            a_s = act(sg[:], gbank, AF.Silu)
            silu_marks[(ti, t2)] = a_s
            wait("vector", "sMM", m_u)
            wait("vector", "sAC", a_s)
            htt_marks[(ti, t2)] = vop(
                "tensor_tensor", h_T[:, ti * TOK + 512 * t2: ti * TOK + 512 * (t2 + 1)],
                sg[:], ubank, ALU.mult)
    if taps:
        wait("gpsimd", "sVE", P.cur("sVE"))
        dma("gpsimd", "sLG", tap_ap["t_hT"], h_T[:])

    wait("gpsimd", "sCAT", 2)

    def _ao_own_load(e):
        with e.register("rc") as rc, e.register("ro") as ro:
            e.reg_load(rc, ap["cid"][0:1, 0:1])
            e.reg_mul(ro, rc, TOK)
            for d in range(NH):
                src = (ag_at_out1 if d % 2 == 0 else ag_at_out2).ap()
                r = d // 2
                e.dma_start(
                    out=ao_own[:, d * TOK:(d + 1) * TOK],
                    in_=src[128 * r:128 * (r + 1),
                            bass.ds(bass.RuntimeValue(ro), TOK)],
                ).then_inc(SEM["sLG"], 16)
    P.bump("sLG", 16 * NH)
    P.emit("gpsimd", _ao_own_load)
    g_ao = P.cur("sLG")
    if taps:
        wait("gpsimd", "sLG", g_ao)
        dma("gpsimd", "sLG", tap_ap["t_ao_own"], ao_own[:])

    sb_free(cm_xnown2)
    sb_free(cm_sg)
    sb_free(wgu_cms[1])
    sb_free(wgu_cms[0])

    wlhs0, _ = sb("wlhs0", [128, (16 + 48) * 128], BF16)
    wlhs1, _ = sb("wlhs1", [128, (16 + 48) * 128], BF16)
    ostage0, _ = sb("ostage0", [128, 512], F32)
    ostage1, _ = sb("ostage1", [128, 512], F32)
    outasm, _ = sb("outasm", [128, 4 * H], F32)
    xres, _ = sb("xres", [128, H], F32)
    wst0, _ = sb("wst0", [128, H], F32)

    wait("sync", "sCWD", 1)
    wait("tensor", "sCWD", 1)
    wait("tensor", "sLG", g_ao)
    barrier_waits("tensor")
    barrier_waits("sync")
    barrier_waits("vector")
    barrier_waits("scalar")

    evac_marks = {}
    tr_marks = {}
    cp_marks = {}
    wo_cast = {}
    wl_last = {0: 0, 1: 0}
    store_marks = []
    hp_seq = [(a2, hp) for a2 in range(2) for hp in range(16)]
    for idx, (a2, hp) in enumerate(hp_seq):
        wl = (wlhs0, wlhs1)[idx % 2]
        osg = (ostage0, ostage1)[idx % 2]
        mainbank = psg[idx % 2][:, 0:512]
        if a2 == 1 and hp == 0 and store_marks:
            wait("vector", "sLS", store_marks[-1])
        wls = "sLw0" if idx % 2 == 0 else "sLw1"
        if wl_last[idx % 2]:
            wait("sync", "sMM", wl_last[idx % 2])
        if idx >= 1:
            wait("sync", "sVE", wo_cast[idx - 1])
        ld_wo = dma("sync", wls, wst0[:].rearrange("p (d w) -> p d w", d=NH),
                    ap["wo"][:, 128 * hp:128 * (hp + 1)].rearrange("(d p) w -> p d w", p=128))
        wd_v = ag_wd_out.ap().rearrange("(e q) w -> q e w", e=NCORE)
        for l6 in range(6):
            ldv = dma("sync", wls,
                      wl[:, H + 1024 * l6:H + 1024 * (l6 + 1)]
                      .rearrange("p (ee w) -> p ee w", ee=8),
                      wd_v[128 * l6:128 * (l6 + 1), :, 128 * hp:128 * (hp + 1)])
        wait("vector", wls, ldv)
        if wl_last[idx % 2]:
            wait("vector", "sMM", wl_last[idx % 2])
        wo_cast[idx] = vop("tensor_copy", wl[:, 0:H], wst0[:])
        wait("tensor", wls, ldv)
        wait("tensor", "sVE", wo_cast[idx])
        if (idx - 2) in evac_marks:
            wait("tensor", "sAC", evac_marks[idx - 2])
        for d in range(NH):
            mm(mainbank, wl[:, 128 * d:128 * (d + 1)],
               ao_own[:, d * TOK + 512 * a2: d * TOK + 512 * (a2 + 1)],
               start=(d == 0), stop=False)
        for ti in range(NIT):
            mm(mainbank, wl[:, H + 128 * ti:H + 128 * (ti + 1)],
               h_T[:, ti * TOK + 512 * a2: ti * TOK + 512 * (a2 + 1)],
               start=False, stop=(ti == NIT - 1), inc=(ti == NIT - 1))
        m_e = P.cur("sMM")
        wl_last[idx % 2] = m_e
        wait("scalar", "sMM", m_e)
        if (idx - 2) in tr_marks:
            wait("scalar", "sMM", tr_marks[idx - 2])
        a_e = act(osg[:], mainbank, AF.Identity,
                  bias=tbl["bo_s"][:, hp:hp + 1], scale=1.0)
        evac_marks[idx] = a_e
        wait("tensor", "sAC", a_e)
        for t4 in range(4):
            tbank = pv2[t4 % 2]
            key = (idx, t4)
            prevkey = (idx, t4 - 2) if t4 >= 2 else (idx - 1, t4 + 2)
            if prevkey in cp_marks:
                wait("tensor", "sVE", cp_marks[prevkey])
            m_t = trans(tbank[:, 0:128], osg[:, 128 * t4:128 * (t4 + 1)],
                        tbl["identf_s"][:], inc=True)
            if t4 == 3:
                tr_marks[idx] = m_t
            wait("vector", "sMM", m_t)
            cp_marks[key] = vop(
                "tensor_copy", outasm[:, t4 * H + 128 * hp: t4 * H + 128 * (hp + 1)],
                tbank[:, 0:128])
        if hp == 15:
            v_add_prev = 0
            for t4 in range(4):
                if v_add_prev:
                    wait("sync", "sVE", v_add_prev)
                ldx = dma("sync", "sLxr", xres[:],
                          ap["x_own"][512 * a2 + 128 * t4: 512 * a2 + 128 * (t4 + 1), :])
                wait("vector", "sLxr", ldx)
                v_add_prev = vop("tensor_tensor", outasm[:, t4 * H:(t4 + 1) * H],
                                 outasm[:, t4 * H:(t4 + 1) * H], xres[:], ALU.add)
                wait("scalar", "sVE", v_add_prev)
                st = dma("scalar", "sLS",
                         out_ap[512 * a2 + 128 * t4: 512 * a2 + 128 * (t4 + 1), :],
                         outasm[:, t4 * H:(t4 + 1) * H])
                store_marks.append(st)

    wait("scalar", "sLS", store_marks[-1])
    wait("gpsimd", "sLS", store_marks[-1])

    with nc.Block() as block:
        @block.sync
        def _(e):
            for f in P.streams["sync"]:
                f(e)

        @block.tensor
        def _(e):
            for f in P.streams["tensor"]:
                f(e)

        @block.vector
        def _(e):
            for f in P.streams["vector"]:
                f(e)

        @block.scalar
        def _(e):
            for f in P.streams["scalar"]:
                f(e)

        @block.gpsimd
        def _(e):
            for f in P.streams["gpsimd"]:
                f(e)

    for cm in reversed(list(open_cms)):
        sb_free(cm)
    for cm in list(reversed(ps_cms)) + list(reversed(sem_cms)):
        cm.__exit__(None, None, None)
    return nc


def _host_inputs(x, Wqkv, bqkv, Wo, bo, lut, gidx, uidx, didx):
    x2d = np.ascontiguousarray(x.reshape(S, H))
    slopes = [2.0 ** (-(k) / 2.0) for k in range(NH)]
    jj = np.arange(128)[:, None]
    icq = np.arange(512)[None, :]
    maskneg = np.zeros((128, 2048), BF)
    for dd in range(2):
        for hl in range(2):
            d = 2 * dd + hl
            maskneg[:, 1024 * dd + 512 * hl:1024 * dd + 512 * (hl + 1)] = \
                np.where(128 * d + jj <= icq, 0.0, NEG).astype(BF)
    identf = np.eye(128, dtype=np.float32)
    pos = np.arange(S, dtype=np.float64)

    perm = np.empty(H, np.int64)
    for r in range(NCORE):
        for s2 in range(2):
            h = r + 8 * s2
            perm[256 * r + 128 * s2:256 * r + 128 * (s2 + 1)] = \
                np.arange(128 * h, 128 * (h + 1))
    wo_perm = np.ascontiguousarray(Wo[perm, :])

    in_maps = []
    for c in range(NCORE):
        heads = (c, 8 + c)
        cols = []
        for base in (0, 2048, 4096):
            for h in heads:
                cols.append((base + 128 * h, base + 128 * (h + 1)))
        wqkv_c = np.ascontiguousarray(
            np.concatenate([Wqkv[:, a:b] for a, b in cols], axis=1))
        bq4 = np.ascontiguousarray(
            np.stack([bqkv[a:b] for a, b in cols[:4]], axis=1))
        bv2 = np.concatenate([bqkv[a:b] for a, b in cols[4:]])[None, :]
        tiles = [c + 8 * k for k in range(6)]

        def _rows(idx_mat, t):
            if t < NIT:
                return idx_mat[128 * t:128 * (t + 1)]
            return np.zeros((128, idx_mat.shape[1]), np.int32)
        gidxT = np.ascontiguousarray(
            np.concatenate([_rows(gidx, t) for t in tiles]).T.astype(np.int32))
        uidxT = np.ascontiguousarray(
            np.concatenate([_rows(uidx, t) for t in tiles]).T.astype(np.int32))

        def _cols(idx_mat, t):
            if t < NIT:
                return idx_mat[:, 128 * t:128 * (t + 1)]
            return np.zeros((idx_mat.shape[0], 128), np.int32)
        didxT = np.ascontiguousarray(
            np.concatenate([_cols(didx, t) for t in tiles], axis=1).T.astype(np.int32))
        b4j = np.zeros((128, 2 * S), BF)
        b4i = np.zeros((128, 2 * S), BF)
        b4j[2, :] = 1.0
        b4j[3, :] = 1.0
        b4i[0, :] = 1.0
        b4i[1, :] = 1.0
        for sl, h in enumerate(heads):
            v = slopes[h] * pos
            hi = v.astype(np.float32).astype(BF)
            lo = (v.astype(np.float32) - hi.astype(np.float32)).astype(BF)
            b4j[0, sl * S:(sl + 1) * S] = hi
            b4j[1, sl * S:(sl + 1) * S] = lo
            b4i[2, sl * S:(sl + 1) * S] = -hi
            b4i[3, sl * S:(sl + 1) * S] = -lo
        in_maps.append({
            "cid": np.array([[c]], np.int32),
            "x_own": np.ascontiguousarray(x2d[TOK * c:TOK * (c + 1)]),
            "wqkv": wqkv_c,
            "bq4": bq4.astype(np.float32),
            "bv2": bv2.astype(np.float32),
            "wo": wo_perm,
            "bo_t": np.ascontiguousarray(bo.reshape(16, 128).T),
            "lut": lut.reshape(1, LUTN).astype(np.float32),
            "gidxT": gidxT, "uidxT": uidxT, "didxT": didxT,
            "b4j": b4j, "b4i": b4i,
            "maskneg": maskneg, "identf": identf,
        })
    return in_maps


def kernel(L0_attn_x, L0_attn_Wqkv, L0_attn_bqkv, L0_attn_Wo, L0_attn_bo,
           L0_ffn_lut, L0_ffn_gate_idx, L0_ffn_up_idx, L0_ffn_down_idx):
    in_maps = _host_inputs(
        np.asarray(L0_attn_x, np.float32), np.asarray(L0_attn_Wqkv, np.float32),
        np.asarray(L0_attn_bqkv, np.float32), np.asarray(L0_attn_Wo, np.float32),
        np.asarray(L0_attn_bo, np.float32), np.asarray(L0_ffn_lut, np.float32),
        np.asarray(L0_ffn_gate_idx, np.int32), np.asarray(L0_ffn_up_idx, np.int32),
        np.asarray(L0_ffn_down_idx, np.int32))
    nc = build_nc(taps=False)
    res = run_bass_kernel_spmd(nc, in_maps, core_ids=list(range(NCORE)))
    out = np.concatenate([np.asarray(res.results[c]["out"]) for c in range(NCORE)],
                         axis=0)
    return out.reshape(1, S, H).astype(np.float32)


# version 19
# speedup vs baseline: 1.0881x; 1.0188x over previous
import numpy as np
import ml_dtypes
import concourse.bass as bass
import concourse.mybir as mybir
from concourse.bass_utils import run_bass_kernel_spmd

F32 = mybir.dt.float32
BF16 = mybir.dt.bfloat16
I32 = mybir.dt.int32
AF = mybir.ActivationFunctionType
ALU = mybir.AluOpType
AX = mybir.AxisListType

NCORE = 8
S = 8192
TOK = 1024
H = 2048
NH = 16
DH = 128
NIT = 43
LUTN = 16
SCALE = float(1.0 / np.sqrt(128.0))
NEG = -1.0e30
BF = ml_dtypes.bfloat16
ATT_LAG = 1


class Plan:
    def __init__(self):
        self.streams = {"sync": [], "tensor": [], "vector": [], "scalar": [], "gpsimd": []}
        self.cnt = {}

    def emit(self, eng, fn):
        self.streams[eng].append(fn)

    def bump(self, s_name, n):
        self.cnt[s_name] = self.cnt.get(s_name, 0) + n
        return self.cnt[s_name]

    def cur(self, s_name):
        return self.cnt.get(s_name, 0)


def build_nc(taps=False):
    nc = bass.Bass("TRN2", target_bir_lowering=False, debug=False)

    ap = {}
    for name, shape, dt in [
        ("cid", [1, 1], I32),
        ("x_own", [TOK, H], F32),
        ("wqkv", [H, 768], F32),
        ("bq4", [128, 4], F32),
        ("bv2", [1, 256], F32),
        ("wo", [H, H], F32),
        ("bo_t", [128, 16], F32),
        ("lut", [1, LUTN], F32),
        ("gidxT", [H, 768], I32),
        ("uidxT", [H, 768], I32),
        ("didxT", [768, H], I32),
        ("b4j", [128, 2 * S], BF16),
        ("b4i", [128, 2 * S], BF16),
        ("maskneg", [128, 2048], BF16),
        ("identf", [128, 128], F32),
    ]:
        ap[name] = nc.dram_tensor(name, shape, dt, kind="ExternalInput").ap()
    out_ap = nc.dram_tensor("out", [TOK, H], F32, kind="ExternalOutput").ap()
    tap_ap = {}
    if taps:
        for name, shape, dt in [
            ("t_xnT", [128, NH * TOK], BF16),
            ("t_qT", [128, 2 * S], BF16),
            ("t_kT", [128, 2 * S], BF16),
            ("t_vT", [128, 2 * S], BF16),
            ("t_aoT", [128, 2 * S], BF16),
            ("t_hT", [128, NIT * TOK], BF16),
            ("t_ao_own", [128, NH * TOK], BF16),
        ]:
            tap_ap[name] = nc.dram_tensor(name, shape, dt, kind="ExternalOutput").ap()

    ag_xn_in = nc.dram_tensor("ag_xn_in", [H, 512], BF16)
    ag_xn_in2 = nc.dram_tensor("ag_xn_in2", [H, 512], BF16)
    ag_xn_out = nc.dram_tensor("ag_xn_out", [NCORE * H, 512], BF16, addr_space="Shared")
    ag_xn_out2 = nc.dram_tensor("ag_xn_out2", [NCORE * H, 512], BF16, addr_space="Shared")
    ag_gu_in = nc.dram_tensor("ag_gu_in", [H, 2 * 768], BF16)
    ag_gu_out = nc.dram_tensor("ag_gu_out", [NCORE * H, 2 * 768], BF16, addr_space="Shared")
    ag_wd_in = nc.dram_tensor("ag_wd_in", [768, H], BF16)
    ag_wd_out = nc.dram_tensor("ag_wd_out", [NCORE * 768, H], BF16, addr_space="Shared")
    ag_at_in1 = nc.dram_tensor("ag_at_in1", [128, S], BF16)
    ag_at_out1 = nc.dram_tensor("ag_at_out1", [NCORE * 128, S], BF16, addr_space="Shared")
    ag_at_in2 = nc.dram_tensor("ag_at_in2", [128, S], BF16)
    ag_at_out2 = nc.dram_tensor("ag_at_out2", [NCORE * 128, S], BF16, addr_space="Shared")

    P = Plan()
    SEM = {}
    open_cms = []

    def sb(name, shape, dt):
        cm = nc.sbuf_tensor(name, shape, dt)
        t = cm.__enter__()
        open_cms.append(cm)
        return t, cm

    def sb_free(cm):
        cm.__exit__(None, None, None)
        open_cms.remove(cm)

    def dma(eng, s_name, out, in_, **kw):
        v = P.bump(s_name, 16)
        s_h = SEM[s_name]
        P.emit(eng, lambda e, o=out, i=in_, s_h=s_h, kw=kw:
               e.dma_start(out=o, in_=i, **kw).then_inc(s_h, 16))
        return v

    def wait(eng, s_name, val):
        if val <= 0:
            return
        s_h = SEM[s_name]
        P.emit(eng, lambda e, s_h=s_h, v=val: e.wait_ge(s_h, v))

    def mm(out, lhsT, rhs, start, stop, inc=False):
        if inc:
            v = P.bump("sMM", 1)
            s_h = SEM["sMM"]
            P.emit("tensor", lambda e, o=out, l=lhsT, r=rhs, a=start, b=stop, s_h=s_h:
                   e.matmul(o, l, r, start=a, stop=b).then_inc(s_h, 1))
            return v
        P.emit("tensor", lambda e, o=out, l=lhsT, r=rhs, a=start, b=stop:
               e.matmul(o, l, r, start=a, stop=b))

    def trans(out, in_, ident, inc=False):
        if inc:
            v = P.bump("sMM", 1)
            s_h = SEM["sMM"]
            P.emit("tensor", lambda e, o=out, i=in_, d=ident, s_h=s_h:
                   e.transpose(o, i, d).then_inc(s_h, 1))
            return v
        P.emit("tensor", lambda e, o=out, i=in_, d=ident: e.transpose(o, i, d))

    def act(out, in_, func, bias=None, scale=None, accum_out=None):
        kw = {}
        if bias is not None:
            kw["bias"] = bias
        if scale is not None:
            kw["scale"] = scale
        if accum_out is not None:
            kw["accum_out"] = accum_out
        prev = P.cur("sAC")
        v = P.bump("sAC", 1)
        s_h = SEM["sAC"]
        P.emit("scalar", lambda e, o=out, i=in_, f=func, kw=kw, s_h=s_h, pv=prev:
               (e.wait_ge(s_h, pv) if pv else None,
                e.activation(o, i, f, **kw).then_inc(s_h, 1)))
        return v

    def vop(name, *args, **kw):
        prev = P.cur("sVE")
        v = P.bump("sVE", 1)
        s_h = SEM["sVE"]
        P.emit("vector", lambda e, n=name, a=args, kw=kw, s_h=s_h, pv=prev:
               (e.wait_ge(s_h, pv) if pv else None,
                getattr(e, n)(*a, **kw).then_inc(s_h, 1)))
        return v

    DMA_SEMS = ("sLtab", "sLln", "sLsel", "sLcon", "sLwo", "sLx0", "sLx1",
                "sLw0", "sLw1", "sLwl", "sLxr", "sLG", "sLS")

    def barrier_waits(eng):
        for s_name in ("sMM", "sAC", "sVE") + DMA_SEMS:
            wait(eng, s_name, P.cur(s_name))

    sem_cms = []
    for s_name in DMA_SEMS + ("sMM", "sAC", "sVE", "sCXN", "sCX2", "sCGU", "sCWD", "sCAT"):
        cm = nc.semaphore(s_name)
        SEM[s_name] = cm.__enter__()
        sem_cms.append(cm)

    ps_cms = []

    def psum(name, shape):
        cm = nc.psum_tensor(name, shape, F32)
        t = cm.__enter__()
        ps_cms.append(cm)
        return t
    psg = [psum("psg0", [128, 1024]), psum("psg1", [128, 1024])]
    pv2 = [psum("pv0", [128, 512]), psum("pv1", [128, 512])]
    db2 = [psum("db0", [128, 512]), psum("db1", [128, 512])]

    tbl = {}
    for name, shape, dt in [
        ("onesj", [128, 128], BF16),
        ("ones1f", [1, 128], F32),
        ("identf_s", [128, 128], F32),
        ("bo_s", [128, 16], F32), ("bq4_s", [128, 4], F32),
        ("bv2f", [1, 256], F32), ("lutrow", [1, LUTN], F32),
        ("lutrep", [128, LUTN], F32), ("eps", [128, 1], F32),
    ]:
        tbl[name], _ = sb(name, shape, dt)

    xnownT, cm_xnownT = sb("xnownT", [128, NH * TOK], BF16)

    ld0 = 0
    for dst, src in [
        (tbl["identf_s"][:], ap["identf"]), (tbl["bo_s"][:], ap["bo_t"]),
        (tbl["bq4_s"][:], ap["bq4"]), (tbl["bv2f"][:], ap["bv2"]),
        (tbl["lutrow"][:], ap["lut"]),
    ]:
        ld0 = dma("sync", "sLtab", dst, src)

    vop("memset", tbl["eps"][:], 1.0e-5)
    vop("memset", tbl["onesj"][:], 1.0)
    v0 = vop("memset", tbl["ones1f"][:], 1.0)
    wait("vector", "sLtab", ld0)
    vop("tensor_scalar", tbl["bq4_s"][:, 0:2], tbl["bq4_s"][:, 0:2],
        SCALE, None, ALU.mult)
    wait("tensor", "sLtab", ld0)
    wait("tensor", "sVE", v0)
    m_lut = mm(db2[0][:, 0:LUTN], tbl["ones1f"][:], tbl["lutrow"][:], True, True, inc=True)
    wait("vector", "sMM", m_lut)
    v_lutrep = vop("tensor_copy", tbl["lutrep"][:], db2[0][:, 0:LUTN])

    ln_x, cm_ln_x = sb("ln_x", [128, 8 * H], F32)
    x2s, cm_x2s = sb("x2s", [128, H], F32)
    ln_st, cm_ln_st = sb("ln_st", [128, 4 * 8], F32)
    xn_t, cm_xn_t = sb("xn_t", [128, 2 * H], F32)

    ld_a = dma("sync", "sLln", ln_x[:, 0:4 * H].rearrange("p (a w) -> p a w", a=4),
               ap["x_own"][0:512, :].rearrange("(a p) w -> p a w", p=128))
    ld_b = dma("sync", "sLx1", ln_x[:, 4 * H:8 * H].rearrange("p (a w) -> p a w", a=4),
               ap["x_own"][512:1024, :].rearrange("(a p) w -> p a w", p=128))
    ln_tr_last = {}
    ln_cp = {}
    xnv = xnownT[:].rearrange("p (h t) -> p h t", h=NH)
    for tau in range(8):
        if tau == 0:
            wait("scalar", "sLln", ld_a)
        elif tau == 4:
            wait("scalar", "sLx1", ld_b)
        xt = ln_x[:, H * tau:H * (tau + 1)]
        act(x2s[:], xt, AF.Identity, scale=1.0,
            accum_out=ln_st[:, 4 * tau:4 * tau + 1])
        a_st = act(x2s[:], xt, AF.Square,
                   accum_out=ln_st[:, 4 * tau + 1:4 * tau + 2])
        wait("vector", "sAC", a_st)
        vop("tensor_scalar", ln_st[:, 4 * tau:4 * tau + 1],
            ln_st[:, 4 * tau:4 * tau + 1], 1.0 / H, None, ALU.mult)
        vop("tensor_scalar", ln_st[:, 4 * tau + 1:4 * tau + 2],
            ln_st[:, 4 * tau + 1:4 * tau + 2], 1.0 / H, None, ALU.mult)
        vop("tensor_tensor", ln_st[:, 4 * tau + 2:4 * tau + 3],
            ln_st[:, 4 * tau:4 * tau + 1], ln_st[:, 4 * tau:4 * tau + 1], ALU.mult)
        v_var = vop("tensor_tensor", ln_st[:, 4 * tau + 2:4 * tau + 3],
                    ln_st[:, 4 * tau + 1:4 * tau + 2],
                    ln_st[:, 4 * tau + 2:4 * tau + 3], ALU.subtract)
        wait("scalar", "sVE", v_var)
        a_sd = act(ln_st[:, 4 * tau + 3:4 * tau + 4],
                   ln_st[:, 4 * tau + 2:4 * tau + 3],
                   AF.Sqrt, bias=tbl["eps"][:], scale=1.0)
        wait("vector", "sAC", a_sd)
        vop("reciprocal", ln_st[:, 4 * tau + 3:4 * tau + 4],
            ln_st[:, 4 * tau + 3:4 * tau + 4])
        xslot = xn_t[:, H * (tau % 2):H * (tau % 2) + H]
        if tau >= 2:
            wait("vector", "sMM", ln_tr_last[tau - 2])
        v_xn = vop("tensor_scalar", xslot, ln_x[:, H * tau:H * (tau + 1)],
                   ln_st[:, 4 * tau:4 * tau + 1],
                   ln_st[:, 4 * tau + 3:4 * tau + 4],
                   ALU.subtract, op1=ALU.mult)
        wait("tensor", "sVE", v_xn)
        for half in range(2):
            if tau >= 1:
                wait("tensor", "sVE", ln_cp[(tau - 1, half)])
            for h8 in range(8):
                hh = 8 * half + h8
                m_t = trans(psg[half][:, 128 * h8:128 * (h8 + 1)],
                            xslot[:, 128 * hh:128 * (hh + 1)],
                            tbl["identf_s"][:], inc=(h8 == 7))
            ln_tr_last[tau] = m_t
            wait("vector", "sMM", m_t)
            ln_cp[(tau, half)] = vop(
                "tensor_copy",
                xnv[:, 8 * half:8 * (half + 1), 128 * tau:128 * (tau + 1)],
                psg[half][:].rearrange("p (h w) -> p h w", h=8))
        if tau == 3:
            wait("gpsimd", "sVE", P.cur("sVE"))
            g_h1 = dma("gpsimd", "sLG",
                       ag_xn_in.ap().rearrange("(h p) t -> p h t", p=128),
                       xnownT[:].rearrange("p (h t) -> p h t", h=NH)[:, :, 0:512])
            wait("gpsimd", "sLG", g_h1)
            P.emit("gpsimd", lambda e: e.collective_compute(
                "AllGather", ALU.bypass, replica_groups=[list(range(NCORE))],
                ins=[ag_xn_in.ap().opt()],
                outs=[ag_xn_out.ap().opt()]).then_inc(SEM["sCXN"]))
    v_ln_done = P.cur("sVE")

    wait("gpsimd", "sVE", v_ln_done)
    g_xn = dma("gpsimd", "sLG",
               ag_xn_in2.ap().rearrange("(h p) t -> p h t", p=128),
               xnownT[:].rearrange("p (h t) -> p h t", h=NH)[:, :, 512:1024])
    wait("gpsimd", "sLG", g_xn)
    P.emit("gpsimd", lambda e: e.collective_compute(
        "AllGather", ALU.bypass, replica_groups=[list(range(NCORE))],
        ins=[ag_xn_in2.ap().opt()], outs=[ag_xn_out2.ap().opt()]).then_inc(SEM["sCX2"]))
    if taps:
        dma("gpsimd", "sLG", tap_ap["t_xnT"], xnownT[:])
    xn_contrib_mark = P.cur("sLG")

    for cm in (cm_xn_t, cm_ln_st, cm_x2s, cm_ln_x, cm_xnownT):
        sb_free(cm)

    qT, cm_qT = sb("qT", [128, 2 * S], BF16)
    kT, cm_kT = sb("kT", [128, 2 * S], BF16)
    vT, cm_vT = sb("vT", [128, 2 * S], BF16)
    aoT, cm_aoT = sb("aoT", [128, 2 * S], BF16)
    ptile, cm_ptile = sb("ptile", [128, 2 * 1024], BF16)
    idx_i, cm_idx_i = sb("idx_i", [128, 1024], I32)
    idx_i2, cm_idx_i2 = sb("idx_i2", [128, 1024], I32)
    selw, cm_selw = sb("selw", [128, 1024], BF16)
    selt, cm_selt = sb("selt", [128, 1024], BF16)
    rcpr, cm_rcpr = sb("rcpr", [128, 512], F32)
    w6, cm_w6 = sb("w6", [128, 4 * H], BF16)
    wv2, cm_wv2 = sb("wv2", [128, 2 * H], BF16)
    wcv0, cm_wcv0 = sb("wcv0", [128, H], F32)
    wcv1, cm_wcv1 = sb("wcv1", [128, H], F32)

    wcv = (wcv0, wcv1)
    cast_marks = {}
    for j in range(6):
        if j >= 2:
            wait("scalar", "sAC", cast_marks[j - 2])
        ldv = dma("scalar", "sLwo",
                  wcv[j % 2][:].rearrange("p (h w) -> p h w", h=NH),
                  ap["wqkv"][:, 128 * j:128 * (j + 1)].rearrange("(h p) w -> p h w", p=128))
        wait("scalar", "sLwo", ldv)
        dst = w6[:, H * j:H * (j + 1)] if j < 4 else wv2[:, H * (j - 4):H * (j - 3)]
        cast_marks[j] = act(dst, wcv[j % 2][:], AF.Identity, scale=1.0)
    a_w_ready = cast_marks[5]
    sb_free(cm_wcv1)
    sb_free(cm_wcv0)

    xnstr, cm_xnstr = sb("xnstr", [128, 2 * 8192], BF16)

    sel_state = {"n": 0, "cp": {}, "last_contrib": 0, "pend": None}

    def flush_contrib():
        if sel_state["pend"] is not None:
            pdst, pcols, pmark = sel_state["pend"]
            wait("gpsimd", "sVE", pmark)
            sel_state["last_contrib"] = dma("gpsimd", "sLcon", pdst,
                                            selw[:, 0:pcols])
            sel_state["pend"] = None

    def emit_select_unit(src_ap, cols, dst_ap):
        ku = sel_state["n"]
        sel_state["n"] = ku + 1
        ib = (idx_i, idx_i2)[ku % 2]
        isem = ("sLsel", "sLwl")[ku % 2]
        gate = sel_state["cp"].get(ku - 2, v_lutrep if ku == 0 else 0)
        if gate:
            wait("gpsimd", "sVE", gate)
        ldv = dma("gpsimd", isem, ib[:, 0:cols], src_ap)
        flush_contrib()
        wait("vector", isem, ldv)
        if sel_state["last_contrib"]:
            wait("vector", "sLcon", sel_state["last_contrib"])
        last = 0
        for k in range(LUTN):
            if k == 0:
                vop("tensor_scalar", selw[:, 0:cols], ib[:, 0:cols],
                    0.0, tbl["lutrep"][:, 0:1], ALU.is_equal, ALU.mult)
            else:
                vop("tensor_scalar", selt[:, 0:cols], ib[:, 0:cols],
                    float(k), tbl["lutrep"][:, k:k + 1], ALU.is_equal, ALU.mult)
                last = vop("tensor_tensor", selw[:, 0:cols], selw[:, 0:cols],
                           selt[:, 0:cols], ALU.add)
        sel_state["cp"][ku] = last
        sel_state["pend"] = (dst_ap, cols, last)

    select_units = []
    for mat_name, col_off in (("gidxT", 0), ("uidxT", 768)):
        for hp in range(16):
            select_units.append((
                ap[mat_name][128 * hp:128 * (hp + 1), :], 768,
                ag_gu_in.ap()[128 * hp:128 * (hp + 1), col_off:col_off + 768]))
    for it6 in range(6):
        for half in range(2):
            select_units.append((
                ap["didxT"][128 * it6:128 * (it6 + 1), 1024 * half:1024 * (half + 1)],
                1024,
                ag_wd_in.ap()[128 * it6:128 * (it6 + 1), 1024 * half:1024 * (half + 1)]))
    sel_idx = [0]
    cc_emitted = {"gu": False, "wd": False}

    def emit_gu_cc():
        wait("gpsimd", "sLcon", P.cur("sLcon"))
        P.emit("gpsimd", lambda e: e.collective_compute(
            "AllGather", ALU.bypass, replica_groups=[list(range(NCORE))],
            ins=[ag_gu_in.ap().opt()], outs=[ag_gu_out.ap().opt()]).then_inc(SEM["sCGU"]))

    def emit_wd_cc():
        wait("gpsimd", "sLcon", P.cur("sLcon"))
        P.emit("gpsimd", lambda e: e.collective_compute(
            "AllGather", ALU.bypass, replica_groups=[list(range(NCORE))],
            ins=[ag_wd_in.ap().opt()], outs=[ag_wd_out.ap().opt()]).then_inc(SEM["sCWD"]))
    def pop_select(n):
        for _ in range(n):
            if sel_idx[0] < len(select_units):
                emit_select_unit(*select_units[sel_idx[0]])
                sel_idx[0] += 1
                if sel_idx[0] == 32 and not cc_emitted["gu"]:
                    cc_emitted["gu"] = True
                    flush_contrib()
                    emit_gu_cc()
                if sel_idx[0] == len(select_units):
                    flush_contrib()

    pop_select(9)

    wait("tensor", "sCXN", 1)
    wait("sync", "sCXN", 1)
    wait("sync", "sLG", xn_contrib_mark)
    wait("sync", "sAC", a_w_ready)
    wait("sync", "sLG", P.cur("sLG"))
    wait("scalar", "sLG", P.cur("sLG"))
    wait("tensor", "sVE", v_ln_done)
    wait("tensor", "sAC", P.cur("sAC"))

    qk_evac = {}
    v_evac = {}
    stream_last_mm = {}
    tt_order = [2 * r for r in range(8)] + [2 * r + 1 for r in range(8)]
    for kk, tt in enumerate(tt_order):
        buf = kk % 2
        xc = buf * 8192
        rank = tt // 2
        src_t = ag_xn_out if tt % 2 == 0 else ag_xn_out2
        if kk == 8:
            wait("sync", "sCX2", 1)
            wait("tensor", "sCX2", 1)
        if kk >= 2:
            wait("sync", "sMM", stream_last_mm[kk - 2])
        ldv = dma("sync", "sLx0" if buf == 0 else "sLx1",
                  xnstr[:, xc:xc + 8192].rearrange("p (h t) -> p h t", h=NH),
                  src_t.ap()[rank * H:(rank + 1) * H, 0:512]
                  .rearrange("(h p) t -> p h t", p=128))
        wait("tensor", "sLx0" if buf == 0 else "sLx1", ldv)
        for j in range(4):
            bank = psg[j // 2][:, 512 * (j % 2):512 * (j % 2) + 512]
            if (j, kk - 1) in qk_evac:
                wait("tensor", "sAC", qk_evac[(j, kk - 1)])
            for hh in range(NH):
                mm(bank, w6[:, H * j + 128 * hh: H * j + 128 * (hh + 1)],
                   xnstr[:, xc + 512 * hh: xc + 512 * (hh + 1)],
                   start=(hh == 0), stop=(hh == NH - 1), inc=(hh == NH - 1))
            m_qk = P.cur("sMM")
            wait("scalar", "sMM", m_qk)
            dst = qT if j < 2 else kT
            slot = j % 2
            scl = SCALE if j < 2 else 1.0
            qk_evac[(j, kk)] = act(
                dst[:, slot * S + 512 * tt: slot * S + 512 * (tt + 1)],
                bank, AF.Identity, bias=tbl["bq4_s"][:, j:j + 1], scale=scl)
        for sub in range(4):
            bankA = pv2[sub % 2]
            bankB = db2[sub % 2]
            key = (sub % 2, 4 * kk + sub)
            prev = (sub % 2, 4 * kk + sub - 2)
            if prev in v_evac:
                wait("tensor", "sAC", v_evac[prev])
            for hh in range(NH):
                for slot, bk in ((0, bankA), (1, bankB)):
                    mm(bk[:, 0:128],
                       xnstr[:, xc + 512 * hh + 128 * sub: xc + 512 * hh + 128 * (sub + 1)],
                       wv2[:, H * slot + 128 * hh: H * slot + 128 * (hh + 1)],
                       start=(hh == 0), stop=False)
            for slot, bk in ((0, bankA), (1, bankB)):
                m_v = mm(bk[:, 0:128], tbl["ones1f"][:],
                         tbl["bv2f"][:, 128 * slot:128 * (slot + 1)],
                         start=False, stop=True, inc=(slot == 1))
            stream_last_mm[kk] = m_v
            wait("scalar", "sMM", m_v)
            tg = 4 * tt + sub
            act(vT[:, 0 * S + 128 * tg: 0 * S + 128 * (tg + 1)],
                bankA[:, 0:128], AF.Identity, scale=1.0)
            v_evac[key] = act(vT[:, S + 128 * tg: S + 128 * (tg + 1)],
                              bankB[:, 0:128], AF.Identity, scale=1.0)
        if kk % 2 == 1:
            pop_select(1)

    sb_free(cm_xnstr)
    sb_free(cm_wv2)
    sb_free(cm_w6)

    b4j_s, cm_b4j = sb("b4j_s", [128, S], BF16)
    b4i_s, cm_b4i = sb("b4i_s", [128, S], BF16)
    masks, cm_masks = sb("masks", [128, 2048], BF16)
    wait("sync", "sMM", P.cur("sMM"))
    wait("sync", "sAC", P.cur("sAC"))
    dma("sync", "sLtab", b4j_s[:], ap["b4j"][:, 0:S])
    dma("sync", "sLtab", b4i_s[:], ap["b4i"][:, 0:S])
    ld_b4 = dma("sync", "sLtab", masks[:], ap["maskneg"])
    wait("tensor", "sLtab", ld_b4)
    wait("vector", "sLtab", ld_b4)
    wait("tensor", "sAC", P.cur("sAC"))

    exp_in_bank = {0: 0, 1: 0}
    pt_mark = {0: 0, 1: 0}
    pvdb_guard = {}

    for g in range(32):
        slot = g // 16
        s16 = g % 16
        so = slot * S
        i0 = 512 * s16
        nch = 4 * s16 + 4
        jc0 = max(0, (i0 - 2048) // 128) if slot == 0 else 0
        jc0 -= jc0 % 2
        ngrp_all = nch // 2
        pvb = pv2[g % 2]
        db = db2[g % 2]
        if g == 16:
            wait("gpsimd", "sVE", pvdb_guard[15][0])
            g_c1 = dma("gpsimd", "sLG", ag_at_in1.ap(), aoT[:, 0:S])
            wait("gpsimd", "sLG", g_c1)
            P.emit("gpsimd", lambda e: e.collective_compute(
                "AllGather", ALU.bypass, replica_groups=[list(range(NCORE))],
                ins=[ag_at_in1.ap().opt()],
                outs=[ag_at_out1.ap().opt()]).then_inc(SEM["sCAT"]))
            last_mm_slot0 = P.cur("sMM")
            wait("sync", "sMM", last_mm_slot0)
            dma("sync", "sLtab", b4j_s[0:4, :], ap["b4j"][0:4, S:2 * S])
            ld_sl = dma("sync", "sLtab", b4i_s[0:4, :], ap["b4i"][0:4, S:2 * S])
            wait("tensor", "sLtab", ld_sl)
        if (g - 2) in pvdb_guard:
            wait("tensor", "sVE", pvdb_guard[g - 2][0])
        den_marks = {}
        exp_marks = {}

        def emit_pvden(grp):
            wait("tensor", "sAC", exp_marks[grp])
            for half in range(2):
                jc = 2 * grp + half
                pt = ptile[:, 1024 * (grp % 2) + 512 * half:
                           1024 * (grp % 2) + 512 * (half + 1)]
                mm(pvb[:], vT[:, so + 128 * jc: so + 128 * (jc + 1)], pt,
                   start=(jc == jc0), stop=(jc == nch - 1))
                den_marks[jc] = mm(db[:], tbl["onesj"][:], pt,
                                   start=(jc == jc0), stop=(jc == nch - 1), inc=True)
            pt_mark[grp % 2] = den_marks[2 * grp + 1]

        g0 = jc0 // 2
        for grp in range(g0, ngrp_all):
            sgb = psg[grp % 2]
            if exp_in_bank[grp % 2]:
                wait("tensor", "sAC", exp_in_bank[grp % 2])
            m_b = 0
            for half in range(2):
                jc = 2 * grp + half
                sl = sgb[:, 512 * half:512 * half + 512]
                mm(sl, kT[:, so + 128 * jc: so + 128 * (jc + 1)],
                   qT[:, so + i0: so + i0 + 512], start=True, stop=False)
                m_b = mm(sl, b4j_s[:, 128 * jc: 128 * (jc + 1)],
                         b4i_s[:, i0: i0 + 512],
                         start=False, stop=True, inc=True)
            if grp >= ngrp_all - 2:
                dd = grp - (ngrp_all - 2)
                wait("vector", "sMM", m_b)
                v_m = vop("tensor_tensor", sgb[:], sgb[:],
                          masks[:, 1024 * dd:1024 * (dd + 1)], ALU.add)
                wait("scalar", "sVE", v_m)
            wait("scalar", "sMM", m_b)
            if pt_mark[grp % 2]:
                wait("scalar", "sMM", pt_mark[grp % 2])
            e_m = act(ptile[:, 1024 * (grp % 2):1024 * (grp % 2 + 1)], sgb[:],
                      AF.Exp)
            exp_in_bank[grp % 2] = e_m
            exp_marks[grp] = e_m
            if grp - ATT_LAG >= g0:
                emit_pvden(grp - ATT_LAG)
        for gtail in range(max(g0, ngrp_all - ATT_LAG), ngrp_all):
            emit_pvden(gtail)
        if not (slot == 1 and s16 < 6):
            pop_select(1)
        m_last_den = den_marks[nch - 1]
        wait("vector", "sMM", m_last_den)
        v_rc = vop("reciprocal", rcpr[:], db[:])
        v_n = vop("tensor_tensor", aoT[:, so + i0: so + i0 + 512],
                  pvb[:], rcpr[:], ALU.mult)
        pvdb_guard[g] = (v_n, v_rc)
    pop_select(len(select_units))

    att_done = P.cur("sVE")
    if taps:
        wait("gpsimd", "sVE", att_done)
        dma("gpsimd", "sLG", tap_ap["t_qT"], qT[:])
        dma("gpsimd", "sLG", tap_ap["t_kT"], kT[:])
        dma("gpsimd", "sLG", tap_ap["t_vT"], vT[:])
        dma("gpsimd", "sLG", tap_ap["t_aoT"], aoT[:])
    wait("gpsimd", "sVE", att_done)
    g_ctr2 = dma("gpsimd", "sLG", ag_at_in2.ap(), aoT[:, S:2 * S])

    for cm in (cm_masks, cm_b4i, cm_b4j, cm_rcpr, cm_selt, cm_selw,
               cm_idx_i2, cm_idx_i, cm_ptile, cm_aoT, cm_vT, cm_kT,
               cm_qT):
        sb_free(cm)

    h_T, cm_hT = sb("h_T", [128, NIT * TOK], BF16)
    ao_own, cm_ao_own = sb("ao_own", [128, NH * TOK], BF16)
    wgu_cms = []
    wgu = []
    for wn in ("wgu0", "wgu1"):
        t, cmw = sb(wn, [128, 2 * H], BF16)
        wgu.append(t)
        wgu_cms.append(cmw)
    sg, cm_sg = sb("sg", [128, 512], BF16)
    xnown2, cm_xnown2 = sb("xnown2", [128, NH * TOK], BF16)

    for s_name in ("sMM", "sVE", "sAC", "sLcon", "sLsel", "sLwl", "sLtab"):
        wait("sync", s_name, P.cur(s_name))
    wait("sync", "sLG", g_ctr2)
    if taps:
        for e2 in ("sync", "vector", "scalar"):
            wait(e2, "sLG", P.cur("sLG"))
    dma("sync", "sLtab",
        xnown2[:].rearrange("p (h t) -> p h t", h=NH)[:, :, 0:512],
        ag_xn_in.ap().rearrange("(h p) t -> p h t", p=128))
    ld_xn2 = dma("sync", "sLtab",
                 xnown2[:].rearrange("p (h t) -> p h t", h=NH)[:, :, 512:1024],
                 ag_xn_in2.ap().rearrange("(h p) t -> p h t", p=128))
    wait("tensor", "sCGU", 1)
    wait("sync", "sCGU", 1)
    wait("tensor", "sLtab", ld_xn2)
    wait("tensor", "sAC", P.cur("sAC"))
    wait("tensor", "sVE", P.cur("sVE"))

    gu_last_mm = {}
    silu_marks = {}
    htt_marks = {}
    for ti in range(NIT):
        buf = wgu[ti % 2]
        rank = ti % NCORE
        loc = ti // NCORE
        gs = "sLw0" if ti % 2 == 0 else "sLw1"
        if ti >= 2:
            wait("sync", "sMM", gu_last_mm[ti - 2])
        dma("sync", gs, buf[:, 0:H].rearrange("p (h w) -> p h w", h=NH),
            ag_gu_out.ap()[rank * H:(rank + 1) * H, 128 * loc:128 * (loc + 1)]
            .rearrange("(h p) w -> p h w", p=128))
        ldv = dma("sync", gs, buf[:, H:2 * H].rearrange("p (h w) -> p h w", h=NH),
                  ag_gu_out.ap()[rank * H:(rank + 1) * H, 768 + 128 * loc:768 + 128 * (loc + 1)]
                  .rearrange("(h p) w -> p h w", p=128))
        if ti == 4:
            wait("gpsimd", gs, ldv)
            P.emit("gpsimd", lambda e: e.collective_compute(
                "AllGather", ALU.bypass, replica_groups=[list(range(NCORE))],
                ins=[ag_at_in2.ap().opt()],
                outs=[ag_at_out2.ap().opt()]).then_inc(SEM["sCAT"]))
        elif ti == 16:
            wait("gpsimd", gs, ldv)
            emit_wd_cc()
        wait("tensor", gs, ldv)
        for t2 in range(2):
            if (ti - 1, t2) in silu_marks:
                wait("tensor", "sAC", silu_marks[(ti - 1, t2)])
            if (ti - 1, t2) in htt_marks:
                wait("tensor", "sVE", htt_marks[(ti - 1, t2)])
        gm = {}
        for hh in range(NH):
            for t2 in range(2):
                gm[t2] = mm(psg[0][:, 512 * t2:512 * (t2 + 1)],
                            buf[:, 128 * hh:128 * (hh + 1)],
                            xnown2[:, hh * TOK + 512 * t2: hh * TOK + 512 * (t2 + 1)],
                            start=(hh == 0), stop=(hh == NH - 1),
                            inc=(hh == NH - 1))
        for hh in range(NH):
            for t2 in range(2):
                gu_last_mm[ti] = mm(
                    psg[1][:, 512 * t2:512 * (t2 + 1)],
                    buf[:, H + 128 * hh:H + 128 * (hh + 1)],
                    xnown2[:, hh * TOK + 512 * t2: hh * TOK + 512 * (t2 + 1)],
                    start=(hh == 0), stop=(hh == NH - 1), inc=(hh == NH - 1))
        m_u = gu_last_mm[ti]
        for t2 in range(2):
            gbank = psg[0][:, 512 * t2:512 * (t2 + 1)]
            ubank = psg[1][:, 512 * t2:512 * (t2 + 1)]
            wait("scalar", "sMM", gm[t2])
            if (ti, t2 - 1) in htt_marks:
                wait("scalar", "sVE", htt_marks[(ti, t2 - 1)])
            elif t2 == 0 and (ti - 1, 1) in htt_marks:
                wait("scalar", "sVE", htt_marks[(ti - 1, 1)])
            a_s = act(sg[:], gbank, AF.Silu)
            silu_marks[(ti, t2)] = a_s
            wait("vector", "sMM", m_u)
            wait("vector", "sAC", a_s)
            htt_marks[(ti, t2)] = vop(
                "tensor_tensor", h_T[:, ti * TOK + 512 * t2: ti * TOK + 512 * (t2 + 1)],
                sg[:], ubank, ALU.mult)
    if taps:
        wait("gpsimd", "sVE", P.cur("sVE"))
        dma("gpsimd", "sLG", tap_ap["t_hT"], h_T[:])

    wait("gpsimd", "sCAT", 2)

    def _ao_own_load(e):
        with e.register("rc") as rc, e.register("ro") as ro:
            e.reg_load(rc, ap["cid"][0:1, 0:1])
            e.reg_mul(ro, rc, TOK)
            for d in range(NH):
                src = (ag_at_out1 if d % 2 == 0 else ag_at_out2).ap()
                r = d // 2
                e.dma_start(
                    out=ao_own[:, d * TOK:(d + 1) * TOK],
                    in_=src[128 * r:128 * (r + 1),
                            bass.ds(bass.RuntimeValue(ro), TOK)],
                ).then_inc(SEM["sLG"], 16)
    P.bump("sLG", 16 * NH)
    P.emit("gpsimd", _ao_own_load)
    g_ao = P.cur("sLG")
    if taps:
        wait("gpsimd", "sLG", g_ao)
        dma("gpsimd", "sLG", tap_ap["t_ao_own"], ao_own[:])

    sb_free(cm_xnown2)
    sb_free(cm_sg)
    sb_free(wgu_cms[1])
    sb_free(wgu_cms[0])

    wlhs0, _ = sb("wlhs0", [128, (16 + 48) * 128], BF16)
    wlhs1, _ = sb("wlhs1", [128, (16 + 48) * 128], BF16)
    ostage0, _ = sb("ostage0", [128, 512], F32)
    ostage1, _ = sb("ostage1", [128, 512], F32)
    outasm, _ = sb("outasm", [128, 4 * H], F32)
    xres, _ = sb("xres", [128, H], F32)
    wst0, _ = sb("wst0", [128, H], F32)

    wait("sync", "sCWD", 1)
    wait("tensor", "sCWD", 1)
    wait("tensor", "sLG", g_ao)
    barrier_waits("tensor")
    barrier_waits("sync")
    barrier_waits("vector")
    barrier_waits("scalar")

    evac_marks = {}
    tr_marks = {}
    cp_marks = {}
    wo_cast = {}
    wl_last = {0: 0, 1: 0}
    store_marks = []
    hp_seq = [(a2, hp) for a2 in range(2) for hp in range(16)]
    for idx, (a2, hp) in enumerate(hp_seq):
        wl = (wlhs0, wlhs1)[idx % 2]
        osg = (ostage0, ostage1)[idx % 2]
        mainbank = psg[idx % 2][:, 0:512]
        if a2 == 1 and hp == 0 and store_marks:
            wait("vector", "sLS", store_marks[-1])
        wls = "sLw0" if idx % 2 == 0 else "sLw1"
        if wl_last[idx % 2]:
            wait("sync", "sMM", wl_last[idx % 2])
        if idx >= 1:
            wait("sync", "sVE", wo_cast[idx - 1])
        ld_wo = dma("sync", wls, wst0[:].rearrange("p (d w) -> p d w", d=NH),
                    ap["wo"][:, 128 * hp:128 * (hp + 1)].rearrange("(d p) w -> p d w", p=128))
        wd_v = ag_wd_out.ap().rearrange("(e q) w -> q e w", e=NCORE)
        for l6 in range(6):
            ldv = dma("sync", wls,
                      wl[:, H + 1024 * l6:H + 1024 * (l6 + 1)]
                      .rearrange("p (ee w) -> p ee w", ee=8),
                      wd_v[128 * l6:128 * (l6 + 1), :, 128 * hp:128 * (hp + 1)])
        wait("vector", wls, ldv)
        if wl_last[idx % 2]:
            wait("vector", "sMM", wl_last[idx % 2])
        wo_cast[idx] = vop("tensor_copy", wl[:, 0:H], wst0[:])
        wait("tensor", wls, ldv)
        wait("tensor", "sVE", wo_cast[idx])
        if (idx - 2) in evac_marks:
            wait("tensor", "sAC", evac_marks[idx - 2])
        for d in range(NH):
            mm(mainbank, wl[:, 128 * d:128 * (d + 1)],
               ao_own[:, d * TOK + 512 * a2: d * TOK + 512 * (a2 + 1)],
               start=(d == 0), stop=False)
        for ti in range(NIT):
            mm(mainbank, wl[:, H + 128 * ti:H + 128 * (ti + 1)],
               h_T[:, ti * TOK + 512 * a2: ti * TOK + 512 * (a2 + 1)],
               start=False, stop=(ti == NIT - 1), inc=(ti == NIT - 1))
        m_e = P.cur("sMM")
        wl_last[idx % 2] = m_e
        wait("scalar", "sMM", m_e)
        if (idx - 2) in tr_marks:
            wait("scalar", "sMM", tr_marks[idx - 2])
        a_e = act(osg[:], mainbank, AF.Identity,
                  bias=tbl["bo_s"][:, hp:hp + 1], scale=1.0)
        evac_marks[idx] = a_e
        wait("tensor", "sAC", a_e)
        for t4 in range(4):
            tbank = pv2[t4 % 2]
            key = (idx, t4)
            prevkey = (idx, t4 - 2) if t4 >= 2 else (idx - 1, t4 + 2)
            if prevkey in cp_marks:
                wait("tensor", "sVE", cp_marks[prevkey])
            m_t = trans(tbank[:, 0:128], osg[:, 128 * t4:128 * (t4 + 1)],
                        tbl["identf_s"][:], inc=True)
            if t4 == 3:
                tr_marks[idx] = m_t
            wait("vector", "sMM", m_t)
            cp_marks[key] = vop(
                "tensor_copy", outasm[:, t4 * H + 128 * hp: t4 * H + 128 * (hp + 1)],
                tbank[:, 0:128])
        if hp == 15:
            v_add_prev = 0
            for t4 in range(4):
                if v_add_prev:
                    wait("sync", "sVE", v_add_prev)
                ldx = dma("sync", "sLxr", xres[:],
                          ap["x_own"][512 * a2 + 128 * t4: 512 * a2 + 128 * (t4 + 1), :])
                wait("vector", "sLxr", ldx)
                v_add_prev = vop("tensor_tensor", outasm[:, t4 * H:(t4 + 1) * H],
                                 outasm[:, t4 * H:(t4 + 1) * H], xres[:], ALU.add)
                wait("scalar", "sVE", v_add_prev)
                st = dma("scalar", "sLS",
                         out_ap[512 * a2 + 128 * t4: 512 * a2 + 128 * (t4 + 1), :],
                         outasm[:, t4 * H:(t4 + 1) * H])
                store_marks.append(st)

    wait("scalar", "sLS", store_marks[-1])
    wait("gpsimd", "sLS", store_marks[-1])

    with nc.Block() as block:
        @block.sync
        def _(e):
            for f in P.streams["sync"]:
                f(e)

        @block.tensor
        def _(e):
            for f in P.streams["tensor"]:
                f(e)

        @block.vector
        def _(e):
            for f in P.streams["vector"]:
                f(e)

        @block.scalar
        def _(e):
            for f in P.streams["scalar"]:
                f(e)

        @block.gpsimd
        def _(e):
            for f in P.streams["gpsimd"]:
                f(e)

    for cm in reversed(list(open_cms)):
        sb_free(cm)
    for cm in list(reversed(ps_cms)) + list(reversed(sem_cms)):
        cm.__exit__(None, None, None)
    return nc


def _host_inputs(x, Wqkv, bqkv, Wo, bo, lut, gidx, uidx, didx):
    x2d = np.ascontiguousarray(x.reshape(S, H))
    slopes = [2.0 ** (-(k) / 2.0) for k in range(NH)]
    jj = np.arange(128)[:, None]
    icq = np.arange(512)[None, :]
    maskneg = np.zeros((128, 2048), BF)
    for dd in range(2):
        for hl in range(2):
            d = 2 * dd + hl
            maskneg[:, 1024 * dd + 512 * hl:1024 * dd + 512 * (hl + 1)] = \
                np.where(128 * d + jj <= icq, 0.0, NEG).astype(BF)
    identf = np.eye(128, dtype=np.float32)
    pos = np.arange(S, dtype=np.float64)

    perm = np.empty(H, np.int64)
    for r in range(NCORE):
        for s2 in range(2):
            h = r + 8 * s2
            perm[256 * r + 128 * s2:256 * r + 128 * (s2 + 1)] = \
                np.arange(128 * h, 128 * (h + 1))
    wo_perm = np.ascontiguousarray(Wo[perm, :])

    in_maps = []
    for c in range(NCORE):
        heads = (c, 8 + c)
        cols = []
        for base in (0, 2048, 4096):
            for h in heads:
                cols.append((base + 128 * h, base + 128 * (h + 1)))
        wqkv_c = np.ascontiguousarray(
            np.concatenate([Wqkv[:, a:b] for a, b in cols], axis=1))
        bq4 = np.ascontiguousarray(
            np.stack([bqkv[a:b] for a, b in cols[:4]], axis=1))
        bv2 = np.concatenate([bqkv[a:b] for a, b in cols[4:]])[None, :]
        tiles = [c + 8 * k for k in range(6)]

        def _rows(idx_mat, t):
            if t < NIT:
                return idx_mat[128 * t:128 * (t + 1)]
            return np.zeros((128, idx_mat.shape[1]), np.int32)
        gidxT = np.ascontiguousarray(
            np.concatenate([_rows(gidx, t) for t in tiles]).T.astype(np.int32))
        uidxT = np.ascontiguousarray(
            np.concatenate([_rows(uidx, t) for t in tiles]).T.astype(np.int32))

        def _cols(idx_mat, t):
            if t < NIT:
                return idx_mat[:, 128 * t:128 * (t + 1)]
            return np.zeros((idx_mat.shape[0], 128), np.int32)
        didxT = np.ascontiguousarray(
            np.concatenate([_cols(didx, t) for t in tiles], axis=1).T.astype(np.int32))
        b4j = np.zeros((128, 2 * S), BF)
        b4i = np.zeros((128, 2 * S), BF)
        b4j[2, :] = 1.0
        b4j[3, :] = 1.0
        b4i[0, :] = 1.0
        b4i[1, :] = 1.0
        for sl, h in enumerate(heads):
            v = slopes[h] * pos
            hi = v.astype(np.float32).astype(BF)
            lo = (v.astype(np.float32) - hi.astype(np.float32)).astype(BF)
            b4j[0, sl * S:(sl + 1) * S] = hi
            b4j[1, sl * S:(sl + 1) * S] = lo
            b4i[2, sl * S:(sl + 1) * S] = -hi
            b4i[3, sl * S:(sl + 1) * S] = -lo
        in_maps.append({
            "cid": np.array([[c]], np.int32),
            "x_own": np.ascontiguousarray(x2d[TOK * c:TOK * (c + 1)]),
            "wqkv": wqkv_c,
            "bq4": bq4.astype(np.float32),
            "bv2": bv2.astype(np.float32),
            "wo": wo_perm,
            "bo_t": np.ascontiguousarray(bo.reshape(16, 128).T),
            "lut": lut.reshape(1, LUTN).astype(np.float32),
            "gidxT": gidxT, "uidxT": uidxT, "didxT": didxT,
            "b4j": b4j, "b4i": b4i,
            "maskneg": maskneg, "identf": identf,
        })
    return in_maps


def kernel(L0_attn_x, L0_attn_Wqkv, L0_attn_bqkv, L0_attn_Wo, L0_attn_bo,
           L0_ffn_lut, L0_ffn_gate_idx, L0_ffn_up_idx, L0_ffn_down_idx):
    in_maps = _host_inputs(
        np.asarray(L0_attn_x, np.float32), np.asarray(L0_attn_Wqkv, np.float32),
        np.asarray(L0_attn_bqkv, np.float32), np.asarray(L0_attn_Wo, np.float32),
        np.asarray(L0_attn_bo, np.float32), np.asarray(L0_ffn_lut, np.float32),
        np.asarray(L0_ffn_gate_idx, np.int32), np.asarray(L0_ffn_up_idx, np.int32),
        np.asarray(L0_ffn_down_idx, np.int32))
    nc = build_nc(taps=False)
    res = run_bass_kernel_spmd(nc, in_maps, core_ids=list(range(NCORE)))
    out = np.concatenate([np.asarray(res.results[c]["out"]) for c in range(NCORE)],
                         axis=0)
    return out.reshape(1, S, H).astype(np.float32)


# version 20
# speedup vs baseline: 1.0998x; 1.0108x over previous
import numpy as np
import ml_dtypes
import concourse.bass as bass
import concourse.mybir as mybir
from concourse.bass_utils import run_bass_kernel_spmd

F32 = mybir.dt.float32
BF16 = mybir.dt.bfloat16
I32 = mybir.dt.int32
AF = mybir.ActivationFunctionType
ALU = mybir.AluOpType
AX = mybir.AxisListType

NCORE = 8
S = 8192
TOK = 1024
H = 2048
NH = 16
DH = 128
NIT = 43
LUTN = 16
SCALE = float(1.0 / np.sqrt(128.0))
NEG = -1.0e30
BF = ml_dtypes.bfloat16
ATT_LAG = 1


class Plan:
    def __init__(self):
        self.streams = {"sync": [], "tensor": [], "vector": [], "scalar": [], "gpsimd": []}
        self.cnt = {}

    def emit(self, eng, fn):
        self.streams[eng].append(fn)

    def bump(self, s_name, n):
        self.cnt[s_name] = self.cnt.get(s_name, 0) + n
        return self.cnt[s_name]

    def cur(self, s_name):
        return self.cnt.get(s_name, 0)


def build_nc(taps=False):
    nc = bass.Bass("TRN2", target_bir_lowering=False, debug=False)

    ap = {}
    for name, shape, dt in [
        ("cid", [1, 1], I32),
        ("x_own", [TOK, H], F32),
        ("wqkv", [H, 768], F32),
        ("bq4", [128, 4], F32),
        ("bv2", [1, 256], F32),
        ("wo", [H, H], F32),
        ("bo_t", [128, 16], F32),
        ("lut", [1, LUTN], F32),
        ("gidxT", [H, 768], I32),
        ("uidxT", [H, 768], I32),
        ("didxT", [768, H], I32),
        ("b4j", [128, 2 * S], BF16),
        ("b4i", [128, 2 * S], BF16),
        ("maskneg", [128, 2048], BF16),
        ("identf", [128, 128], F32),
    ]:
        ap[name] = nc.dram_tensor(name, shape, dt, kind="ExternalInput").ap()
    out_ap = nc.dram_tensor("out", [TOK, H], F32, kind="ExternalOutput").ap()
    tap_ap = {}
    if taps:
        for name, shape, dt in [
            ("t_xnT", [128, NH * TOK], BF16),
            ("t_qT", [128, 2 * S], BF16),
            ("t_kT", [128, 2 * S], BF16),
            ("t_vT", [128, 2 * S], BF16),
            ("t_aoT", [128, 2 * S], BF16),
            ("t_hT", [128, NIT * TOK], BF16),
            ("t_ao_own", [128, NH * TOK], BF16),
        ]:
            tap_ap[name] = nc.dram_tensor(name, shape, dt, kind="ExternalOutput").ap()

    ag_xn_in = nc.dram_tensor("ag_xn_in", [H, 512], BF16)
    ag_xn_in2 = nc.dram_tensor("ag_xn_in2", [H, 512], BF16)
    ag_xn_out = nc.dram_tensor("ag_xn_out", [NCORE * H, 512], BF16, addr_space="Shared")
    ag_xn_out2 = nc.dram_tensor("ag_xn_out2", [NCORE * H, 512], BF16, addr_space="Shared")
    ag_gu_in = nc.dram_tensor("ag_gu_in", [H, 2 * 768], BF16)
    ag_gu_out = nc.dram_tensor("ag_gu_out", [NCORE * H, 2 * 768], BF16, addr_space="Shared")
    ag_wd_in = nc.dram_tensor("ag_wd_in", [768, H], BF16)
    ag_wd_out = nc.dram_tensor("ag_wd_out", [NCORE * 768, H], BF16, addr_space="Shared")
    ag_at_in1 = nc.dram_tensor("ag_at_in1", [128, S], BF16)
    ag_at_out1 = nc.dram_tensor("ag_at_out1", [NCORE * 128, S], BF16, addr_space="Shared")
    ag_at_in2 = nc.dram_tensor("ag_at_in2", [128, S], BF16)
    ag_at_out2 = nc.dram_tensor("ag_at_out2", [NCORE * 128, S], BF16, addr_space="Shared")

    P = Plan()
    SEM = {}
    open_cms = []

    def sb(name, shape, dt):
        cm = nc.sbuf_tensor(name, shape, dt)
        t = cm.__enter__()
        open_cms.append(cm)
        return t, cm

    def sb_free(cm):
        cm.__exit__(None, None, None)
        open_cms.remove(cm)

    def dma(eng, s_name, out, in_, **kw):
        v = P.bump(s_name, 16)
        s_h = SEM[s_name]
        P.emit(eng, lambda e, o=out, i=in_, s_h=s_h, kw=kw:
               e.dma_start(out=o, in_=i, **kw).then_inc(s_h, 16))
        return v

    def wait(eng, s_name, val):
        if val <= 0:
            return
        s_h = SEM[s_name]
        P.emit(eng, lambda e, s_h=s_h, v=val: e.wait_ge(s_h, v))

    def mm(out, lhsT, rhs, start, stop, inc=False):
        if inc:
            v = P.bump("sMM", 1)
            s_h = SEM["sMM"]
            P.emit("tensor", lambda e, o=out, l=lhsT, r=rhs, a=start, b=stop, s_h=s_h:
                   e.matmul(o, l, r, start=a, stop=b).then_inc(s_h, 1))
            return v
        P.emit("tensor", lambda e, o=out, l=lhsT, r=rhs, a=start, b=stop:
               e.matmul(o, l, r, start=a, stop=b))

    def trans(out, in_, ident, inc=False):
        if inc:
            v = P.bump("sMM", 1)
            s_h = SEM["sMM"]
            P.emit("tensor", lambda e, o=out, i=in_, d=ident, s_h=s_h:
                   e.transpose(o, i, d).then_inc(s_h, 1))
            return v
        P.emit("tensor", lambda e, o=out, i=in_, d=ident: e.transpose(o, i, d))

    def act(out, in_, func, bias=None, scale=None, accum_out=None):
        kw = {}
        if bias is not None:
            kw["bias"] = bias
        if scale is not None:
            kw["scale"] = scale
        if accum_out is not None:
            kw["accum_out"] = accum_out
        prev = P.cur("sAC")
        v = P.bump("sAC", 1)
        s_h = SEM["sAC"]
        P.emit("scalar", lambda e, o=out, i=in_, f=func, kw=kw, s_h=s_h, pv=prev:
               (e.wait_ge(s_h, pv) if pv else None,
                e.activation(o, i, f, **kw).then_inc(s_h, 1)))
        return v

    def vop(name, *args, **kw):
        prev = P.cur("sVE")
        v = P.bump("sVE", 1)
        s_h = SEM["sVE"]
        P.emit("vector", lambda e, n=name, a=args, kw=kw, s_h=s_h, pv=prev:
               (e.wait_ge(s_h, pv) if pv else None,
                getattr(e, n)(*a, **kw).then_inc(s_h, 1)))
        return v

    DMA_SEMS = ("sLtab", "sLln", "sLsel", "sLcon", "sLwo", "sLx0", "sLx1",
                "sLw0", "sLw1", "sLwl", "sLxr", "sLG", "sLS")

    def barrier_waits(eng):
        for s_name in ("sMM", "sAC", "sVE") + DMA_SEMS:
            wait(eng, s_name, P.cur(s_name))

    sem_cms = []
    for s_name in DMA_SEMS + ("sMM", "sAC", "sVE", "sCXN", "sCX2", "sCGU", "sCWD", "sCAT"):
        cm = nc.semaphore(s_name)
        SEM[s_name] = cm.__enter__()
        sem_cms.append(cm)

    ps_cms = []

    def psum(name, shape):
        cm = nc.psum_tensor(name, shape, F32)
        t = cm.__enter__()
        ps_cms.append(cm)
        return t
    psg = [psum("psg0", [128, 1024]), psum("psg1", [128, 1024])]
    pv2 = [psum("pv0", [128, 512]), psum("pv1", [128, 512])]
    db2 = [psum("db0", [128, 512]), psum("db1", [128, 512])]

    tbl = {}
    for name, shape, dt in [
        ("onesj", [128, 128], BF16),
        ("ones1f", [1, 128], F32),
        ("identf_s", [128, 128], F32),
        ("bo_s", [128, 16], F32), ("bq4_s", [128, 4], F32),
        ("bv2f", [1, 256], F32), ("lutrow", [1, LUTN], F32),
        ("lutrep", [128, LUTN], F32), ("eps", [128, 1], F32),
    ]:
        tbl[name], _ = sb(name, shape, dt)

    xnownT, cm_xnownT = sb("xnownT", [128, NH * TOK], BF16)

    ld0 = 0
    for dst, src in [
        (tbl["identf_s"][:], ap["identf"]), (tbl["bo_s"][:], ap["bo_t"]),
        (tbl["bq4_s"][:], ap["bq4"]), (tbl["bv2f"][:], ap["bv2"]),
        (tbl["lutrow"][:], ap["lut"]),
    ]:
        ld0 = dma("sync", "sLtab", dst, src)

    vop("memset", tbl["eps"][:], 1.0e-5)
    vop("memset", tbl["onesj"][:], 1.0)
    v0 = vop("memset", tbl["ones1f"][:], 1.0)
    wait("vector", "sLtab", ld0)
    vop("tensor_scalar", tbl["bq4_s"][:, 0:2], tbl["bq4_s"][:, 0:2],
        SCALE, None, ALU.mult)
    wait("tensor", "sLtab", ld0)
    wait("tensor", "sVE", v0)
    m_lut = mm(db2[0][:, 0:LUTN], tbl["ones1f"][:], tbl["lutrow"][:], True, True, inc=True)
    wait("vector", "sMM", m_lut)
    v_lutrep = vop("tensor_copy", tbl["lutrep"][:], db2[0][:, 0:LUTN])

    ln_x, cm_ln_x = sb("ln_x", [128, 8 * H], F32)
    x2s, cm_x2s = sb("x2s", [128, H], F32)
    ln_st, cm_ln_st = sb("ln_st", [128, 4 * 8], F32)
    xn_t, cm_xn_t = sb("xn_t", [128, 2 * H], F32)

    ld_a = dma("sync", "sLln", ln_x[:, 0:4 * H].rearrange("p (a w) -> p a w", a=4),
               ap["x_own"][0:512, :].rearrange("(a p) w -> p a w", p=128))
    ld_b = dma("sync", "sLx1", ln_x[:, 4 * H:8 * H].rearrange("p (a w) -> p a w", a=4),
               ap["x_own"][512:1024, :].rearrange("(a p) w -> p a w", p=128))
    ln_tr_last = {}
    ln_cp = {}
    xnv = xnownT[:].rearrange("p (h t) -> p h t", h=NH)
    for tau in range(8):
        if tau == 0:
            wait("scalar", "sLln", ld_a)
        elif tau == 4:
            wait("scalar", "sLx1", ld_b)
        xt = ln_x[:, H * tau:H * (tau + 1)]
        act(x2s[:], xt, AF.Identity, scale=1.0,
            accum_out=ln_st[:, 4 * tau:4 * tau + 1])
        a_st = act(x2s[:], xt, AF.Square,
                   accum_out=ln_st[:, 4 * tau + 1:4 * tau + 2])
        wait("vector", "sAC", a_st)
        vop("tensor_scalar", ln_st[:, 4 * tau:4 * tau + 1],
            ln_st[:, 4 * tau:4 * tau + 1], 1.0 / H, None, ALU.mult)
        vop("tensor_scalar", ln_st[:, 4 * tau + 1:4 * tau + 2],
            ln_st[:, 4 * tau + 1:4 * tau + 2], 1.0 / H, None, ALU.mult)
        vop("tensor_tensor", ln_st[:, 4 * tau + 2:4 * tau + 3],
            ln_st[:, 4 * tau:4 * tau + 1], ln_st[:, 4 * tau:4 * tau + 1], ALU.mult)
        v_var = vop("tensor_tensor", ln_st[:, 4 * tau + 2:4 * tau + 3],
                    ln_st[:, 4 * tau + 1:4 * tau + 2],
                    ln_st[:, 4 * tau + 2:4 * tau + 3], ALU.subtract)
        wait("scalar", "sVE", v_var)
        a_sd = act(ln_st[:, 4 * tau + 3:4 * tau + 4],
                   ln_st[:, 4 * tau + 2:4 * tau + 3],
                   AF.Sqrt, bias=tbl["eps"][:], scale=1.0)
        wait("vector", "sAC", a_sd)
        vop("reciprocal", ln_st[:, 4 * tau + 3:4 * tau + 4],
            ln_st[:, 4 * tau + 3:4 * tau + 4])
        xslot = xn_t[:, H * (tau % 2):H * (tau % 2) + H]
        if tau >= 2:
            wait("vector", "sMM", ln_tr_last[tau - 2])
        v_xn = vop("tensor_scalar", xslot, ln_x[:, H * tau:H * (tau + 1)],
                   ln_st[:, 4 * tau:4 * tau + 1],
                   ln_st[:, 4 * tau + 3:4 * tau + 4],
                   ALU.subtract, op1=ALU.mult)
        wait("tensor", "sVE", v_xn)
        for half in range(2):
            if tau >= 1:
                wait("tensor", "sVE", ln_cp[(tau - 1, half)])
            for h8 in range(8):
                hh = 8 * half + h8
                m_t = trans(psg[half][:, 128 * h8:128 * (h8 + 1)],
                            xslot[:, 128 * hh:128 * (hh + 1)],
                            tbl["identf_s"][:], inc=(h8 == 7))
            ln_tr_last[tau] = m_t
            wait("vector", "sMM", m_t)
            ln_cp[(tau, half)] = vop(
                "tensor_copy",
                xnv[:, 8 * half:8 * (half + 1), 128 * tau:128 * (tau + 1)],
                psg[half][:].rearrange("p (h w) -> p h w", h=8))
        if tau == 3:
            wait("gpsimd", "sVE", P.cur("sVE"))
            g_h1 = dma("gpsimd", "sLG",
                       ag_xn_in.ap().rearrange("(h p) t -> p h t", p=128),
                       xnownT[:].rearrange("p (h t) -> p h t", h=NH)[:, :, 0:512])
            wait("gpsimd", "sLG", g_h1)
            P.emit("gpsimd", lambda e: e.collective_compute(
                "AllGather", ALU.bypass, replica_groups=[list(range(NCORE))],
                ins=[ag_xn_in.ap().opt()],
                outs=[ag_xn_out.ap().opt()]).then_inc(SEM["sCXN"]))
    v_ln_done = P.cur("sVE")

    wait("gpsimd", "sVE", v_ln_done)
    g_xn = dma("gpsimd", "sLG",
               ag_xn_in2.ap().rearrange("(h p) t -> p h t", p=128),
               xnownT[:].rearrange("p (h t) -> p h t", h=NH)[:, :, 512:1024])
    wait("gpsimd", "sLG", g_xn)
    P.emit("gpsimd", lambda e: e.collective_compute(
        "AllGather", ALU.bypass, replica_groups=[list(range(NCORE))],
        ins=[ag_xn_in2.ap().opt()], outs=[ag_xn_out2.ap().opt()]).then_inc(SEM["sCX2"]))
    if taps:
        dma("gpsimd", "sLG", tap_ap["t_xnT"], xnownT[:])
    xn_contrib_mark = P.cur("sLG")

    for cm in (cm_xn_t, cm_ln_st, cm_x2s, cm_ln_x, cm_xnownT):
        sb_free(cm)

    qT, cm_qT = sb("qT", [128, 2 * S], BF16)
    kT, cm_kT = sb("kT", [128, 2 * S], BF16)
    vT, cm_vT = sb("vT", [128, 2 * S], BF16)
    aoT, cm_aoT = sb("aoT", [128, 2 * S], BF16)
    ptile, cm_ptile = sb("ptile", [128, 2 * 1024], BF16)
    idx_i, cm_idx_i = sb("idx_i", [128, 1024], I32)
    idx_i2, cm_idx_i2 = sb("idx_i2", [128, 1024], I32)
    selw, cm_selw = sb("selw", [128, 1024], BF16)
    selt, cm_selt = sb("selt", [128, 1024], BF16)
    lntmp, cm_lntmp = sb("lntmp", [128, 512], F32)
    rcpr, cm_rcpr = sb("rcpr", [128, 512], F32)
    w6, cm_w6 = sb("w6", [128, 4 * H], BF16)
    wv2, cm_wv2 = sb("wv2", [128, 2 * H], BF16)
    wcv0, cm_wcv0 = sb("wcv0", [128, H], F32)
    wcv1, cm_wcv1 = sb("wcv1", [128, H], F32)

    wcv = (wcv0, wcv1)
    cast_marks = {}
    for j in range(6):
        if j >= 2:
            wait("scalar", "sAC", cast_marks[j - 2])
        ldv = dma("scalar", "sLwo",
                  wcv[j % 2][:].rearrange("p (h w) -> p h w", h=NH),
                  ap["wqkv"][:, 128 * j:128 * (j + 1)].rearrange("(h p) w -> p h w", p=128))
        wait("scalar", "sLwo", ldv)
        dst = w6[:, H * j:H * (j + 1)] if j < 4 else wv2[:, H * (j - 4):H * (j - 3)]
        cast_marks[j] = act(dst, wcv[j % 2][:], AF.Identity, scale=1.0)
    a_w_ready = cast_marks[5]
    sb_free(cm_wcv1)
    sb_free(cm_wcv0)

    xnstr, cm_xnstr = sb("xnstr", [128, 2 * 8192], BF16)

    sel_state = {"n": 0, "cp": {}, "last_contrib": 0, "pend": None}

    def flush_contrib():
        if sel_state["pend"] is not None:
            pdst, pcols, pmark = sel_state["pend"]
            wait("gpsimd", "sVE", pmark)
            sel_state["last_contrib"] = dma("gpsimd", "sLcon", pdst,
                                            selw[:, 0:pcols])
            sel_state["pend"] = None

    def emit_select_unit(src_ap, cols, dst_ap):
        ku = sel_state["n"]
        sel_state["n"] = ku + 1
        ib = (idx_i, idx_i2)[ku % 2]
        isem = ("sLsel", "sLwl")[ku % 2]
        gate = sel_state["cp"].get(ku - 2, v_lutrep if ku == 0 else 0)
        if gate:
            wait("gpsimd", "sVE", gate)
        ldv = dma("gpsimd", isem, ib[:, 0:cols], src_ap)
        flush_contrib()
        wait("vector", isem, ldv)
        if sel_state["last_contrib"]:
            wait("vector", "sLcon", sel_state["last_contrib"])
        last = 0
        for k in range(LUTN):
            if k == 0:
                vop("tensor_scalar", selw[:, 0:cols], ib[:, 0:cols],
                    0.0, tbl["lutrep"][:, 0:1], ALU.is_equal, ALU.mult)
            else:
                vop("tensor_scalar", selt[:, 0:cols], ib[:, 0:cols],
                    float(k), tbl["lutrep"][:, k:k + 1], ALU.is_equal, ALU.mult)
                last = vop("tensor_tensor", selw[:, 0:cols], selw[:, 0:cols],
                           selt[:, 0:cols], ALU.add)
        sel_state["cp"][ku] = last
        sel_state["pend"] = (dst_ap, cols, last)

    select_units = []
    for mat_name, col_off in (("gidxT", 0), ("uidxT", 768)):
        for hp in range(16):
            select_units.append((
                ap[mat_name][128 * hp:128 * (hp + 1), :], 768,
                ag_gu_in.ap()[128 * hp:128 * (hp + 1), col_off:col_off + 768]))
    for it6 in range(6):
        for half in range(2):
            select_units.append((
                ap["didxT"][128 * it6:128 * (it6 + 1), 1024 * half:1024 * (half + 1)],
                1024,
                ag_wd_in.ap()[128 * it6:128 * (it6 + 1), 1024 * half:1024 * (half + 1)]))
    sel_idx = [0]
    cc_emitted = {"gu": False, "wd": False}

    def emit_gu_cc():
        wait("gpsimd", "sLcon", P.cur("sLcon"))
        P.emit("gpsimd", lambda e: e.collective_compute(
            "AllGather", ALU.bypass, replica_groups=[list(range(NCORE))],
            ins=[ag_gu_in.ap().opt()], outs=[ag_gu_out.ap().opt()]).then_inc(SEM["sCGU"]))

    def emit_wd_cc():
        wait("gpsimd", "sLcon", P.cur("sLcon"))
        P.emit("gpsimd", lambda e: e.collective_compute(
            "AllGather", ALU.bypass, replica_groups=[list(range(NCORE))],
            ins=[ag_wd_in.ap().opt()], outs=[ag_wd_out.ap().opt()]).then_inc(SEM["sCWD"]))
    def pop_select(n):
        for _ in range(n):
            if sel_idx[0] < len(select_units):
                emit_select_unit(*select_units[sel_idx[0]])
                sel_idx[0] += 1
                if sel_idx[0] == 32 and not cc_emitted["gu"]:
                    cc_emitted["gu"] = True
                    flush_contrib()
                    emit_gu_cc()
                if sel_idx[0] == len(select_units):
                    flush_contrib()

    pop_select(9)

    wait("tensor", "sCXN", 1)
    wait("sync", "sCXN", 1)
    wait("sync", "sLG", xn_contrib_mark)
    wait("sync", "sAC", a_w_ready)
    wait("sync", "sLG", P.cur("sLG"))
    wait("scalar", "sLG", P.cur("sLG"))
    wait("tensor", "sVE", v_ln_done)
    wait("tensor", "sAC", P.cur("sAC"))

    qk_evac = {}
    v_evac = {}
    stream_last_mm = {}
    tt_order = [2 * r for r in range(8)] + [2 * r + 1 for r in range(8)]
    for kk, tt in enumerate(tt_order):
        buf = kk % 2
        xc = buf * 8192
        rank = tt // 2
        src_t = ag_xn_out if tt % 2 == 0 else ag_xn_out2
        if kk == 8:
            wait("sync", "sCX2", 1)
            wait("tensor", "sCX2", 1)
        if kk >= 2:
            wait("sync", "sMM", stream_last_mm[kk - 2])
        ldv = dma("sync", "sLx0" if buf == 0 else "sLx1",
                  xnstr[:, xc:xc + 8192].rearrange("p (h t) -> p h t", h=NH),
                  src_t.ap()[rank * H:(rank + 1) * H, 0:512]
                  .rearrange("(h p) t -> p h t", p=128))
        wait("tensor", "sLx0" if buf == 0 else "sLx1", ldv)
        for j in range(4):
            bank = psg[j // 2][:, 512 * (j % 2):512 * (j % 2) + 512]
            if (j, kk - 1) in qk_evac:
                wait("tensor", "sAC", qk_evac[(j, kk - 1)])
            for hh in range(NH):
                mm(bank, w6[:, H * j + 128 * hh: H * j + 128 * (hh + 1)],
                   xnstr[:, xc + 512 * hh: xc + 512 * (hh + 1)],
                   start=(hh == 0), stop=(hh == NH - 1), inc=(hh == NH - 1))
            m_qk = P.cur("sMM")
            wait("scalar", "sMM", m_qk)
            dst = qT if j < 2 else kT
            slot = j % 2
            scl = SCALE if j < 2 else 1.0
            qk_evac[(j, kk)] = act(
                dst[:, slot * S + 512 * tt: slot * S + 512 * (tt + 1)],
                bank, AF.Identity, bias=tbl["bq4_s"][:, j:j + 1], scale=scl)
        for sub in range(4):
            bankA = pv2[sub % 2]
            bankB = db2[sub % 2]
            key = (sub % 2, 4 * kk + sub)
            prev = (sub % 2, 4 * kk + sub - 2)
            if prev in v_evac:
                wait("tensor", "sAC", v_evac[prev])
            for hh in range(NH):
                for slot, bk in ((0, bankA), (1, bankB)):
                    mm(bk[:, 0:128],
                       xnstr[:, xc + 512 * hh + 128 * sub: xc + 512 * hh + 128 * (sub + 1)],
                       wv2[:, H * slot + 128 * hh: H * slot + 128 * (hh + 1)],
                       start=(hh == 0), stop=False)
            for slot, bk in ((0, bankA), (1, bankB)):
                m_v = mm(bk[:, 0:128], tbl["ones1f"][:],
                         tbl["bv2f"][:, 128 * slot:128 * (slot + 1)],
                         start=False, stop=True, inc=(slot == 1))
            stream_last_mm[kk] = m_v
            wait("scalar", "sMM", m_v)
            tg = 4 * tt + sub
            act(vT[:, 0 * S + 128 * tg: 0 * S + 128 * (tg + 1)],
                bankA[:, 0:128], AF.Identity, scale=1.0)
            v_evac[key] = act(vT[:, S + 128 * tg: S + 128 * (tg + 1)],
                              bankB[:, 0:128], AF.Identity, scale=1.0)
        if kk % 2 == 1:
            pop_select(1)

    sb_free(cm_xnstr)
    sb_free(cm_wv2)
    sb_free(cm_w6)

    b4j_s, cm_b4j = sb("b4j_s", [128, S], BF16)
    b4i_s, cm_b4i = sb("b4i_s", [128, S], BF16)
    masks, cm_masks = sb("masks", [128, 2048], BF16)
    wait("sync", "sMM", P.cur("sMM"))
    wait("sync", "sAC", P.cur("sAC"))
    dma("sync", "sLtab", b4j_s[:], ap["b4j"][:, 0:S])
    dma("sync", "sLtab", b4i_s[:], ap["b4i"][:, 0:S])
    ld_b4 = dma("sync", "sLtab", masks[:], ap["maskneg"])
    wait("tensor", "sLtab", ld_b4)
    wait("vector", "sLtab", ld_b4)
    wait("tensor", "sAC", P.cur("sAC"))

    exp_in_bank = {0: 0, 1: 0}
    pt_mark = {0: 0, 1: 0}
    pvdb_guard = {}

    for g in range(32):
        slot = g // 16
        s16 = g % 16
        so = slot * S
        i0 = 512 * s16
        nch = 4 * s16 + 4
        jc0 = max(0, (i0 - 2048) // 128) if slot == 0 else 0
        jc0 -= jc0 % 2
        ngrp_all = nch // 2
        pvb = pv2[g % 2]
        db = db2[g % 2]
        if g == 16:
            wait("gpsimd", "sVE", pvdb_guard[15][0])
            g_c1 = dma("gpsimd", "sLG", ag_at_in1.ap(), aoT[:, 0:S])
            wait("gpsimd", "sLG", g_c1)
            P.emit("gpsimd", lambda e: e.collective_compute(
                "AllGather", ALU.bypass, replica_groups=[list(range(NCORE))],
                ins=[ag_at_in1.ap().opt()],
                outs=[ag_at_out1.ap().opt()]).then_inc(SEM["sCAT"]))
            last_mm_slot0 = P.cur("sMM")
            wait("sync", "sMM", last_mm_slot0)
            dma("sync", "sLtab", b4j_s[0:4, :], ap["b4j"][0:4, S:2 * S])
            ld_sl = dma("sync", "sLtab", b4i_s[0:4, :], ap["b4i"][0:4, S:2 * S])
            wait("tensor", "sLtab", ld_sl)
        if (g - 2) in pvdb_guard:
            v_n2, a_ln2 = pvdb_guard[g - 2]
            wait("tensor", "sVE", v_n2)
            wait("tensor", "sAC", a_ln2)
        den_marks = {}
        exp_marks = {}

        def emit_pvden(grp):
            wait("tensor", "sAC", exp_marks[grp])
            for half in range(2):
                jc = 2 * grp + half
                pt = ptile[:, 1024 * (grp % 2) + 512 * half:
                           1024 * (grp % 2) + 512 * (half + 1)]
                mm(pvb[:], vT[:, so + 128 * jc: so + 128 * (jc + 1)], pt,
                   start=(jc == jc0), stop=(jc == nch - 1))
                den_marks[jc] = mm(db[:], tbl["onesj"][:], pt,
                                   start=(jc == jc0), stop=(jc == nch - 1), inc=True)
            pt_mark[grp % 2] = den_marks[2 * grp + 1]

        g0 = jc0 // 2
        for grp in range(g0, ngrp_all):
            sgb = psg[grp % 2]
            if exp_in_bank[grp % 2]:
                wait("tensor", "sAC", exp_in_bank[grp % 2])
            m_b = 0
            for half in range(2):
                jc = 2 * grp + half
                sl = sgb[:, 512 * half:512 * half + 512]
                mm(sl, kT[:, so + 128 * jc: so + 128 * (jc + 1)],
                   qT[:, so + i0: so + i0 + 512], start=True, stop=False)
                m_b = mm(sl, b4j_s[:, 128 * jc: 128 * (jc + 1)],
                         b4i_s[:, i0: i0 + 512],
                         start=False, stop=True, inc=True)
            if grp >= ngrp_all - 2:
                dd = grp - (ngrp_all - 2)
                wait("vector", "sMM", m_b)
                v_m = vop("tensor_tensor", sgb[:], sgb[:],
                          masks[:, 1024 * dd:1024 * (dd + 1)], ALU.add)
                wait("scalar", "sVE", v_m)
            wait("scalar", "sMM", m_b)
            if pt_mark[grp % 2]:
                wait("scalar", "sMM", pt_mark[grp % 2])
            e_m = act(ptile[:, 1024 * (grp % 2):1024 * (grp % 2 + 1)], sgb[:],
                      AF.Exp)
            exp_in_bank[grp % 2] = e_m
            exp_marks[grp] = e_m
            if grp - ATT_LAG >= g0:
                emit_pvden(grp - ATT_LAG)
        for gtail in range(max(g0, ngrp_all - ATT_LAG), ngrp_all):
            emit_pvden(gtail)
        if not (slot == 1 and s16 < 6):
            pop_select(1)
        m_last_den = den_marks[nch - 1]
        wait("scalar", "sMM", m_last_den)
        act(lntmp[:], db[:], AF.Ln)
        a_ln = act(rcpr[:], lntmp[:], AF.Exp, scale=-1.0)
        wait("vector", "sAC", a_ln)
        v_n = vop("tensor_tensor", aoT[:, so + i0: so + i0 + 512],
                  pvb[:], rcpr[:], ALU.mult)
        pvdb_guard[g] = (v_n, a_ln)
    pop_select(len(select_units))

    att_done = P.cur("sVE")
    if taps:
        wait("gpsimd", "sVE", att_done)
        dma("gpsimd", "sLG", tap_ap["t_qT"], qT[:])
        dma("gpsimd", "sLG", tap_ap["t_kT"], kT[:])
        dma("gpsimd", "sLG", tap_ap["t_vT"], vT[:])
        dma("gpsimd", "sLG", tap_ap["t_aoT"], aoT[:])
    wait("gpsimd", "sVE", att_done)
    g_ctr2 = dma("gpsimd", "sLG", ag_at_in2.ap(), aoT[:, S:2 * S])

    for cm in (cm_masks, cm_b4i, cm_b4j, cm_rcpr, cm_lntmp, cm_selt, cm_selw,
               cm_idx_i2, cm_idx_i, cm_ptile, cm_aoT, cm_vT, cm_kT,
               cm_qT):
        sb_free(cm)

    h_T, cm_hT = sb("h_T", [128, NIT * TOK], BF16)
    ao_own, cm_ao_own = sb("ao_own", [128, NH * TOK], BF16)
    wgu_cms = []
    wgu = []
    for wn in ("wgu0", "wgu1"):
        t, cmw = sb(wn, [128, 2 * H], BF16)
        wgu.append(t)
        wgu_cms.append(cmw)
    sg, cm_sg = sb("sg", [128, 512], BF16)
    xnown2, cm_xnown2 = sb("xnown2", [128, NH * TOK], BF16)

    for s_name in ("sMM", "sVE", "sAC", "sLcon", "sLsel", "sLwl", "sLtab"):
        wait("sync", s_name, P.cur(s_name))
    wait("sync", "sLG", g_ctr2)
    if taps:
        for e2 in ("sync", "vector", "scalar"):
            wait(e2, "sLG", P.cur("sLG"))
    dma("sync", "sLtab",
        xnown2[:].rearrange("p (h t) -> p h t", h=NH)[:, :, 0:512],
        ag_xn_in.ap().rearrange("(h p) t -> p h t", p=128))
    ld_xn2 = dma("sync", "sLtab",
                 xnown2[:].rearrange("p (h t) -> p h t", h=NH)[:, :, 512:1024],
                 ag_xn_in2.ap().rearrange("(h p) t -> p h t", p=128))
    wait("tensor", "sCGU", 1)
    wait("sync", "sCGU", 1)
    wait("tensor", "sLtab", ld_xn2)
    wait("tensor", "sAC", P.cur("sAC"))
    wait("tensor", "sVE", P.cur("sVE"))

    gu_last_mm = {}
    silu_marks = {}
    htt_marks = {}
    for ti in range(NIT):
        buf = wgu[ti % 2]
        rank = ti % NCORE
        loc = ti // NCORE
        gs = "sLw0" if ti % 2 == 0 else "sLw1"
        if ti >= 2:
            wait("sync", "sMM", gu_last_mm[ti - 2])
        dma("sync", gs, buf[:, 0:H].rearrange("p (h w) -> p h w", h=NH),
            ag_gu_out.ap()[rank * H:(rank + 1) * H, 128 * loc:128 * (loc + 1)]
            .rearrange("(h p) w -> p h w", p=128))
        ldv = dma("sync", gs, buf[:, H:2 * H].rearrange("p (h w) -> p h w", h=NH),
                  ag_gu_out.ap()[rank * H:(rank + 1) * H, 768 + 128 * loc:768 + 128 * (loc + 1)]
                  .rearrange("(h p) w -> p h w", p=128))
        if ti == 4:
            wait("gpsimd", gs, ldv)
            P.emit("gpsimd", lambda e: e.collective_compute(
                "AllGather", ALU.bypass, replica_groups=[list(range(NCORE))],
                ins=[ag_at_in2.ap().opt()],
                outs=[ag_at_out2.ap().opt()]).then_inc(SEM["sCAT"]))
        elif ti == 16:
            wait("gpsimd", gs, ldv)
            emit_wd_cc()
        wait("tensor", gs, ldv)
        for t2 in range(2):
            if (ti - 1, t2) in silu_marks:
                wait("tensor", "sAC", silu_marks[(ti - 1, t2)])
            if (ti - 1, t2) in htt_marks:
                wait("tensor", "sVE", htt_marks[(ti - 1, t2)])
        gm = {}
        for hh in range(NH):
            for t2 in range(2):
                gm[t2] = mm(psg[0][:, 512 * t2:512 * (t2 + 1)],
                            buf[:, 128 * hh:128 * (hh + 1)],
                            xnown2[:, hh * TOK + 512 * t2: hh * TOK + 512 * (t2 + 1)],
                            start=(hh == 0), stop=(hh == NH - 1),
                            inc=(hh == NH - 1))
        for hh in range(NH):
            for t2 in range(2):
                gu_last_mm[ti] = mm(
                    psg[1][:, 512 * t2:512 * (t2 + 1)],
                    buf[:, H + 128 * hh:H + 128 * (hh + 1)],
                    xnown2[:, hh * TOK + 512 * t2: hh * TOK + 512 * (t2 + 1)],
                    start=(hh == 0), stop=(hh == NH - 1), inc=(hh == NH - 1))
        m_u = gu_last_mm[ti]
        for t2 in range(2):
            gbank = psg[0][:, 512 * t2:512 * (t2 + 1)]
            ubank = psg[1][:, 512 * t2:512 * (t2 + 1)]
            wait("scalar", "sMM", gm[t2])
            if (ti, t2 - 1) in htt_marks:
                wait("scalar", "sVE", htt_marks[(ti, t2 - 1)])
            elif t2 == 0 and (ti - 1, 1) in htt_marks:
                wait("scalar", "sVE", htt_marks[(ti - 1, 1)])
            a_s = act(sg[:], gbank, AF.Silu)
            silu_marks[(ti, t2)] = a_s
            wait("vector", "sMM", m_u)
            wait("vector", "sAC", a_s)
            htt_marks[(ti, t2)] = vop(
                "tensor_tensor", h_T[:, ti * TOK + 512 * t2: ti * TOK + 512 * (t2 + 1)],
                sg[:], ubank, ALU.mult)
    if taps:
        wait("gpsimd", "sVE", P.cur("sVE"))
        dma("gpsimd", "sLG", tap_ap["t_hT"], h_T[:])

    wait("gpsimd", "sCAT", 2)

    def _ao_own_load(e):
        with e.register("rc") as rc, e.register("ro") as ro:
            e.reg_load(rc, ap["cid"][0:1, 0:1])
            e.reg_mul(ro, rc, TOK)
            for d in range(NH):
                src = (ag_at_out1 if d % 2 == 0 else ag_at_out2).ap()
                r = d // 2
                e.dma_start(
                    out=ao_own[:, d * TOK:(d + 1) * TOK],
                    in_=src[128 * r:128 * (r + 1),
                            bass.ds(bass.RuntimeValue(ro), TOK)],
                ).then_inc(SEM["sLG"], 16)
    P.bump("sLG", 16 * NH)
    P.emit("gpsimd", _ao_own_load)
    g_ao = P.cur("sLG")
    if taps:
        wait("gpsimd", "sLG", g_ao)
        dma("gpsimd", "sLG", tap_ap["t_ao_own"], ao_own[:])

    sb_free(cm_xnown2)
    sb_free(cm_sg)
    sb_free(wgu_cms[1])
    sb_free(wgu_cms[0])

    wlhs0, _ = sb("wlhs0", [128, (16 + 48) * 128], BF16)
    wlhs1, _ = sb("wlhs1", [128, (16 + 48) * 128], BF16)
    ostage0, _ = sb("ostage0", [128, 512], F32)
    ostage1, _ = sb("ostage1", [128, 512], F32)
    outasm, _ = sb("outasm", [128, 4 * H], F32)
    xres, _ = sb("xres", [128, H], F32)
    wst0, _ = sb("wst0", [128, H], F32)

    wait("sync", "sCWD", 1)
    wait("tensor", "sCWD", 1)
    wait("tensor", "sLG", g_ao)
    barrier_waits("tensor")
    barrier_waits("sync")
    barrier_waits("vector")
    barrier_waits("scalar")

    evac_marks = {}
    tr_marks = {}
    cp_marks = {}
    wo_cast = {}
    wl_last = {0: 0, 1: 0}
    store_marks = []
    hp_seq = [(a2, hp) for a2 in range(2) for hp in range(16)]
    for idx, (a2, hp) in enumerate(hp_seq):
        wl = (wlhs0, wlhs1)[idx % 2]
        osg = (ostage0, ostage1)[idx % 2]
        mainbank = psg[idx % 2][:, 0:512]
        if a2 == 1 and hp == 0 and store_marks:
            wait("vector", "sLS", store_marks[-1])
        wls = "sLw0" if idx % 2 == 0 else "sLw1"
        if wl_last[idx % 2]:
            wait("sync", "sMM", wl_last[idx % 2])
        if idx >= 1:
            wait("sync", "sVE", wo_cast[idx - 1])
        ld_wo = dma("sync", wls, wst0[:].rearrange("p (d w) -> p d w", d=NH),
                    ap["wo"][:, 128 * hp:128 * (hp + 1)].rearrange("(d p) w -> p d w", p=128))
        wd_v = ag_wd_out.ap().rearrange("(e q) w -> q e w", e=NCORE)
        for l6 in range(6):
            ldv = dma("sync", wls,
                      wl[:, H + 1024 * l6:H + 1024 * (l6 + 1)]
                      .rearrange("p (ee w) -> p ee w", ee=8),
                      wd_v[128 * l6:128 * (l6 + 1), :, 128 * hp:128 * (hp + 1)])
        wait("vector", wls, ldv)
        if wl_last[idx % 2]:
            wait("vector", "sMM", wl_last[idx % 2])
        wo_cast[idx] = vop("tensor_copy", wl[:, 0:H], wst0[:])
        wait("tensor", wls, ldv)
        wait("tensor", "sVE", wo_cast[idx])
        if (idx - 2) in evac_marks:
            wait("tensor", "sAC", evac_marks[idx - 2])
        for d in range(NH):
            mm(mainbank, wl[:, 128 * d:128 * (d + 1)],
               ao_own[:, d * TOK + 512 * a2: d * TOK + 512 * (a2 + 1)],
               start=(d == 0), stop=False)
        for ti in range(NIT):
            mm(mainbank, wl[:, H + 128 * ti:H + 128 * (ti + 1)],
               h_T[:, ti * TOK + 512 * a2: ti * TOK + 512 * (a2 + 1)],
               start=False, stop=(ti == NIT - 1), inc=(ti == NIT - 1))
        m_e = P.cur("sMM")
        wl_last[idx % 2] = m_e
        wait("scalar", "sMM", m_e)
        if (idx - 2) in tr_marks:
            wait("scalar", "sMM", tr_marks[idx - 2])
        a_e = act(osg[:], mainbank, AF.Identity,
                  bias=tbl["bo_s"][:, hp:hp + 1], scale=1.0)
        evac_marks[idx] = a_e
        wait("tensor", "sAC", a_e)
        for t4 in range(4):
            tbank = pv2[t4 % 2]
            key = (idx, t4)
            prevkey = (idx, t4 - 2) if t4 >= 2 else (idx - 1, t4 + 2)
            if prevkey in cp_marks:
                wait("tensor", "sVE", cp_marks[prevkey])
            m_t = trans(tbank[:, 0:128], osg[:, 128 * t4:128 * (t4 + 1)],
                        tbl["identf_s"][:], inc=True)
            if t4 == 3:
                tr_marks[idx] = m_t
            wait("vector", "sMM", m_t)
            cp_marks[key] = vop(
                "tensor_copy", outasm[:, t4 * H + 128 * hp: t4 * H + 128 * (hp + 1)],
                tbank[:, 0:128])
        if hp == 15:
            v_add_prev = 0
            for t4 in range(4):
                if v_add_prev:
                    wait("sync", "sVE", v_add_prev)
                ldx = dma("sync", "sLxr", xres[:],
                          ap["x_own"][512 * a2 + 128 * t4: 512 * a2 + 128 * (t4 + 1), :])
                wait("vector", "sLxr", ldx)
                v_add_prev = vop("tensor_tensor", outasm[:, t4 * H:(t4 + 1) * H],
                                 outasm[:, t4 * H:(t4 + 1) * H], xres[:], ALU.add)
                wait("scalar", "sVE", v_add_prev)
                st = dma("scalar", "sLS",
                         out_ap[512 * a2 + 128 * t4: 512 * a2 + 128 * (t4 + 1), :],
                         outasm[:, t4 * H:(t4 + 1) * H])
                store_marks.append(st)

    wait("scalar", "sLS", store_marks[-1])
    wait("gpsimd", "sLS", store_marks[-1])

    with nc.Block() as block:
        @block.sync
        def _(e):
            for f in P.streams["sync"]:
                f(e)

        @block.tensor
        def _(e):
            for f in P.streams["tensor"]:
                f(e)

        @block.vector
        def _(e):
            for f in P.streams["vector"]:
                f(e)

        @block.scalar
        def _(e):
            for f in P.streams["scalar"]:
                f(e)

        @block.gpsimd
        def _(e):
            for f in P.streams["gpsimd"]:
                f(e)

    for cm in reversed(list(open_cms)):
        sb_free(cm)
    for cm in list(reversed(ps_cms)) + list(reversed(sem_cms)):
        cm.__exit__(None, None, None)
    return nc


def _host_inputs(x, Wqkv, bqkv, Wo, bo, lut, gidx, uidx, didx):
    x2d = np.ascontiguousarray(x.reshape(S, H))
    slopes = [2.0 ** (-(k) / 2.0) for k in range(NH)]
    jj = np.arange(128)[:, None]
    icq = np.arange(512)[None, :]
    maskneg = np.zeros((128, 2048), BF)
    for dd in range(2):
        for hl in range(2):
            d = 2 * dd + hl
            maskneg[:, 1024 * dd + 512 * hl:1024 * dd + 512 * (hl + 1)] = \
                np.where(128 * d + jj <= icq, 0.0, NEG).astype(BF)
    identf = np.eye(128, dtype=np.float32)
    pos = np.arange(S, dtype=np.float64)

    perm = np.empty(H, np.int64)
    for r in range(NCORE):
        for s2 in range(2):
            h = r + 8 * s2
            perm[256 * r + 128 * s2:256 * r + 128 * (s2 + 1)] = \
                np.arange(128 * h, 128 * (h + 1))
    wo_perm = np.ascontiguousarray(Wo[perm, :])

    in_maps = []
    for c in range(NCORE):
        heads = (c, 8 + c)
        cols = []
        for base in (0, 2048, 4096):
            for h in heads:
                cols.append((base + 128 * h, base + 128 * (h + 1)))
        wqkv_c = np.ascontiguousarray(
            np.concatenate([Wqkv[:, a:b] for a, b in cols], axis=1))
        bq4 = np.ascontiguousarray(
            np.stack([bqkv[a:b] for a, b in cols[:4]], axis=1))
        bv2 = np.concatenate([bqkv[a:b] for a, b in cols[4:]])[None, :]
        tiles = [c + 8 * k for k in range(6)]

        def _rows(idx_mat, t):
            if t < NIT:
                return idx_mat[128 * t:128 * (t + 1)]
            return np.zeros((128, idx_mat.shape[1]), np.int32)
        gidxT = np.ascontiguousarray(
            np.concatenate([_rows(gidx, t) for t in tiles]).T.astype(np.int32))
        uidxT = np.ascontiguousarray(
            np.concatenate([_rows(uidx, t) for t in tiles]).T.astype(np.int32))

        def _cols(idx_mat, t):
            if t < NIT:
                return idx_mat[:, 128 * t:128 * (t + 1)]
            return np.zeros((idx_mat.shape[0], 128), np.int32)
        didxT = np.ascontiguousarray(
            np.concatenate([_cols(didx, t) for t in tiles], axis=1).T.astype(np.int32))
        b4j = np.zeros((128, 2 * S), BF)
        b4i = np.zeros((128, 2 * S), BF)
        b4j[2, :] = 1.0
        b4j[3, :] = 1.0
        b4i[0, :] = 1.0
        b4i[1, :] = 1.0
        for sl, h in enumerate(heads):
            v = slopes[h] * pos
            hi = v.astype(np.float32).astype(BF)
            lo = (v.astype(np.float32) - hi.astype(np.float32)).astype(BF)
            b4j[0, sl * S:(sl + 1) * S] = hi
            b4j[1, sl * S:(sl + 1) * S] = lo
            b4i[2, sl * S:(sl + 1) * S] = -hi
            b4i[3, sl * S:(sl + 1) * S] = -lo
        in_maps.append({
            "cid": np.array([[c]], np.int32),
            "x_own": np.ascontiguousarray(x2d[TOK * c:TOK * (c + 1)]),
            "wqkv": wqkv_c,
            "bq4": bq4.astype(np.float32),
            "bv2": bv2.astype(np.float32),
            "wo": wo_perm,
            "bo_t": np.ascontiguousarray(bo.reshape(16, 128).T),
            "lut": lut.reshape(1, LUTN).astype(np.float32),
            "gidxT": gidxT, "uidxT": uidxT, "didxT": didxT,
            "b4j": b4j, "b4i": b4i,
            "maskneg": maskneg, "identf": identf,
        })
    return in_maps


def kernel(L0_attn_x, L0_attn_Wqkv, L0_attn_bqkv, L0_attn_Wo, L0_attn_bo,
           L0_ffn_lut, L0_ffn_gate_idx, L0_ffn_up_idx, L0_ffn_down_idx):
    in_maps = _host_inputs(
        np.asarray(L0_attn_x, np.float32), np.asarray(L0_attn_Wqkv, np.float32),
        np.asarray(L0_attn_bqkv, np.float32), np.asarray(L0_attn_Wo, np.float32),
        np.asarray(L0_attn_bo, np.float32), np.asarray(L0_ffn_lut, np.float32),
        np.asarray(L0_ffn_gate_idx, np.int32), np.asarray(L0_ffn_up_idx, np.int32),
        np.asarray(L0_ffn_down_idx, np.int32))
    nc = build_nc(taps=False)
    res = run_bass_kernel_spmd(nc, in_maps, core_ids=list(range(NCORE)))
    out = np.concatenate([np.asarray(res.results[c]["out"]) for c in range(NCORE)],
                         axis=0)
    return out.reshape(1, S, H).astype(np.float32)


# version 21
# speedup vs baseline: 1.1009x; 1.0010x over previous
import numpy as np
import ml_dtypes
import concourse.bass as bass
import concourse.mybir as mybir
from concourse.bass_utils import run_bass_kernel_spmd

F32 = mybir.dt.float32
BF16 = mybir.dt.bfloat16
I32 = mybir.dt.int32
AF = mybir.ActivationFunctionType
ALU = mybir.AluOpType
AX = mybir.AxisListType

NCORE = 8
S = 8192
TOK = 1024
H = 2048
NH = 16
DH = 128
NIT = 43
LUTN = 16
SCALE = float(1.0 / np.sqrt(128.0))
NEG = -1.0e30
BF = ml_dtypes.bfloat16
ATT_LAG = 1


class Plan:
    def __init__(self):
        self.streams = {"sync": [], "tensor": [], "vector": [], "scalar": [], "gpsimd": []}
        self.cnt = {}

    def emit(self, eng, fn):
        self.streams[eng].append(fn)

    def bump(self, s_name, n):
        self.cnt[s_name] = self.cnt.get(s_name, 0) + n
        return self.cnt[s_name]

    def cur(self, s_name):
        return self.cnt.get(s_name, 0)


def build_nc(taps=False):
    nc = bass.Bass("TRN2", target_bir_lowering=False, debug=False)

    ap = {}
    for name, shape, dt in [
        ("cid", [1, 1], I32),
        ("x_own", [TOK, H], F32),
        ("wqkv", [H, 768], F32),
        ("bq4", [128, 4], F32),
        ("bv2", [1, 256], F32),
        ("wo", [H, H], F32),
        ("bo_t", [128, 16], F32),
        ("lut", [1, LUTN], F32),
        ("gidxT", [H, 768], I32),
        ("uidxT", [H, 768], I32),
        ("didxT", [768, H], I32),
        ("b4j", [128, 2 * S], BF16),
        ("b4i", [128, 2 * S], BF16),
        ("maskneg", [128, 2048], BF16),
        ("identf", [128, 128], F32),
    ]:
        ap[name] = nc.dram_tensor(name, shape, dt, kind="ExternalInput").ap()
    out_ap = nc.dram_tensor("out", [TOK, H], F32, kind="ExternalOutput").ap()
    tap_ap = {}
    if taps:
        for name, shape, dt in [
            ("t_xnT", [128, NH * TOK], BF16),
            ("t_qT", [128, 2 * S], BF16),
            ("t_kT", [128, 2 * S], BF16),
            ("t_vT", [128, 2 * S], BF16),
            ("t_aoT", [128, 2 * S], BF16),
            ("t_hT", [128, NIT * TOK], BF16),
            ("t_ao_own", [128, NH * TOK], BF16),
        ]:
            tap_ap[name] = nc.dram_tensor(name, shape, dt, kind="ExternalOutput").ap()

    ag_xn_in = nc.dram_tensor("ag_xn_in", [H, 512], BF16)
    ag_xn_in2 = nc.dram_tensor("ag_xn_in2", [H, 512], BF16)
    ag_xn_out = nc.dram_tensor("ag_xn_out", [NCORE * H, 512], BF16, addr_space="Shared")
    ag_xn_out2 = nc.dram_tensor("ag_xn_out2", [NCORE * H, 512], BF16, addr_space="Shared")
    ag_gu_in = nc.dram_tensor("ag_gu_in", [H, 2 * 768], BF16)
    ag_gu_out = nc.dram_tensor("ag_gu_out", [NCORE * H, 2 * 768], BF16, addr_space="Shared")
    ag_wd_in = nc.dram_tensor("ag_wd_in", [768, H], BF16)
    ag_wd_out = nc.dram_tensor("ag_wd_out", [NCORE * 768, H], BF16, addr_space="Shared")
    ag_at_in1 = nc.dram_tensor("ag_at_in1", [128, S], BF16)
    ag_at_out1 = nc.dram_tensor("ag_at_out1", [NCORE * 128, S], BF16, addr_space="Shared")
    ag_at_in2 = nc.dram_tensor("ag_at_in2", [128, S], BF16)
    ag_at_out2 = nc.dram_tensor("ag_at_out2", [NCORE * 128, S], BF16, addr_space="Shared")

    P = Plan()
    SEM = {}
    open_cms = []

    def sb(name, shape, dt):
        cm = nc.sbuf_tensor(name, shape, dt)
        t = cm.__enter__()
        open_cms.append(cm)
        return t, cm

    def sb_free(cm):
        cm.__exit__(None, None, None)
        open_cms.remove(cm)

    def dma(eng, s_name, out, in_, **kw):
        v = P.bump(s_name, 16)
        s_h = SEM[s_name]
        P.emit(eng, lambda e, o=out, i=in_, s_h=s_h, kw=kw:
               e.dma_start(out=o, in_=i, **kw).then_inc(s_h, 16))
        return v

    def wait(eng, s_name, val):
        if val <= 0:
            return
        s_h = SEM[s_name]
        P.emit(eng, lambda e, s_h=s_h, v=val: e.wait_ge(s_h, v))

    def mm(out, lhsT, rhs, start, stop, inc=False):
        if inc:
            v = P.bump("sMM", 1)
            s_h = SEM["sMM"]
            P.emit("tensor", lambda e, o=out, l=lhsT, r=rhs, a=start, b=stop, s_h=s_h:
                   e.matmul(o, l, r, start=a, stop=b).then_inc(s_h, 1))
            return v
        P.emit("tensor", lambda e, o=out, l=lhsT, r=rhs, a=start, b=stop:
               e.matmul(o, l, r, start=a, stop=b))

    def trans(out, in_, ident, inc=False):
        if inc:
            v = P.bump("sMM", 1)
            s_h = SEM["sMM"]
            P.emit("tensor", lambda e, o=out, i=in_, d=ident, s_h=s_h:
                   e.transpose(o, i, d).then_inc(s_h, 1))
            return v
        P.emit("tensor", lambda e, o=out, i=in_, d=ident: e.transpose(o, i, d))

    def act(out, in_, func, bias=None, scale=None, accum_out=None):
        kw = {}
        if bias is not None:
            kw["bias"] = bias
        if scale is not None:
            kw["scale"] = scale
        if accum_out is not None:
            kw["accum_out"] = accum_out
        prev = P.cur("sAC")
        v = P.bump("sAC", 1)
        s_h = SEM["sAC"]
        P.emit("scalar", lambda e, o=out, i=in_, f=func, kw=kw, s_h=s_h, pv=prev:
               (e.wait_ge(s_h, pv) if pv else None,
                e.activation(o, i, f, **kw).then_inc(s_h, 1)))
        return v

    def vop(name, *args, **kw):
        prev = P.cur("sVE")
        v = P.bump("sVE", 1)
        s_h = SEM["sVE"]
        P.emit("vector", lambda e, n=name, a=args, kw=kw, s_h=s_h, pv=prev:
               (e.wait_ge(s_h, pv) if pv else None,
                getattr(e, n)(*a, **kw).then_inc(s_h, 1)))
        return v

    DMA_SEMS = ("sLtab", "sLln", "sLsel", "sLcon", "sLwo", "sLx0", "sLx1",
                "sLw0", "sLw1", "sLwl", "sLxr", "sLG", "sLS")

    def barrier_waits(eng):
        for s_name in ("sMM", "sAC", "sVE") + DMA_SEMS:
            wait(eng, s_name, P.cur(s_name))

    sem_cms = []
    for s_name in DMA_SEMS + ("sMM", "sAC", "sVE", "sCXN", "sCX2", "sCGU", "sCWD", "sCAT"):
        cm = nc.semaphore(s_name)
        SEM[s_name] = cm.__enter__()
        sem_cms.append(cm)

    ps_cms = []

    def psum(name, shape):
        cm = nc.psum_tensor(name, shape, F32)
        t = cm.__enter__()
        ps_cms.append(cm)
        return t
    psg = [psum("psg0", [128, 1024]), psum("psg1", [128, 1024])]
    pv2 = [psum("pv0", [128, 512]), psum("pv1", [128, 512])]
    db2 = [psum("db0", [128, 512]), psum("db1", [128, 512])]

    tbl = {}
    for name, shape, dt in [
        ("onesj", [128, 128], BF16),
        ("ones1f", [1, 128], F32),
        ("identf_s", [128, 128], F32),
        ("bo_s", [128, 16], F32), ("bq4_s", [128, 4], F32),
        ("bv2f", [1, 256], F32), ("lutrow", [1, LUTN], F32),
        ("lutrep", [128, LUTN], F32), ("eps", [128, 1], F32),
    ]:
        tbl[name], _ = sb(name, shape, dt)

    xnownT, cm_xnownT = sb("xnownT", [128, NH * TOK], BF16)

    ld0 = 0
    for dst, src in [
        (tbl["identf_s"][:], ap["identf"]), (tbl["bo_s"][:], ap["bo_t"]),
        (tbl["bq4_s"][:], ap["bq4"]), (tbl["bv2f"][:], ap["bv2"]),
        (tbl["lutrow"][:], ap["lut"]),
    ]:
        ld0 = dma("sync", "sLtab", dst, src)

    vop("memset", tbl["eps"][:], 1.0e-5)
    vop("memset", tbl["onesj"][:], 1.0)
    v0 = vop("memset", tbl["ones1f"][:], 1.0)
    wait("vector", "sLtab", ld0)
    vop("tensor_scalar", tbl["bq4_s"][:, 0:2], tbl["bq4_s"][:, 0:2],
        SCALE, None, ALU.mult)
    wait("tensor", "sLtab", ld0)
    wait("tensor", "sVE", v0)
    m_lut = mm(db2[0][:, 0:LUTN], tbl["ones1f"][:], tbl["lutrow"][:], True, True, inc=True)
    wait("vector", "sMM", m_lut)
    v_lutrep = vop("tensor_copy", tbl["lutrep"][:], db2[0][:, 0:LUTN])

    ln_x, cm_ln_x = sb("ln_x", [128, 8 * H], F32)
    x2s, cm_x2s = sb("x2s", [128, H], F32)
    ln_st, cm_ln_st = sb("ln_st", [128, 4 * 8], F32)
    xn_t, cm_xn_t = sb("xn_t", [128, 2 * H], F32)

    ld_a = dma("sync", "sLln", ln_x[:, 0:4 * H].rearrange("p (a w) -> p a w", a=4),
               ap["x_own"][0:512, :].rearrange("(a p) w -> p a w", p=128))
    ld_b = dma("sync", "sLx1", ln_x[:, 4 * H:8 * H].rearrange("p (a w) -> p a w", a=4),
               ap["x_own"][512:1024, :].rearrange("(a p) w -> p a w", p=128))
    ln_tr_last = {}
    ln_cp = {}
    xnv = xnownT[:].rearrange("p (h t) -> p h t", h=NH)
    for tau in range(8):
        if tau == 0:
            wait("scalar", "sLln", ld_a)
        elif tau == 4:
            wait("scalar", "sLx1", ld_b)
        xt = ln_x[:, H * tau:H * (tau + 1)]
        act(x2s[:], xt, AF.Identity, scale=1.0,
            accum_out=ln_st[:, 4 * tau:4 * tau + 1])
        a_st = act(x2s[:], xt, AF.Square,
                   accum_out=ln_st[:, 4 * tau + 1:4 * tau + 2])
        wait("vector", "sAC", a_st)
        vop("tensor_scalar", ln_st[:, 4 * tau:4 * tau + 1],
            ln_st[:, 4 * tau:4 * tau + 1], 1.0 / H, None, ALU.mult)
        vop("tensor_scalar", ln_st[:, 4 * tau + 1:4 * tau + 2],
            ln_st[:, 4 * tau + 1:4 * tau + 2], 1.0 / H, None, ALU.mult)
        vop("tensor_tensor", ln_st[:, 4 * tau + 2:4 * tau + 3],
            ln_st[:, 4 * tau:4 * tau + 1], ln_st[:, 4 * tau:4 * tau + 1], ALU.mult)
        v_var = vop("tensor_tensor", ln_st[:, 4 * tau + 2:4 * tau + 3],
                    ln_st[:, 4 * tau + 1:4 * tau + 2],
                    ln_st[:, 4 * tau + 2:4 * tau + 3], ALU.subtract)
        wait("scalar", "sVE", v_var)
        a_sd = act(ln_st[:, 4 * tau + 3:4 * tau + 4],
                   ln_st[:, 4 * tau + 2:4 * tau + 3],
                   AF.Sqrt, bias=tbl["eps"][:], scale=1.0)
        wait("vector", "sAC", a_sd)
        vop("reciprocal", ln_st[:, 4 * tau + 3:4 * tau + 4],
            ln_st[:, 4 * tau + 3:4 * tau + 4])
        xslot = xn_t[:, H * (tau % 2):H * (tau % 2) + H]
        if tau >= 2:
            wait("vector", "sMM", ln_tr_last[tau - 2])
        v_xn = vop("tensor_scalar", xslot, ln_x[:, H * tau:H * (tau + 1)],
                   ln_st[:, 4 * tau:4 * tau + 1],
                   ln_st[:, 4 * tau + 3:4 * tau + 4],
                   ALU.subtract, op1=ALU.mult)
        wait("tensor", "sVE", v_xn)
        for half in range(2):
            if tau >= 1:
                wait("tensor", "sVE", ln_cp[(tau - 1, half)])
            for h8 in range(8):
                hh = 8 * half + h8
                m_t = trans(psg[half][:, 128 * h8:128 * (h8 + 1)],
                            xslot[:, 128 * hh:128 * (hh + 1)],
                            tbl["identf_s"][:], inc=(h8 == 7))
            ln_tr_last[tau] = m_t
            wait("vector", "sMM", m_t)
            ln_cp[(tau, half)] = vop(
                "tensor_copy",
                xnv[:, 8 * half:8 * (half + 1), 128 * tau:128 * (tau + 1)],
                psg[half][:].rearrange("p (h w) -> p h w", h=8))
        if tau == 3:
            wait("gpsimd", "sVE", P.cur("sVE"))
            g_h1 = dma("gpsimd", "sLG",
                       ag_xn_in.ap().rearrange("(h p) t -> p h t", p=128),
                       xnownT[:].rearrange("p (h t) -> p h t", h=NH)[:, :, 0:512])
            wait("gpsimd", "sLG", g_h1)
            P.emit("gpsimd", lambda e: e.collective_compute(
                "AllGather", ALU.bypass, replica_groups=[list(range(NCORE))],
                ins=[ag_xn_in.ap().opt()],
                outs=[ag_xn_out.ap().opt()]).then_inc(SEM["sCXN"]))
    v_ln_done = P.cur("sVE")

    wait("gpsimd", "sVE", v_ln_done)
    g_xn = dma("gpsimd", "sLG",
               ag_xn_in2.ap().rearrange("(h p) t -> p h t", p=128),
               xnownT[:].rearrange("p (h t) -> p h t", h=NH)[:, :, 512:1024])
    wait("gpsimd", "sLG", g_xn)
    P.emit("gpsimd", lambda e: e.collective_compute(
        "AllGather", ALU.bypass, replica_groups=[list(range(NCORE))],
        ins=[ag_xn_in2.ap().opt()], outs=[ag_xn_out2.ap().opt()]).then_inc(SEM["sCX2"]))
    if taps:
        dma("gpsimd", "sLG", tap_ap["t_xnT"], xnownT[:])
    xn_contrib_mark = P.cur("sLG")

    for cm in (cm_xn_t, cm_ln_st, cm_x2s, cm_ln_x, cm_xnownT):
        sb_free(cm)

    qT, cm_qT = sb("qT", [128, 2 * S], BF16)
    kT, cm_kT = sb("kT", [128, 2 * S], BF16)
    vT, cm_vT = sb("vT", [128, 2 * S], BF16)
    aoT, cm_aoT = sb("aoT", [128, 2 * S], BF16)
    ptile, cm_ptile = sb("ptile", [128, 2 * 1024], BF16)
    idx_i, cm_idx_i = sb("idx_i", [128, 1024], I32)
    idx_i2, cm_idx_i2 = sb("idx_i2", [128, 1024], I32)
    selw, cm_selw = sb("selw", [128, 1024], BF16)
    selt, cm_selt = sb("selt", [128, 1024], BF16)
    lntmp, cm_lntmp = sb("lntmp", [128, 512], F32)
    rcpr, cm_rcpr = sb("rcpr", [128, 512], F32)
    w6, cm_w6 = sb("w6", [128, 4 * H], BF16)
    wv2, cm_wv2 = sb("wv2", [128, 2 * H], BF16)
    wcv0, cm_wcv0 = sb("wcv0", [128, H], F32)
    wcv1, cm_wcv1 = sb("wcv1", [128, H], F32)

    wcv = (wcv0, wcv1)
    cast_marks = {}
    for j in range(6):
        if j >= 2:
            wait("scalar", "sAC", cast_marks[j - 2])
        ldv = dma("scalar", "sLwo",
                  wcv[j % 2][:].rearrange("p (h w) -> p h w", h=NH),
                  ap["wqkv"][:, 128 * j:128 * (j + 1)].rearrange("(h p) w -> p h w", p=128))
        wait("scalar", "sLwo", ldv)
        dst = w6[:, H * j:H * (j + 1)] if j < 4 else wv2[:, H * (j - 4):H * (j - 3)]
        cast_marks[j] = act(dst, wcv[j % 2][:], AF.Identity, scale=1.0)
    a_w_ready = cast_marks[5]
    sb_free(cm_wcv1)
    sb_free(cm_wcv0)

    xnstr, cm_xnstr = sb("xnstr", [128, 2 * 8192], BF16)

    sel_state = {"n": 0, "cp": {}, "last_contrib": 0, "pend": None}

    def flush_contrib():
        if sel_state["pend"] is not None:
            pdst, pcols, pmark = sel_state["pend"]
            wait("gpsimd", "sVE", pmark)
            sel_state["last_contrib"] = dma("gpsimd", "sLcon", pdst,
                                            selw[:, 0:pcols])
            sel_state["pend"] = None

    def emit_select_unit(src_ap, cols, dst_ap):
        ku = sel_state["n"]
        sel_state["n"] = ku + 1
        ib = (idx_i, idx_i2)[ku % 2]
        isem = ("sLsel", "sLwl")[ku % 2]
        gate = sel_state["cp"].get(ku - 2, v_lutrep if ku == 0 else 0)
        if gate:
            wait("gpsimd", "sVE", gate)
        ldv = dma("gpsimd", isem, ib[:, 0:cols], src_ap)
        flush_contrib()
        wait("vector", isem, ldv)
        if sel_state["last_contrib"]:
            wait("vector", "sLcon", sel_state["last_contrib"])
        last = 0
        for k in range(LUTN):
            if k == 0:
                vop("tensor_scalar", selw[:, 0:cols], ib[:, 0:cols],
                    0.0, tbl["lutrep"][:, 0:1], ALU.is_equal, ALU.mult)
            else:
                vop("tensor_scalar", selt[:, 0:cols], ib[:, 0:cols],
                    float(k), tbl["lutrep"][:, k:k + 1], ALU.is_equal, ALU.mult)
                last = vop("tensor_tensor", selw[:, 0:cols], selw[:, 0:cols],
                           selt[:, 0:cols], ALU.add)
        sel_state["cp"][ku] = last
        sel_state["pend"] = (dst_ap, cols, last)

    select_units = []
    for mat_name, col_off in (("gidxT", 0), ("uidxT", 768)):
        for hp in range(16):
            select_units.append((
                ap[mat_name][128 * hp:128 * (hp + 1), :], 768,
                ag_gu_in.ap()[128 * hp:128 * (hp + 1), col_off:col_off + 768]))
    for it6 in range(6):
        for half in range(2):
            select_units.append((
                ap["didxT"][128 * it6:128 * (it6 + 1), 1024 * half:1024 * (half + 1)],
                1024,
                ag_wd_in.ap()[128 * it6:128 * (it6 + 1), 1024 * half:1024 * (half + 1)]))
    sel_idx = [0]
    cc_emitted = {"gu": False, "wd": False}

    def emit_gu_cc():
        wait("gpsimd", "sLcon", P.cur("sLcon"))
        P.emit("gpsimd", lambda e: e.collective_compute(
            "AllGather", ALU.bypass, replica_groups=[list(range(NCORE))],
            ins=[ag_gu_in.ap().opt()], outs=[ag_gu_out.ap().opt()]).then_inc(SEM["sCGU"]))

    def emit_wd_cc():
        wait("gpsimd", "sLcon", P.cur("sLcon"))
        P.emit("gpsimd", lambda e: e.collective_compute(
            "AllGather", ALU.bypass, replica_groups=[list(range(NCORE))],
            ins=[ag_wd_in.ap().opt()], outs=[ag_wd_out.ap().opt()]).then_inc(SEM["sCWD"]))
    def pop_select(n):
        for _ in range(n):
            if sel_idx[0] < len(select_units):
                emit_select_unit(*select_units[sel_idx[0]])
                sel_idx[0] += 1
                if sel_idx[0] == 32 and not cc_emitted["gu"]:
                    cc_emitted["gu"] = True
                    flush_contrib()
                    emit_gu_cc()
                if sel_idx[0] == len(select_units):
                    flush_contrib()

    pop_select(9)

    wait("tensor", "sCXN", 1)
    wait("sync", "sCXN", 1)
    wait("sync", "sLG", xn_contrib_mark)
    wait("sync", "sAC", a_w_ready)
    wait("sync", "sLG", P.cur("sLG"))
    wait("scalar", "sLG", P.cur("sLG"))
    wait("tensor", "sVE", v_ln_done)
    wait("tensor", "sAC", P.cur("sAC"))

    qk_evac = {}
    v_evac = {}
    stream_last_mm = {}
    tt_order = [2 * r for r in range(8)] + [2 * r + 1 for r in range(8)]
    for kk, tt in enumerate(tt_order):
        buf = kk % 2
        xc = buf * 8192
        rank = tt // 2
        src_t = ag_xn_out if tt % 2 == 0 else ag_xn_out2
        if kk == 8:
            wait("sync", "sCX2", 1)
            wait("tensor", "sCX2", 1)
        if kk >= 2:
            wait("sync", "sMM", stream_last_mm[kk - 2])
        ldv = dma("sync", "sLx0" if buf == 0 else "sLx1",
                  xnstr[:, xc:xc + 8192].rearrange("p (h t) -> p h t", h=NH),
                  src_t.ap()[rank * H:(rank + 1) * H, 0:512]
                  .rearrange("(h p) t -> p h t", p=128))
        wait("tensor", "sLx0" if buf == 0 else "sLx1", ldv)
        for j in range(4):
            bank = psg[j // 2][:, 512 * (j % 2):512 * (j % 2) + 512]
            if (j, kk - 1) in qk_evac:
                wait("tensor", "sAC", qk_evac[(j, kk - 1)])
            for hh in range(NH):
                mm(bank, w6[:, H * j + 128 * hh: H * j + 128 * (hh + 1)],
                   xnstr[:, xc + 512 * hh: xc + 512 * (hh + 1)],
                   start=(hh == 0), stop=(hh == NH - 1), inc=(hh == NH - 1))
            m_qk = P.cur("sMM")
            wait("scalar", "sMM", m_qk)
            dst = qT if j < 2 else kT
            slot = j % 2
            scl = SCALE if j < 2 else 1.0
            qk_evac[(j, kk)] = act(
                dst[:, slot * S + 512 * tt: slot * S + 512 * (tt + 1)],
                bank, AF.Identity, bias=tbl["bq4_s"][:, j:j + 1], scale=scl)
        for sub in range(4):
            bankA = pv2[sub % 2]
            bankB = db2[sub % 2]
            key = (sub % 2, 4 * kk + sub)
            prev = (sub % 2, 4 * kk + sub - 2)
            if prev in v_evac:
                wait("tensor", "sAC", v_evac[prev])
            for hh in range(NH):
                for slot, bk in ((0, bankA), (1, bankB)):
                    mm(bk[:, 0:128],
                       xnstr[:, xc + 512 * hh + 128 * sub: xc + 512 * hh + 128 * (sub + 1)],
                       wv2[:, H * slot + 128 * hh: H * slot + 128 * (hh + 1)],
                       start=(hh == 0), stop=False)
            for slot, bk in ((0, bankA), (1, bankB)):
                m_v = mm(bk[:, 0:128], tbl["ones1f"][:],
                         tbl["bv2f"][:, 128 * slot:128 * (slot + 1)],
                         start=False, stop=True, inc=(slot == 1))
            stream_last_mm[kk] = m_v
            wait("scalar", "sMM", m_v)
            tg = 4 * tt + sub
            act(vT[:, 0 * S + 128 * tg: 0 * S + 128 * (tg + 1)],
                bankA[:, 0:128], AF.Identity, scale=1.0)
            v_evac[key] = act(vT[:, S + 128 * tg: S + 128 * (tg + 1)],
                              bankB[:, 0:128], AF.Identity, scale=1.0)
        if kk % 2 == 1:
            pop_select(1)

    sb_free(cm_xnstr)
    sb_free(cm_wv2)
    sb_free(cm_w6)

    b4j_s, cm_b4j = sb("b4j_s", [128, S], BF16)
    b4i_s, cm_b4i = sb("b4i_s", [128, S], BF16)
    masks, cm_masks = sb("masks", [128, 2048], BF16)
    wait("sync", "sMM", P.cur("sMM"))
    wait("sync", "sAC", P.cur("sAC"))
    dma("sync", "sLtab", b4j_s[:], ap["b4j"][:, 0:S])
    dma("sync", "sLtab", b4i_s[:], ap["b4i"][:, 0:S])
    ld_b4 = dma("sync", "sLtab", masks[:], ap["maskneg"])
    wait("tensor", "sLtab", ld_b4)
    wait("vector", "sLtab", ld_b4)
    wait("tensor", "sAC", P.cur("sAC"))

    exp_in_bank = {0: 0, 1: 0}
    pt_mark = {0: 0, 1: 0}
    pvdb_guard = {}

    for g in range(32):
        slot = g // 16
        s16 = g % 16
        so = slot * S
        i0 = 512 * s16
        nch = 4 * s16 + 4
        jc0 = max(0, (i0 - 2048) // 128) if slot == 0 else 0
        jc0 -= jc0 % 2
        ngrp_all = nch // 2
        pvb = pv2[g % 2]
        db = db2[g % 2]
        if g == 16:
            wait("gpsimd", "sVE", pvdb_guard[15][0])
            g_c1 = dma("gpsimd", "sLG", ag_at_in1.ap(), aoT[:, 0:S])
            wait("gpsimd", "sLG", g_c1)
            P.emit("gpsimd", lambda e: e.collective_compute(
                "AllGather", ALU.bypass, replica_groups=[list(range(NCORE))],
                ins=[ag_at_in1.ap().opt()],
                outs=[ag_at_out1.ap().opt()]).then_inc(SEM["sCAT"]))
            last_mm_slot0 = P.cur("sMM")
            wait("sync", "sMM", last_mm_slot0)
            dma("sync", "sLtab", b4j_s[0:4, :], ap["b4j"][0:4, S:2 * S])
            ld_sl = dma("sync", "sLtab", b4i_s[0:4, :], ap["b4i"][0:4, S:2 * S])
            wait("tensor", "sLtab", ld_sl)
        if (g - 2) in pvdb_guard:
            v_n2, a_ln2 = pvdb_guard[g - 2]
            wait("tensor", "sVE", v_n2)
            wait("tensor", "sAC", a_ln2)
        den_marks = {}
        exp_marks = {}

        def emit_pvden(grp):
            wait("tensor", "sAC", exp_marks[grp])
            for half in range(2):
                jc = 2 * grp + half
                pt = ptile[:, 1024 * (grp % 2) + 512 * half:
                           1024 * (grp % 2) + 512 * (half + 1)]
                mm(pvb[:], vT[:, so + 128 * jc: so + 128 * (jc + 1)], pt,
                   start=(jc == jc0), stop=(jc == nch - 1))
                den_marks[jc] = mm(db[:], tbl["onesj"][:], pt,
                                   start=(jc == jc0), stop=(jc == nch - 1),
                                   inc=(half == 1))
            pt_mark[grp % 2] = den_marks[2 * grp + 1]

        g0 = jc0 // 2
        for grp in range(g0, ngrp_all):
            sgb = psg[grp % 2]
            if exp_in_bank[grp % 2]:
                wait("tensor", "sAC", exp_in_bank[grp % 2])
            m_b = 0
            for half in range(2):
                jc = 2 * grp + half
                sl = sgb[:, 512 * half:512 * half + 512]
                mm(sl, kT[:, so + 128 * jc: so + 128 * (jc + 1)],
                   qT[:, so + i0: so + i0 + 512], start=True, stop=False)
                m_b = mm(sl, b4j_s[:, 128 * jc: 128 * (jc + 1)],
                         b4i_s[:, i0: i0 + 512],
                         start=False, stop=True, inc=True)
            if grp >= ngrp_all - 2:
                dd = grp - (ngrp_all - 2)
                wait("vector", "sMM", m_b)
                v_m = vop("tensor_tensor", sgb[:], sgb[:],
                          masks[:, 1024 * dd:1024 * (dd + 1)], ALU.add)
                wait("scalar", "sVE", v_m)
            wait("scalar", "sMM", m_b)
            if pt_mark[grp % 2]:
                wait("scalar", "sMM", pt_mark[grp % 2])
            e_m = act(ptile[:, 1024 * (grp % 2):1024 * (grp % 2 + 1)], sgb[:],
                      AF.Exp)
            exp_in_bank[grp % 2] = e_m
            exp_marks[grp] = e_m
            if grp - ATT_LAG >= g0:
                emit_pvden(grp - ATT_LAG)
        for gtail in range(max(g0, ngrp_all - ATT_LAG), ngrp_all):
            emit_pvden(gtail)
        if not (slot == 1 and s16 < 3):
            pop_select(1)
        m_last_den = den_marks[nch - 1]
        wait("scalar", "sMM", m_last_den)
        act(lntmp[:], db[:], AF.Ln)
        a_ln = act(rcpr[:], lntmp[:], AF.Exp, scale=-1.0)
        wait("vector", "sAC", a_ln)
        v_n = vop("tensor_tensor", aoT[:, so + i0: so + i0 + 512],
                  pvb[:], rcpr[:], ALU.mult)
        pvdb_guard[g] = (v_n, a_ln)
    pop_select(len(select_units))

    att_done = P.cur("sVE")
    if taps:
        wait("gpsimd", "sVE", att_done)
        dma("gpsimd", "sLG", tap_ap["t_qT"], qT[:])
        dma("gpsimd", "sLG", tap_ap["t_kT"], kT[:])
        dma("gpsimd", "sLG", tap_ap["t_vT"], vT[:])
        dma("gpsimd", "sLG", tap_ap["t_aoT"], aoT[:])
    wait("gpsimd", "sVE", att_done)
    g_ctr2 = dma("gpsimd", "sLG", ag_at_in2.ap(), aoT[:, S:2 * S])

    for cm in (cm_masks, cm_b4i, cm_b4j, cm_rcpr, cm_lntmp, cm_selt, cm_selw,
               cm_idx_i2, cm_idx_i, cm_ptile, cm_aoT, cm_vT, cm_kT,
               cm_qT):
        sb_free(cm)

    h_T, cm_hT = sb("h_T", [128, NIT * TOK], BF16)
    ao_own, cm_ao_own = sb("ao_own", [128, NH * TOK], BF16)
    wgu_cms = []
    wgu = []
    for wn in ("wgu0", "wgu1"):
        t, cmw = sb(wn, [128, 2 * H], BF16)
        wgu.append(t)
        wgu_cms.append(cmw)
    sg, cm_sg = sb("sg", [128, 512], BF16)
    xnown2, cm_xnown2 = sb("xnown2", [128, NH * TOK], BF16)

    for s_name in ("sMM", "sVE", "sAC", "sLcon", "sLsel", "sLwl", "sLtab"):
        wait("sync", s_name, P.cur(s_name))
    wait("sync", "sLG", g_ctr2)
    if taps:
        for e2 in ("sync", "vector", "scalar"):
            wait(e2, "sLG", P.cur("sLG"))
    dma("sync", "sLtab",
        xnown2[:].rearrange("p (h t) -> p h t", h=NH)[:, :, 0:512],
        ag_xn_in.ap().rearrange("(h p) t -> p h t", p=128))
    ld_xn2 = dma("sync", "sLtab",
                 xnown2[:].rearrange("p (h t) -> p h t", h=NH)[:, :, 512:1024],
                 ag_xn_in2.ap().rearrange("(h p) t -> p h t", p=128))
    wait("tensor", "sCGU", 1)
    wait("sync", "sCGU", 1)
    wait("tensor", "sLtab", ld_xn2)
    wait("tensor", "sAC", P.cur("sAC"))
    wait("tensor", "sVE", P.cur("sVE"))

    gu_last_mm = {}
    silu_marks = {}
    htt_marks = {}
    for ti in range(NIT):
        buf = wgu[ti % 2]
        rank = ti % NCORE
        loc = ti // NCORE
        gs = "sLw0" if ti % 2 == 0 else "sLw1"
        if ti >= 2:
            wait("sync", "sMM", gu_last_mm[ti - 2])
        dma("sync", gs, buf[:, 0:H].rearrange("p (h w) -> p h w", h=NH),
            ag_gu_out.ap()[rank * H:(rank + 1) * H, 128 * loc:128 * (loc + 1)]
            .rearrange("(h p) w -> p h w", p=128))
        ldv = dma("sync", gs, buf[:, H:2 * H].rearrange("p (h w) -> p h w", h=NH),
                  ag_gu_out.ap()[rank * H:(rank + 1) * H, 768 + 128 * loc:768 + 128 * (loc + 1)]
                  .rearrange("(h p) w -> p h w", p=128))
        if ti == 4:
            wait("gpsimd", gs, ldv)
            P.emit("gpsimd", lambda e: e.collective_compute(
                "AllGather", ALU.bypass, replica_groups=[list(range(NCORE))],
                ins=[ag_at_in2.ap().opt()],
                outs=[ag_at_out2.ap().opt()]).then_inc(SEM["sCAT"]))
        elif ti == 16:
            wait("gpsimd", gs, ldv)
            emit_wd_cc()
        wait("tensor", gs, ldv)
        for t2 in range(2):
            if (ti - 1, t2) in silu_marks:
                wait("tensor", "sAC", silu_marks[(ti - 1, t2)])
            if (ti - 1, t2) in htt_marks:
                wait("tensor", "sVE", htt_marks[(ti - 1, t2)])
        gm = {}
        for hh in range(NH):
            for t2 in range(2):
                gm[t2] = mm(psg[0][:, 512 * t2:512 * (t2 + 1)],
                            buf[:, 128 * hh:128 * (hh + 1)],
                            xnown2[:, hh * TOK + 512 * t2: hh * TOK + 512 * (t2 + 1)],
                            start=(hh == 0), stop=(hh == NH - 1),
                            inc=(hh == NH - 1))
        for hh in range(NH):
            for t2 in range(2):
                gu_last_mm[ti] = mm(
                    psg[1][:, 512 * t2:512 * (t2 + 1)],
                    buf[:, H + 128 * hh:H + 128 * (hh + 1)],
                    xnown2[:, hh * TOK + 512 * t2: hh * TOK + 512 * (t2 + 1)],
                    start=(hh == 0), stop=(hh == NH - 1), inc=(hh == NH - 1))
        m_u = gu_last_mm[ti]
        for t2 in range(2):
            gbank = psg[0][:, 512 * t2:512 * (t2 + 1)]
            ubank = psg[1][:, 512 * t2:512 * (t2 + 1)]
            wait("scalar", "sMM", gm[t2])
            if (ti, t2 - 1) in htt_marks:
                wait("scalar", "sVE", htt_marks[(ti, t2 - 1)])
            elif t2 == 0 and (ti - 1, 1) in htt_marks:
                wait("scalar", "sVE", htt_marks[(ti - 1, 1)])
            a_s = act(sg[:], gbank, AF.Silu)
            silu_marks[(ti, t2)] = a_s
            wait("vector", "sMM", m_u)
            wait("vector", "sAC", a_s)
            htt_marks[(ti, t2)] = vop(
                "tensor_tensor", h_T[:, ti * TOK + 512 * t2: ti * TOK + 512 * (t2 + 1)],
                sg[:], ubank, ALU.mult)
    if taps:
        wait("gpsimd", "sVE", P.cur("sVE"))
        dma("gpsimd", "sLG", tap_ap["t_hT"], h_T[:])

    wait("gpsimd", "sCAT", 2)

    def _ao_own_load(e):
        with e.register("rc") as rc, e.register("ro") as ro:
            e.reg_load(rc, ap["cid"][0:1, 0:1])
            e.reg_mul(ro, rc, TOK)
            for d in range(NH):
                src = (ag_at_out1 if d % 2 == 0 else ag_at_out2).ap()
                r = d // 2
                e.dma_start(
                    out=ao_own[:, d * TOK:(d + 1) * TOK],
                    in_=src[128 * r:128 * (r + 1),
                            bass.ds(bass.RuntimeValue(ro), TOK)],
                ).then_inc(SEM["sLG"], 16)
    P.bump("sLG", 16 * NH)
    P.emit("gpsimd", _ao_own_load)
    g_ao = P.cur("sLG")
    if taps:
        wait("gpsimd", "sLG", g_ao)
        dma("gpsimd", "sLG", tap_ap["t_ao_own"], ao_own[:])

    sb_free(cm_xnown2)
    sb_free(cm_sg)
    sb_free(wgu_cms[1])
    sb_free(wgu_cms[0])

    wlhs0, _ = sb("wlhs0", [128, (16 + 48) * 128], BF16)
    wlhs1, _ = sb("wlhs1", [128, (16 + 48) * 128], BF16)
    ostage0, _ = sb("ostage0", [128, 512], F32)
    ostage1, _ = sb("ostage1", [128, 512], F32)
    outasm, _ = sb("outasm", [128, 4 * H], F32)
    xres, _ = sb("xres", [128, H], F32)
    wst0, _ = sb("wst0", [128, H], F32)

    wait("sync", "sCWD", 1)
    wait("tensor", "sCWD", 1)
    wait("tensor", "sLG", g_ao)
    barrier_waits("tensor")
    barrier_waits("sync")
    barrier_waits("vector")
    barrier_waits("scalar")

    evac_marks = {}
    tr_marks = {}
    cp_marks = {}
    wo_cast = {}
    wl_last = {0: 0, 1: 0}
    store_marks = []
    hp_seq = [(a2, hp) for a2 in range(2) for hp in range(16)]
    for idx, (a2, hp) in enumerate(hp_seq):
        wl = (wlhs0, wlhs1)[idx % 2]
        osg = (ostage0, ostage1)[idx % 2]
        mainbank = psg[idx % 2][:, 0:512]
        if a2 == 1 and hp == 0 and store_marks:
            wait("vector", "sLS", store_marks[-1])
        wls = "sLw0" if idx % 2 == 0 else "sLw1"
        if wl_last[idx % 2]:
            wait("sync", "sMM", wl_last[idx % 2])
        if idx >= 1:
            wait("sync", "sVE", wo_cast[idx - 1])
        ld_wo = dma("sync", wls, wst0[:].rearrange("p (d w) -> p d w", d=NH),
                    ap["wo"][:, 128 * hp:128 * (hp + 1)].rearrange("(d p) w -> p d w", p=128))
        wd_v = ag_wd_out.ap().rearrange("(e q) w -> q e w", e=NCORE)
        for l6 in range(6):
            ldv = dma("sync", wls,
                      wl[:, H + 1024 * l6:H + 1024 * (l6 + 1)]
                      .rearrange("p (ee w) -> p ee w", ee=8),
                      wd_v[128 * l6:128 * (l6 + 1), :, 128 * hp:128 * (hp + 1)])
        wait("vector", wls, ldv)
        if wl_last[idx % 2]:
            wait("vector", "sMM", wl_last[idx % 2])
        wo_cast[idx] = vop("tensor_copy", wl[:, 0:H], wst0[:])
        wait("tensor", wls, ldv)
        wait("tensor", "sVE", wo_cast[idx])
        if (idx - 2) in evac_marks:
            wait("tensor", "sAC", evac_marks[idx - 2])
        for d in range(NH):
            mm(mainbank, wl[:, 128 * d:128 * (d + 1)],
               ao_own[:, d * TOK + 512 * a2: d * TOK + 512 * (a2 + 1)],
               start=(d == 0), stop=False)
        for ti in range(NIT):
            mm(mainbank, wl[:, H + 128 * ti:H + 128 * (ti + 1)],
               h_T[:, ti * TOK + 512 * a2: ti * TOK + 512 * (a2 + 1)],
               start=False, stop=(ti == NIT - 1), inc=(ti == NIT - 1))
        m_e = P.cur("sMM")
        wl_last[idx % 2] = m_e
        wait("scalar", "sMM", m_e)
        if (idx - 2) in tr_marks:
            wait("scalar", "sMM", tr_marks[idx - 2])
        a_e = act(osg[:], mainbank, AF.Identity,
                  bias=tbl["bo_s"][:, hp:hp + 1], scale=1.0)
        evac_marks[idx] = a_e
        wait("tensor", "sAC", a_e)
        for t4 in range(4):
            tbank = pv2[t4 % 2]
            key = (idx, t4)
            prevkey = (idx, t4 - 2) if t4 >= 2 else (idx - 1, t4 + 2)
            if prevkey in cp_marks:
                wait("tensor", "sVE", cp_marks[prevkey])
            m_t = trans(tbank[:, 0:128], osg[:, 128 * t4:128 * (t4 + 1)],
                        tbl["identf_s"][:], inc=True)
            if t4 == 3:
                tr_marks[idx] = m_t
            wait("vector", "sMM", m_t)
            cp_marks[key] = vop(
                "tensor_copy", outasm[:, t4 * H + 128 * hp: t4 * H + 128 * (hp + 1)],
                tbank[:, 0:128])
        if hp == 15:
            v_add_prev = 0
            for t4 in range(4):
                if v_add_prev:
                    wait("sync", "sVE", v_add_prev)
                ldx = dma("sync", "sLxr", xres[:],
                          ap["x_own"][512 * a2 + 128 * t4: 512 * a2 + 128 * (t4 + 1), :])
                wait("vector", "sLxr", ldx)
                v_add_prev = vop("tensor_tensor", outasm[:, t4 * H:(t4 + 1) * H],
                                 outasm[:, t4 * H:(t4 + 1) * H], xres[:], ALU.add)
                wait("scalar", "sVE", v_add_prev)
                st = dma("scalar", "sLS",
                         out_ap[512 * a2 + 128 * t4: 512 * a2 + 128 * (t4 + 1), :],
                         outasm[:, t4 * H:(t4 + 1) * H])
                store_marks.append(st)

    wait("scalar", "sLS", store_marks[-1])
    wait("gpsimd", "sLS", store_marks[-1])

    with nc.Block() as block:
        @block.sync
        def _(e):
            for f in P.streams["sync"]:
                f(e)

        @block.tensor
        def _(e):
            for f in P.streams["tensor"]:
                f(e)

        @block.vector
        def _(e):
            for f in P.streams["vector"]:
                f(e)

        @block.scalar
        def _(e):
            for f in P.streams["scalar"]:
                f(e)

        @block.gpsimd
        def _(e):
            for f in P.streams["gpsimd"]:
                f(e)

    for cm in reversed(list(open_cms)):
        sb_free(cm)
    for cm in list(reversed(ps_cms)) + list(reversed(sem_cms)):
        cm.__exit__(None, None, None)
    return nc


def _host_inputs(x, Wqkv, bqkv, Wo, bo, lut, gidx, uidx, didx):
    x2d = np.ascontiguousarray(x.reshape(S, H))
    slopes = [2.0 ** (-(k) / 2.0) for k in range(NH)]
    jj = np.arange(128)[:, None]
    icq = np.arange(512)[None, :]
    maskneg = np.zeros((128, 2048), BF)
    for dd in range(2):
        for hl in range(2):
            d = 2 * dd + hl
            maskneg[:, 1024 * dd + 512 * hl:1024 * dd + 512 * (hl + 1)] = \
                np.where(128 * d + jj <= icq, 0.0, NEG).astype(BF)
    identf = np.eye(128, dtype=np.float32)
    pos = np.arange(S, dtype=np.float64)

    perm = np.empty(H, np.int64)
    for r in range(NCORE):
        for s2 in range(2):
            h = r + 8 * s2
            perm[256 * r + 128 * s2:256 * r + 128 * (s2 + 1)] = \
                np.arange(128 * h, 128 * (h + 1))
    wo_perm = np.ascontiguousarray(Wo[perm, :])

    in_maps = []
    for c in range(NCORE):
        heads = (c, 8 + c)
        cols = []
        for base in (0, 2048, 4096):
            for h in heads:
                cols.append((base + 128 * h, base + 128 * (h + 1)))
        wqkv_c = np.ascontiguousarray(
            np.concatenate([Wqkv[:, a:b] for a, b in cols], axis=1))
        bq4 = np.ascontiguousarray(
            np.stack([bqkv[a:b] for a, b in cols[:4]], axis=1))
        bv2 = np.concatenate([bqkv[a:b] for a, b in cols[4:]])[None, :]
        tiles = [c + 8 * k for k in range(6)]

        def _rows(idx_mat, t):
            if t < NIT:
                return idx_mat[128 * t:128 * (t + 1)]
            return np.zeros((128, idx_mat.shape[1]), np.int32)
        gidxT = np.ascontiguousarray(
            np.concatenate([_rows(gidx, t) for t in tiles]).T.astype(np.int32))
        uidxT = np.ascontiguousarray(
            np.concatenate([_rows(uidx, t) for t in tiles]).T.astype(np.int32))

        def _cols(idx_mat, t):
            if t < NIT:
                return idx_mat[:, 128 * t:128 * (t + 1)]
            return np.zeros((idx_mat.shape[0], 128), np.int32)
        didxT = np.ascontiguousarray(
            np.concatenate([_cols(didx, t) for t in tiles], axis=1).T.astype(np.int32))
        b4j = np.zeros((128, 2 * S), BF)
        b4i = np.zeros((128, 2 * S), BF)
        b4j[2, :] = 1.0
        b4j[3, :] = 1.0
        b4i[0, :] = 1.0
        b4i[1, :] = 1.0
        for sl, h in enumerate(heads):
            v = slopes[h] * pos
            hi = v.astype(np.float32).astype(BF)
            lo = (v.astype(np.float32) - hi.astype(np.float32)).astype(BF)
            b4j[0, sl * S:(sl + 1) * S] = hi
            b4j[1, sl * S:(sl + 1) * S] = lo
            b4i[2, sl * S:(sl + 1) * S] = -hi
            b4i[3, sl * S:(sl + 1) * S] = -lo
        in_maps.append({
            "cid": np.array([[c]], np.int32),
            "x_own": np.ascontiguousarray(x2d[TOK * c:TOK * (c + 1)]),
            "wqkv": wqkv_c,
            "bq4": bq4.astype(np.float32),
            "bv2": bv2.astype(np.float32),
            "wo": wo_perm,
            "bo_t": np.ascontiguousarray(bo.reshape(16, 128).T),
            "lut": lut.reshape(1, LUTN).astype(np.float32),
            "gidxT": gidxT, "uidxT": uidxT, "didxT": didxT,
            "b4j": b4j, "b4i": b4i,
            "maskneg": maskneg, "identf": identf,
        })
    return in_maps


def kernel(L0_attn_x, L0_attn_Wqkv, L0_attn_bqkv, L0_attn_Wo, L0_attn_bo,
           L0_ffn_lut, L0_ffn_gate_idx, L0_ffn_up_idx, L0_ffn_down_idx):
    in_maps = _host_inputs(
        np.asarray(L0_attn_x, np.float32), np.asarray(L0_attn_Wqkv, np.float32),
        np.asarray(L0_attn_bqkv, np.float32), np.asarray(L0_attn_Wo, np.float32),
        np.asarray(L0_attn_bo, np.float32), np.asarray(L0_ffn_lut, np.float32),
        np.asarray(L0_ffn_gate_idx, np.int32), np.asarray(L0_ffn_up_idx, np.int32),
        np.asarray(L0_ffn_down_idx, np.int32))
    nc = build_nc(taps=False)
    res = run_bass_kernel_spmd(nc, in_maps, core_ids=list(range(NCORE)))
    out = np.concatenate([np.asarray(res.results[c]["out"]) for c in range(NCORE)],
                         axis=0)
    return out.reshape(1, S, H).astype(np.float32)
